# version 13
# baseline (speedup 1.0000x reference)
import contextlib
import math
import numpy as np
import ml_dtypes
import concourse.bass as bass
import concourse.mybir as mybir
from concourse.bass_utils import run_bass_kernel_spmd

F32 = mybir.dt.float32
BF16 = mybir.dt.bfloat16
AF = mybir.ActivationFunctionType
OP = mybir.AluOpType
AX = mybir.AxisListType
BF = ml_dtypes.bfloat16

ENGINES = ("sync", "tensor", "vector", "scalar", "gpsimd")
T = 2048
NG = 512
NGRP = T // NG
LN16 = math.log(16.0)


def ts(i, n):
    return slice(i * n, (i + 1) * n)


class Prog:
    def __init__(self, nc, stack):
        self.nc = nc
        self.stack = stack
        self.qs = {e: [] for e in ENGINES}
        self.semnames = []
        self.cnt = {}

    def sem(self, name):
        assert name not in self.cnt
        self.semnames.append(name)
        self.cnt[name] = 0
        return name

    def sbuf(self, name, shape, dtype):
        return self.stack.enter_context(self.nc.sbuf_tensor(name, shape, dtype))

    def psum(self, name, shape, dtype=F32):
        return self.stack.enter_context(self.nc.psum_tensor(name, shape, dtype))

    def emit(self, eng, fn):
        self.qs[eng].append(("op", fn))

    def wait(self, eng, sem, val=None):
        v = self.cnt[sem] if val is None else val
        self.qs[eng].append(("wait", sem, v))

    def inc(self, eng, fn, sem, n=1):
        self.cnt[sem] += n
        self.qs[eng].append(("inc", fn, sem, n))
        return self.cnt[sem]

    def tick(self, eng, sem):
        return self.inc(eng, lambda E: E.nop(), sem)

    def fence(self, eng, sem):
        q = self.qs[eng]
        if q and q[-1][0] == "op":
            self.inc_last(eng, sem)
        self.wait(eng, sem)

    def inc_last(self, eng, sem, n=1):
        q = self.qs[eng]
        if q and q[-1][0] == "op":
            fn = q[-1][1]
            self.cnt[sem] += n
            q[-1] = ("inc", fn, sem, n)
            return self.cnt[sem]
        return self.inc(eng, lambda E: E.nop(), sem, n)

    def barrier2(self, engines, sem):
        for e in engines:
            q = self.qs[e]
            if q and q[-1][0] == "op":
                self.inc_last(e, sem)
            else:
                if q and q[-1][0] == "inc":
                    self.wait(e, q[-1][2], self.cnt[q[-1][2]])
                self.tick(e, sem)
        tot = self.cnt[sem]
        for e in engines:
            self.wait(e, sem, tot)

    def dma(self, eng, sem, out, in_):
        self.cnt[sem] += 16
        self.qs[eng].append(("dma", out, in_, sem))
        return self.cnt[sem]

    def barrier(self, engines, sem):
        for e in engines:
            self.tick(e, sem)
        tot = self.cnt[sem]
        for e in engines:
            self.wait(e, sem, tot)

    def n_instr(self):
        return sum(len(q) for q in self.qs.values())

    def finalize(self):
        semh = {n: self.stack.enter_context(self.nc.semaphore(n)) for n in self.semnames}
        with self.nc.Block() as block:
            for e in ENGINES:
                if not self.qs[e]:
                    continue
                items = self.qs[e]

                def body(E, items=items):
                    for item in items:
                        kind = item[0]
                        if kind == "wait":
                            E.wait_ge(semh[item[1]], item[2])
                        elif kind == "op":
                            item[1](E)
                        elif kind == "inc":
                            item[1](E).then_inc(semh[item[2]], item[3])
                        elif kind == "dma":
                            E.dma_start(out=item[1], in_=item[2]).then_inc(semh[item[3]], 16)

                getattr(block, e)(body)


COMPUTE = ("tensor", "vector", "scalar")


class Ctx:

    def __init__(self, P):
        self.P = P
        self.wcf = set()
        self.wct = {}
        self.use_wcache = False
        self.s_dma = P.sem("s_dma")
        self.s_dmh = P.sem("s_dmh")
        self.s_w0 = P.sem("s_w0")
        self.s_w1 = P.sem("s_w1")
        self.s_mm = P.sem("s_mm")
        self.s_ve = P.sem("s_ve")
        self.s_sc = P.sem("s_sc")
        self.s_gp = P.sem("s_gp")
        self.s_out = P.sem("s_out")
        self.s_bar = P.sem("s_bar")
        self.ones_bf = P.sbuf("ones_bf", [128, 128], BF16)
        self.ones_f32 = P.sbuf("ones_f32", [128, 128], F32)
        self.ident_bf = P.sbuf("ident_bf", [128, 128], BF16)
        self.zp1_f32 = P.sbuf("zp1_f32", [128, 128], F32)
        self.negk_f32 = P.sbuf("negk_f32", [64, 256], F32)
        self.CCOL = P.sbuf("CCOL", [128, 8], F32)
        self.LTRI = P.sbuf("LTRI", [128, 128], BF16)
        self.PROJB = P.sbuf("PROJB", [64, 64], BF16)
        self.XB = P.sbuf("XB", [128, 8 * T], BF16)
        self.A1 = P.sbuf("A1", [128, 8 * T], BF16)
        self.A2 = P.sbuf("A2", [128, 8 * T], BF16)
        self.RB = P.sbuf("RB", [128, T], F32)
        self.RBbf = P.sbuf("RBbf", [128, T], BF16)
        self.MRBbf = P.sbuf("MRBbf", [128, T], BF16)
        self.SC1 = P.sbuf("SC1", [128, T], F32)
        self.SC2 = P.sbuf("SC2", [128, T], F32)
        self.W0 = P.sbuf("W0", [128, 8 * 1024], BF16)
        self.TR0 = P.sbuf("TR0", [128, 8 * T], BF16)
        self.TR1 = P.sbuf("TR1", [128, 6 * T], BF16)
        self.BIAS = P.sbuf("BIAS", [128, 128], F32)
        self.orfTs = P.sbuf("orfTs", [64, 256], BF16)
        self.KEYS = P.sbuf("KEYS", [64, 512], BF16)
        self.SMS = P.sbuf("SMS", [128, 128], BF16)
        self.KVS = P.sbuf("KVS", [64, 66], BF16)
        self.KVF = P.sbuf("KVF", [64, 66], F32)
        self.VT2 = P.sbuf("VT2", [128, 1040], BF16)
        self.PSBIG = P.psum("PSBIG", [128, 4 * NG], F32)
        self.PSA = P.psum("PSA", [128, NG], F32)
        self.PSB = P.psum("PSB", [128, NG], F32)
        self.PSTR = P.psum("PSTR", [128, NG], BF16)
        self.PSM = P.psum("PSM", [128, 128], F32)

    def dbg_dump(self, slot, ap, rows=128):
        if getattr(self, "dbg", None) is None:
            return
        P = self.P
        P.barrier2(COMPUTE, self.s_bar)
        for e in COMPUTE:
            P.inc_last(e, self.s_bar) if False else None
        P.wait("gpsimd", self.s_bar)
        P.dma("gpsimd", self.s_out, self.dbg[slot][0:rows, 0:ap.shape[-1]], ap)
        P.wait("gpsimd", self.s_out)
        P.tick("gpsimd", self.s_gp)
        for e in COMPUTE:
            P.wait(e, self.s_gp)

    def wcache(self, name, shape):
        if not self.use_wcache:
            return None
        if name not in self.wct:
            self.wct[name] = self.P.nc.dram_tensor("wc_" + name, shape, BF16,
                                                   kind="Internal").ap()
        return self.wct[name]

    def wcmark(self, name):
        if name in self.wct:
            self.wcf.add(name)

    def drain(self, st):
        if st.pending is not None:
            st._flush()
        for i, t in enumerate(st.dwar):
            if t is not None:
                self.P.wait(st.queue, *t)
                if st.queue != "sync":
                    self.P.wait("sync", *t)
                st.dwar[i] = None

    def cload(self, st, idx, slot_ap, name, entry, f32_ap, view=None):
        wc = self.wct.get(name)
        if wc is None:
            return st.load(idx, slot_ap if view is None else view, f32_ap)
        if name in self.wcf:
            return st.load(idx, slot_ap, wc[entry])
        return st.load(idx, slot_ap if view is None else view, f32_ap,
                       spill=(wc[entry], slot_ap))

    def load_consts(self, consts_ap):
        P = self.P
        P.dma("gpsimd", self.s_dma, self.ones_bf[:], consts_ap[:, 0:128])
        P.dma("sync", self.s_dmh, self.ones_f32[:], consts_ap[:, 0:128])
        P.dma("gpsimd", self.s_dma, self.ident_bf[:], consts_ap[:, 128:256])
        P.wait("gpsimd", self.s_dma)
        P.dma("sync", self.s_dmh, self.zp1_f32[:], consts_ap[:, 256:384])
        P.dma("sync", self.s_dmh, self.negk_f32[:], consts_ap[0:64, 384:640])
        P.dma("sync", self.s_dmh, self.CCOL[:], consts_ap[:, 640:648])
        P.dma("gpsimd", self.s_dma, self.LTRI[:], consts_ap[:, 704:832])

    def ln_stats(self, get_chunk, d_total=1024, eps=1e-5, n_chunks=8):
        P = self.P
        for g in range(NGRP):
            for c in range(n_chunks):
                P.emit("tensor", lambda E, c=c, g=g, a=get_chunk(c): E.matmul(
                    self.PSBIG[:, ts(g, NG)], self.ones_bf[:], a[:, ts(g, NG)],
                    start=(c == 0), stop=(c == n_chunks - 1)))
        P.inc_last("tensor", self.s_mm)
        P.wait("vector", self.s_mm)
        P.inc("vector", lambda E: E.tensor_scalar_mul(self.SC2[:], self.PSBIG[:, 0:T], 1.0 / d_total),
              self.s_ve)
        P.wait("scalar", self.s_ve)
        mm_done = []
        for hc in range(2 * n_chunks):
            c, h = divmod(hc, 2)
            if hc >= 2:
                P.wait("scalar", self.s_mm, mm_done[hc - 2])
            P.inc("scalar", lambda E, c=c, h=h, a=get_chunk(c): E.activation(
                out=self.SC1[:, h * 1024:(h + 1) * 1024], in_=a[:, h * 1024:(h + 1) * 1024],
                func=AF.Square), self.s_sc)
            P.wait("tensor", self.s_sc)
            for g in range(2):
                gg = h * 2 + g
                P.emit("tensor", lambda E, c=c, gg=gg: E.matmul(
                    self.PSBIG[:, ts(gg, NG)], self.ones_f32[:], self.SC1[:, ts(gg, NG)],
                    start=(c == 0), stop=(c == n_chunks - 1)))
            mm_done.append(P.inc_last("tensor", self.s_mm))
        P.wait("vector", self.s_mm)
        P.emit("vector", lambda E: E.tensor_scalar_mul(self.RB[:], self.PSBIG[:, 0:T], 1.0 / d_total))
        P.emit("vector", lambda E: E.tensor_mul(self.SC1[:], self.SC2[:], self.SC2[:]))
        P.fence("vector", self.s_ve)
        P.inc("vector", lambda E: E.tensor_sub(self.RB[:], self.RB[:], self.SC1[:]), self.s_ve)
        P.wait("scalar", self.s_ve)
        P.emit("scalar", lambda E: E.activation(out=self.SC1[:], in_=self.RB[:], func=AF.Ln, bias=self.CCOL[:, 0:1]))
        P.fence("scalar", self.s_sc)
        P.inc("scalar", lambda E: E.activation(out=self.RB[:], in_=self.SC1[:], func=AF.Exp, scale=-0.5),
              self.s_sc)
        P.wait("vector", self.s_sc)
        P.emit("vector", lambda E: E.tensor_copy(self.RBbf[:], self.RB[:]))
        P.inc("vector", lambda E: E.scalar_tensor_tensor(
            out=self.MRBbf[:], in0=self.SC2[:], scalar=1.0, in1=self.RB[:],
            op0=OP.mult, op1=OP.mult), self.s_ve)
        P.fence("vector", self.s_ve)

    def ln_apply(self, get_src, get_dst, n_chunks=8):
        P = self.P
        for c in range(n_chunks):
            P.emit("vector", lambda E, c=c: E.tensor_mul(get_dst(c), get_src(c), self.RBbf[:]))
            P.fence("vector", self.s_ve)
            P.emit("vector", lambda E, c=c: E.tensor_sub(get_dst(c), get_dst(c), self.MRBbf[:]))
        P.inc_last("vector", self.s_ve)

    def ln_apply_affine(self, get_src, get_dst, g_col, b_col, n_chunks=8):
        P = self.P
        for c in range(n_chunks):
            P.emit("vector", lambda E, c=c: E.tensor_mul(get_dst(c), get_src(c), self.RBbf[:]))
            P.fence("vector", self.s_ve)
            P.emit("vector", lambda E, c=c: E.tensor_sub(get_dst(c), get_dst(c), self.MRBbf[:]))
            P.fence("vector", self.s_ve)
            P.emit("vector", lambda E, c=c: E.tensor_scalar(
                out=get_dst(c), in0=get_dst(c), scalar1=g_col(c), scalar2=b_col(c),
                op0=OP.mult, op1=OP.add))
        P.inc_last("vector", self.s_ve)

    def ln_residual_update(self, get_z, n_chunks=8):
        P = self.P
        for c in range(n_chunks):
            xc = self.XB[:, ts(c, T)]
            P.emit("vector", lambda E, c=c, xc=xc: E.tensor_sub(xc, xc, self.MRBbf[:]))
            P.emit("vector", lambda E, c=c: E.tensor_mul(self.SC1[:], get_z(c), self.RBbf[:]))
            P.fence("vector", self.s_ve)
            P.emit("vector", lambda E, c=c, xc=xc: E.tensor_add(xc, xc, self.SC1[:]))
            P.fence("vector", self.s_ve)
        P.inc_last("vector", self.s_ve)

    def fc(self, w_slices, rhs_chunks, evac, nk=8, bias_col=None,
           evac_eng="scalar", extra_sem_wait=None):
        P = self.P
        sem_e = {"scalar": self.s_sc, "vector": self.s_ve}[evac_eng]
        for j in range(len(w_slices)):
            if j >= 1:
                P.wait("tensor", sem_e)
            for g in range(NGRP):
                for k in range(nk):
                    P.emit("tensor", lambda E, j=j, g=g, k=k: E.matmul(
                        self.PSBIG[:, ts(g, NG)], w_slices[j](k), rhs_chunks(k)[:, ts(g, NG)],
                        start=(k == 0), stop=(k == nk - 1)))
            P.inc_last("tensor", self.s_mm)
            P.wait(evac_eng, self.s_mm)
            P.inc(evac_eng, lambda E, j=j: evac(E, j), sem_e)



def emit_l1_attn(ctx, seq_aps, w):
    P = ctx.P
    X, Y, ATT = ctx.XB, ctx.A1, ctx.A2
    TR0, TR1, W0 = ctx.TR0, ctx.TR1, ctx.W0
    PHQ = lambda mh: TR0[:, mh * T: (mh + 1) * T]
    PKT_half = lambda hf: TR0[:, 2 * T + hf * T: 2 * T + (hf + 1) * T]
    VT = TR0[:, 5 * T: 5 * T + 16 * 130]
    Q2 = TR1[:, 0:T]
    K2 = TR1[:, T:2 * T]
    V2 = TR1[:, 2 * T:3 * T]
    KV = TR1[:, 3 * T: 3 * T + 130]
    QSTG = TR1[0:64, 3 * T + 256: 4 * T + 256]
    KSTG = TR1[0:64, 4 * T + 256: 5 * T + 256]
    s_dma, s_mm, s_ve, s_sc, s_bar = ctx.s_dma, ctx.s_mm, ctx.s_ve, ctx.s_sc, ctx.s_bar
    if not getattr(ctx, "_l1_prepped", False):
        ctx._l1_prepped = True
        P.wait("sync", ctx.s_ve)
        P.dma("sync", ctx.s_dmh, ctx.SC1[0:64, 0:256], w["orft"][:, :])
        P.wait("vector", ctx.s_dmh)
        P.inc("vector", lambda E: E.tensor_scalar_mul(ctx.orfTs[:], ctx.SC1[0:64, 0:256], 0.35355339059327373), ctx.s_ve)
    P.wait("sync", ctx.s_sc)
    P.wait("sync", ctx.s_ve)
    P.dma("sync", ctx.s_dmh, ctx.BIAS[:, 0:32], w["biaspack"][:, :])
    for e in COMPUTE:
        P.wait(e, ctx.s_ve)
        P.wait(e, ctx.s_dmh)
        P.wait(e, s_dma)

    if seq_aps.get("xin") is not None:
        P.wait("gpsimd", ctx.s_out)
        for c in range(8):
            P.dma("gpsimd", s_dma, X[:, ts(c, T)], seq_aps["xin"][ts(c, 128), :])
        P.wait("gpsimd", s_dma)
        P.wait("tensor", s_dma)
        P.wait("vector", s_dma)
        P.wait("scalar", s_dma)

    ctx.ln_stats(lambda c: X[:, ts(c, T)])
    for e in COMPUTE:
        P.wait(e, s_ve)
    ctx.ln_apply(lambda c: X[:, ts(c, T)], lambda c: Y[:, ts(c, T)])
    P.barrier2(COMPUTE, s_bar)
    ctx.dbg_dump(0, X[:, ts(0, T)])
    ctx.dbg_dump(1, Y[:, ts(0, T)])
    ctx.dbg_dump(15, ctx.RB[:, :])

    P.wait("gpsimd", s_mm)
    pair_mm_done = []
    for p in range(8):
        slot = (p % 2) * 3072
        s_w = (ctx.s_w0, ctx.s_w1)[p % 2]
        if p >= 2:
            P.wait("gpsimd", s_mm, pair_mm_done[p - 2])
        for wi, wn in enumerate(("wq_t", "wk_t", "wv_t")):
            P.dma("gpsimd", s_w, W0[:, slot + wi * 1024: slot + (wi + 1) * 1024]
                  .rearrange("q (kc col) -> q kc col", col=128),
                  w[wn][p])
        P.wait("tensor", s_w)
        wsl = lambda wi, k, slot=slot: W0[:, slot + wi * 1024 + k * 128: slot + wi * 1024 + (k + 1) * 128]
        for wi in range(3):
            if not (p == 0 and wi == 0):
                P.wait("tensor", s_sc)
            for g in range(NGRP):
                for k in range(8):
                    P.emit("tensor", lambda E, wi=wi, g=g, k=k, wsl=wsl: E.matmul(
                        ctx.PSBIG[:, ts(g, NG)], wsl(wi, k), Y[:, k * T + ts(g, NG).start: k * T + ts(g, NG).stop],
                        start=(k == 0), stop=(k == 7)))
            P.inc_last("tensor", s_mm)
            P.wait("scalar", s_mm)
            bcol = wi * 8 + p
            dst = (Q2, K2, V2)[wi]
            P.inc("scalar", lambda E, dst=dst, bcol=bcol: E.add(
                dst[:], ctx.PSBIG[:, 0:T], ctx.BIAS[:, bcol:bcol + 1]), s_sc)
        P.barrier2(COMPUTE, s_bar)
        if p == 0:
            ctx.dbg_dump(2, Q2[:])
            ctx.dbg_dump(3, K2[:])
            ctx.dbg_dump(4, V2[:])
        P.emit("vector", lambda E: E.memset(
            VT.rearrange("q (lc s) -> q lc s", s=130)[:, :, 64:65], 1.0))
        P.inc("vector", lambda E: E.memset(
            VT.rearrange("q (lc s) -> q lc s", s=130)[:, :, 129:130], 1.0), s_ve)
        for lb in range(4):
            if lb > 0:
                P.wait("tensor", s_ve)
            for j in range(4):
                lc = lb * 4 + j
                P.emit("tensor", lambda E, lc=lc, j=j: E.transpose(
                    ctx.PSTR[:, ts(j, 128)], V2[:, ts(lc, 128)], ctx.ident_bf[:]))
            P.inc_last("tensor", s_mm)
            P.wait("vector", s_mm)
            for j in range(4):
                lc = lb * 4 + j
                P.inc("vector", lambda E, lc=lc, j=j: E.tensor_copy(
                    VT[:, lc * 130: lc * 130 + 130].rearrange("q (two s) -> q two s", s=65)[:, :, 0:64],
                    ctx.PSTR[:, ts(j, 128)].rearrange("q (two s) -> q two s", s=64)), s_ve)
        P.barrier2(COMPUTE, s_bar)

        for hh in range(2):
            if hh == 0:
                qsrc, ksrc = Q2[0:64, :], K2[0:64, :]
            else:
                P.emit("vector", lambda E: E.tensor_copy(QSTG, Q2[64:128, :]))
                P.inc("vector", lambda E: E.tensor_copy(KSTG, K2[64:128, :]), s_ve)
                P.wait("scalar", s_ve)
                P.wait("tensor", s_ve)
                qsrc, ksrc = QSTG, KSTG
            P.inc("scalar", lambda E, qsrc=qsrc: E.activation(out=ctx.SC1[0:64, :], in_=qsrc, func=AF.Square), s_sc)
            for mh in range(2):
                P.wait("tensor", s_sc)
                for g in range(NGRP):
                    P.emit("tensor", lambda E, mh=mh, g=g, qsrc=qsrc: E.matmul(
                        ctx.PSBIG[:, ts(g, NG)], ctx.orfTs[:, ts(mh, 128)], qsrc[:, ts(g, NG)],
                        start=True, stop=False))
                    P.emit("tensor", lambda E, g=g: E.matmul(
                        ctx.PSBIG[:, ts(g, NG)], ctx.negk_f32[:, 0:128], ctx.SC1[0:64, ts(g, NG)],
                        start=False, stop=True))
                P.inc_last("tensor", s_mm)
                P.wait("scalar", s_mm)
                P.inc("scalar", lambda E, mh=mh: E.activation(
                    out=PHQ(mh), in_=ctx.PSBIG[:, 0:T], func=AF.Exp, bias=ctx.CCOL[:, 1:2]), s_sc)
                P.wait("tensor", s_sc)
            P.inc("scalar", lambda E, ksrc=ksrc: E.activation(out=ctx.SC2[0:64, :], in_=ksrc, func=AF.Square), s_sc)
            for hf in range(2):
                P.wait("tensor", s_sc)
                for j in range(8):
                    lc = hf * 8 + j
                    P.emit("tensor", lambda E, lc=lc, j=j, ksrc=ksrc: E.matmul(
                        ctx.PSBIG[:, j * 256:(j + 1) * 256], ksrc[:, ts(lc, 128)], ctx.orfTs[:],
                        start=True, stop=False))
                    P.emit("tensor", lambda E, lc=lc, j=j: E.matmul(
                        ctx.PSBIG[:, j * 256:(j + 1) * 256], ctx.SC2[0:64, ts(lc, 128)], ctx.negk_f32[:],
                        start=False, stop=True))
                P.inc_last("tensor", s_mm)
                P.wait("scalar", s_mm)
                P.inc("scalar", lambda E, hf=hf: E.activation(
                    out=PKT_half(hf), in_=ctx.PSBIG[:, 0:T], func=AF.Exp, bias=ctx.CCOL[:, 1:2]), s_sc)
                P.wait("tensor", s_sc)
            for hm in range(2):
                for lc in range(16):
                    hf, j = divmod(lc, 8)
                    P.emit("tensor", lambda E, hm=hm, hf=hf, j=j, lc=lc, hh=hh: E.matmul(
                        ctx.PSA[:, hm * 65:(hm + 1) * 65],
                        PKT_half(hf)[:, j * 256 + hm * 128: j * 256 + hm * 128 + 128],
                        VT[:, lc * 130 + hh * 65: lc * 130 + hh * 65 + 65],
                        start=(lc == 0), stop=(lc == 15)))
            P.inc_last("tensor", s_mm)
            P.wait("vector", s_mm)
            P.inc("vector", lambda E: E.tensor_copy(KV[:], ctx.PSA[:, 0:130]), s_ve)
            P.wait("tensor", s_ve)
            for hm in range(2):
                for g in range(NGRP):
                    P.emit("tensor", lambda E, hm=hm, g=g: E.matmul(
                        ctx.PSBIG[0:65, ts(g, NG)], KV[:, hm * 65:(hm + 1) * 65],
                        PHQ(hm)[:, ts(g, NG)], start=(hm == 0), stop=(hm == 1)))
            P.inc_last("tensor", s_mm)
            P.wait("vector", s_mm)
            P.wait("scalar", s_mm)
            P.emit("scalar", lambda E: E.activation(out=ctx.SC1[0:1, 0:T], in_=ctx.PSBIG[64:65, 0:T],
                                                    func=AF.Ln))
            P.fence("scalar", s_sc)
            P.inc("scalar", lambda E: E.activation(out=ctx.SC1[0:1, 0:T], in_=ctx.SC1[0:1, 0:T],
                                                   func=AF.Exp, scale=-1.0), s_sc)
            P.inc("scalar", lambda E: E.copy(ctx.SC2[0:64, :], ctx.PSBIG[0:64, 0:T]), s_sc)
            P.wait("tensor", s_sc)
            P.wait("vector", s_sc)
            for g in range(NGRP):
                psn = (ctx.PSA, ctx.PSB)[g % 2]
                if g >= 2:
                    P.wait("tensor", s_ve)
                P.emit("tensor", lambda E, g=g, psn=psn: E.matmul(
                    psn[0:64, 0:NG], ctx.zp1_f32[:, 0:64], ctx.SC1[:, ts(g, NG)],
                    start=True, stop=True))
                P.inc_last("tensor", s_mm)
                P.wait("vector", s_mm)
                dst = ATT[hh * 64: hh * 64 + 64, p * T + g * NG: p * T + (g + 1) * NG]
                P.inc("vector", lambda E, g=g, dst=dst, psn=psn: E.tensor_mul(
                    dst, ctx.SC2[0:64, ts(g, NG)], psn[0:64, 0:NG]), s_ve)
            if p == 0 and hh == 0:
                ctx.dbg_dump(5, PHQ(0))
                ctx.dbg_dump(6, PKT_half(0))
                ctx.dbg_dump(7, KV[:, 0:130])
                ctx.dbg_dump(8, ctx.SC2[0:64, :], rows=64)
                ctx.dbg_dump(9, ctx.SC1[0:1, :], rows=1)
                ctx.dbg_dump(10, ATT[0:64, 0:T], rows=64)
                ctx.dbg_dump(14, VT[:, 0:2048])
                ctx.dbg_dump(12, PKT_half(1))
                ctx.dbg_dump(13, VT[:, 2048:2080])
            P.barrier2(COMPUTE, s_bar)
        pair_mm_done.append(P.inc_last("tensor", s_mm))
    wo_done_counts = []
    for j in range(8):
        slot = (j % 2) * 1024
        s_w = (ctx.s_w0, ctx.s_w1)[j % 2]
        if j < 2:
            P.wait("gpsimd", s_mm, pair_mm_done[6 + j])
        else:
            P.wait("gpsimd", s_mm, wo_done_counts[j - 2])
        P.dma("gpsimd", s_w, W0[:, slot: slot + 1024].rearrange("q (kc col) -> q kc col", col=128),
              w["wo_t"][j])
        P.wait("tensor", s_w)
        if j > 0:
            P.wait("tensor", s_sc)
        for g in range(NGRP):
            for k in range(8):
                P.emit("tensor", lambda E, j=j, g=g, k=k, slot=slot: E.matmul(
                    ctx.PSBIG[:, ts(g, NG)], W0[:, slot + k * 128: slot + (k + 1) * 128],
                    ATT[:, k * T + g * NG: k * T + (g + 1) * NG], start=(k == 0), stop=(k == 7)))
        wo_done_counts.append(P.inc_last("tensor", s_mm))
        P.wait("scalar", s_mm)
        P.inc("scalar", lambda E, j=j: E.add(
            ctx.A1[:, ts(j, T)], ctx.PSBIG[:, 0:T], ctx.BIAS[:, 24 + j: 25 + j]), s_sc)
    P.barrier2(COMPUTE, s_bar)
    ctx.dbg_dump(11, ctx.A1[:, ts(0, T)])
    ctx.ln_stats(lambda c: ctx.A1[:, ts(c, T)])
    for e in COMPUTE:
        P.wait(e, s_ve)
    ctx.ln_residual_update(lambda c: ctx.A1[:, ts(c, T)])
    P.barrier2(COMPUTE, s_bar)
    if seq_aps.get("xout") is not None:
        P.wait("gpsimd", s_ve)
        for c in range(8):
            P.dma("gpsimd", ctx.s_out, seq_aps["xout"][ts(c, 128), :], X[:, ts(c, T)])



def tile_w(Wt):
    K, M = Wt.shape
    assert K % 128 == 0 and M % 128 == 0
    return np.ascontiguousarray(
        Wt.reshape(K // 128, 128, M // 128, 128).transpose(2, 1, 0, 3))


def make_consts():
    c = np.zeros((128, 832), np.float32)
    c[:, 0:128] = 1.0
    c[:, 128:256] = np.eye(128, dtype=np.float32)
    c[0, 256:384] = 1.0
    c[0:64, 384:640] = -1.0 / 16.0
    c[:, 640] = 1e-5
    c[:, 641] = -LN16
    c[:, 642] = 1e-6
    c[:, 704:832] = np.tril(np.ones((128, 128), np.float32)).T
    return c


def pack_bias_cols(*biases):
    return np.ascontiguousarray(
        np.concatenate([np.asarray(b, np.float32).reshape(-1, 128).T for b in biases], axis=1))



def build_l1_standalone(nseq):
    nc = bass.Bass("TRN2", target_bir_lowering=False)
    xin = nc.dram_tensor("xin", [nseq, 1024, T], F32, kind="ExternalInput").ap()
    xout = nc.dram_tensor("xout", [nseq, 1024, T], F32, kind="ExternalOutput").ap()
    consts = nc.dram_tensor("consts", [128, 832], F32, kind="ExternalInput").ap()
    w = {}
    for n in ("wq_t", "wk_t", "wv_t", "wo_t"):
        w[n] = nc.dram_tensor(n, [8, 128, 8, 128], F32, kind="ExternalInput").ap()
    w["biaspack"] = nc.dram_tensor("biaspack", [128, 32], F32, kind="ExternalInput").ap()
    w["orft"] = nc.dram_tensor("orft", [64, 256], F32, kind="ExternalInput").ap()
    dbg = nc.dram_tensor("dbg", [16, 128, T], F32, kind="ExternalOutput").ap()
    stack = contextlib.ExitStack()
    with stack:
        P = Prog(nc, stack)
        ctx = Ctx(P)
        ctx.dbg = dbg
        ctx.load_consts(consts)
        for s in range(nseq):
            emit_l1_attn(ctx, {"xin": xin[s], "xout": xout[s]}, w)
        P.wait("gpsimd", ctx.s_out)
        print("n queue items:", P.n_instr())
        P.finalize()
    return nc


def l1_in_maps(x_fm, weights):
    m = {
        "xin": x_fm,
        "consts": make_consts(),
        "wq_t": tile_w(weights["Wq"].T),
        "wk_t": tile_w(weights["Wk"].T),
        "wv_t": tile_w(weights["Wv"].T),
        "wo_t": tile_w(weights["Wo"].T),
        "biaspack": pack_bias_cols(weights["bq"], weights["bk"], weights["bv"], weights["bo"]),
        "orft": np.ascontiguousarray(weights["orf"].T),
    }
    return m



class DStream:
    def __init__(self, P, name, queue, nslots=2):
        self.P = P
        self.queue = queue
        self.n = nslots
        self.sems = [P.sem(name + str(i)) for i in range(nslots)]
        self.war = [None] * nslots
        self.dwar = [None] * nslots
        self.pending = None

    def load(self, i, out_ap, in_ap, spill=None):
        slot = i % self.n
        if self.war[slot] is not None:
            self.P.wait(self.queue, *self.war[slot])
            self.war[slot] = None
        if self.dwar[slot] is not None:
            self.P.wait(self.queue, *self.dwar[slot])
            self.dwar[slot] = None
        c = self.P.dma(self.queue, self.sems[slot], out_ap, in_ap)
        if self.pending is not None:
            self._flush()
        if spill is not None:
            self.pending = (slot, spill[1], spill[0])
        return (self.sems[slot], c)

    def _flush(self):
        slot, srcap, dstap = self.pending
        self.pending = None
        sem = self.sems[slot]
        self.P.wait(self.queue, sem, self.P.cnt[sem])
        c2 = self.P.dma(self.queue, sem, dstap, srcap)
        self.dwar[slot] = (sem, c2)

    def wait(self, eng, tok):
        self.P.wait(eng, tok[0], tok[1])

    def set_war(self, i, sem, count):
        self.war[i % self.n] = (sem, count)

    def reset(self):
        if self.pending is not None:
            self._flush()
        self.war = [None] * self.n



def emit_l1_ffn(ctx, seq_aps, w):
    P = ctx.P
    X, XR, Z = ctx.XB, ctx.A1, ctx.A2
    TR0, TR1, W0 = ctx.TR0, ctx.TR1, ctx.W0
    s_dma, s_mm, s_ve, s_sc, s_gp, s_bar = ctx.s_dma, ctx.s_mm, ctx.s_ve, ctx.s_sc, ctx.s_gp, ctx.s_bar
    UV = TR0[:, 0:16 * 512]
    SD = [TR0[:, 16 * 512 + i * 4096: 16 * 512 + (i + 1) * 4096] for i in range(2)]
    A = TR1[:, 0:4096]
    ET = [TR1[:, 4096:8192], TR1[:, 8192:12288]]
    QSTG = TR1[0:64, 4096:6144]
    ATMP = [TR1[:, 4096 + i * 512: 4608 + i * 512] for i in range(4)]
    SDH = [TR1[:, 6144 + i * 512: 6656 + i * 512] for i in range(4)]
    ATS = W0[:, 0:1024]
    UES = W0[:, 1024:2048]
    QF = lambda c: TR1[:, c * T:(c + 1) * T]
    if not hasattr(ctx, "st_w"):
        ctx.st_w = DStream(P, "st_w", "gpsimd")
    if not hasattr(ctx, "st_sd"):
        ctx.st_sd = DStream(P, "st_sd", "gpsimd", nslots=4)
        ctx.st_sdr = DStream(P, "st_sdr", "sync")
        ctx.st_at = DStream(P, "st_at", "sync", nslots=4)
        ctx.st_ats = DStream(P, "st_ats", "sync", nslots=4)
        ctx.st_ue = DStream(P, "st_ue", "gpsimd", nslots=4)
    stw, stsd, stsdr, stat, stats_, stue = (ctx.st_w, ctx.st_sd, ctx.st_sdr,
                                            ctx.st_at, ctx.st_ats, ctx.st_ue)

    if seq_aps.get("xin") is not None:
        P.wait("gpsimd", ctx.s_out)
        for c in range(8):
            P.dma("gpsimd", s_dma, X[:, ts(c, T)], seq_aps["xin"][ts(c, 128), :])
        P.wait("gpsimd", s_dma)
        for e in COMPUTE:
            P.wait(e, s_dma)
            P.wait(e, ctx.s_dmh)
    ctx.ln_stats(lambda c: X[:, ts(c, T)])
    for e in COMPUTE:
        P.wait(e, ctx.s_ve)
    ctx.ln_apply(lambda c: X[:, ts(c, T)], lambda c: XR[:, ts(c, T)])
    P.barrier2(COMPUTE, s_bar)

    if not getattr(ctx, "_l1f_prepped", False):
        ctx._l1f_prepped = True
        for p in range(2):
            for h in range(4):
                P.dma("gpsimd", s_dma, ctx.KEYS[:, (p * 4 + h) * 64:(p * 4 + h) * 64 + 64],
                      w["keysp"][p, h])
        P.wait("gpsimd", s_dma)
        for e in COMPUTE:
            P.wait(e, s_dma)
    P.wait("gpsimd", s_mm)
    ctx.wcache("c_wq", [4, 128, 1024])
    ctx.wcache("c_det", [8, 128, 4096])
    ctx.wcache("c_ue", [64, 128, 512])
    ctx.wcache("c_wup", [32, 128, 1024])
    ctx.wcache("c_wdn", [8, 128, 4096])
    for j in range(4):
        tok = ctx.cload(stw, j, W0[:, (j % 2) * 4096: (j % 2) * 4096 + 1024], "c_wq", j, w["wq_t"][j],
                        view=W0[:, (j % 2) * 4096: (j % 2) * 4096 + 1024].rearrange("q (kc col) -> q kc col", col=128))
        stw.wait("tensor", tok)
        for g in range(NGRP):
            for k in range(8):
                P.emit("tensor", lambda E, j=j, g=g, k=k: E.matmul(
                    ctx.PSBIG[:, ts(g, NG)], W0[:, (j % 2) * 4096 + k * 128: (j % 2) * 4096 + (k + 1) * 128],
                    XR[:, k * T + g * NG: k * T + (g + 1) * NG], start=(k == 0), stop=(k == 7)))
        cnt = P.inc_last("tensor", s_mm)
        stw.set_war(j, "s_mm", cnt)
        P.wait("scalar", s_mm)
        P.inc("scalar", lambda E, j=j: E.copy(QF(j)[:], ctx.PSBIG[:, 0:T]), s_sc)
        P.wait("tensor", s_sc)
    ctx.wcmark("c_wq")
    P.barrier2(COMPUTE, s_bar)
    for p in range(2):
        for h in range(4):
            c = p * 2 + h // 2
            if h % 2 == 0:
                qsrc = QF(c)[0:64, :]
            else:
                P.inc("vector", lambda E, c=c: E.tensor_copy(QSTG, QF(c)[64:128, :]), s_ve)
                P.wait("tensor", s_ve)
                qsrc = QSTG
            for half in range(2):
                ps = (ctx.PSA, ctx.PSB)[half]
                if not (p == 0 and h == 0):
                    P.wait("tensor", s_ve)
                for q8 in range(8):
                    tc = half * 8 + q8
                    P.emit("tensor", lambda E, tc=tc, q8=q8, p=p, h=h, qsrc=qsrc, ps=ps: E.matmul(
                        ps[:, q8 * 64:(q8 + 1) * 64], qsrc[:, ts(tc, 128)],
                        ctx.KEYS[:, (p * 4 + h) * 64:(p * 4 + h) * 64 + 64],
                        start=True, stop=True))
                P.inc_last("tensor", s_mm)
                P.wait("vector", s_mm)
                P.inc("vector", lambda E, half=half, p=p, h=h, ps=ps: E.tensor_copy(
                    UV.rearrange("q (tc w) -> q tc w", w=512)[:, half * 8:(half + 1) * 8,
                                 (p * 4 + h) * 64:(p * 4 + h) * 64 + 64],
                    ps[:].rearrange("q (t w) -> q t w", w=64)), s_ve)
    P.barrier2(COMPUTE, s_bar)

    for tc in range(16):
        for h in range(4):
            simx = UV[:, tc * 512 + h * 64: tc * 512 + h * 64 + 64]
            simy = UV[:, tc * 512 + 256 + h * 64: tc * 512 + 256 + h * 64 + 64]
            col = tc * 4 + h
            P.emit("vector", lambda E, simx=simx: E.max(out=ctx.SC1[:, 0:8], in_=simx))
            P.emit("vector", lambda E, simy=simy: E.max(out=ctx.SC1[:, 8:16], in_=simy))
            P.fence("vector", s_ve)
            P.emit("vector", lambda E: E.tensor_tensor(
                out=ctx.SC1[:, 16:80].rearrange("q (a b) -> q a b", b=8),
                in0=ctx.SC1[:, 0:8].rearrange("q (a b) -> q a b", b=1).broadcast_to([128, 8, 8]),
                in1=ctx.SC1[:, 8:16].rearrange("q (a b) -> q a b", a=1).broadcast_to([128, 8, 8]),
                op=OP.add))
            P.fence("vector", s_ve)
            P.emit("vector", lambda E: E.max(out=ctx.SC1[:, 80:88], in_=ctx.SC1[:, 16:80]))
            P.fence("vector", s_ve)
            P.inc("vector", lambda E: E.tensor_scalar_mul(ctx.SC1[:, 96:97], ctx.SC1[:, 80:81], -1.0), s_ve)
            P.wait("scalar", s_ve)
            P.inc("scalar", lambda E: E.activation(out=ctx.SC1[:, 88:96], in_=ctx.SC1[:, 80:88],
                                                   func=AF.Exp, bias=ctx.SC1[:, 96:97]), s_sc)
            P.wait("vector", s_sc)
            P.emit("vector", lambda E: E.reduce_sum(ctx.SC1[:, 97:98], ctx.SC1[:, 88:96], axis=AX.X))
            P.fence("vector", s_ve)
            P.emit("vector", lambda E, col=col: E.reciprocal(ctx.SC2[:, col:col + 1], ctx.SC1[:, 97:98]))
            P.inc("vector", lambda E, col=col: E.tensor_scalar_mul(
                ctx.SC2[:, 64 + col: 65 + col], ctx.SC1[:, 95:96], 1.0 - 1.0 / 32.0), s_ve)
            P.wait("scalar", s_ve)
            P.emit("scalar", lambda E, simx=simx: E.activation(out=simx, in_=simx, func=AF.Exp,
                                                               bias=ctx.SC1[:, 96:97]))
            P.inc("scalar", lambda E, simy=simy: E.activation(out=simy, in_=simy, func=AF.Exp), s_sc)
            P.wait("vector", s_sc)
            SC2bf = ctx.SC2[:].bitcast(BF16)
            P.emit("vector", lambda E, simy=simy, col=col: E.tensor_tensor(
                out=simy, in0=simy, in1=ctx.SC2[:, col:col + 1].broadcast_to([128, 64]),
                op=OP.mult))
            P.fence("vector", s_ve)
            P.inc("vector", lambda E, col=col, SC2bf=SC2bf: E.tensor_tensor(
                out=SC2bf[:, 256 + col: 257 + col], in0=ctx.SC2[:, 64 + col: 65 + col],
                in1=ctx.SC2[:, col: col + 1], op=OP.mult), s_ve)
    P.barrier2(COMPUTE, s_bar)

    P.wait("gpsimd", s_mm)
    stw.war = [None, None]
    for eg in range(8):
        slot = (eg % 2) * 4096
        tok = ctx.cload(stw, eg, W0[:, slot: slot + 4096], "c_det", eg, w["det"][eg],
                        view=W0[:, slot: slot + 4096].rearrange("q (kc c) -> q kc c", c=512))
        stw.wait("tensor", tok)
        sevac = getattr(ctx, "_sd_evac", {})
        for tc in range(16):
            ps = (ctx.PSA, ctx.PSB)[tc % 2]
            if (eg, tc) != (0, 0) and (eg, tc) != (0, 1):
                P.wait("tensor", s_sc, sevac.get((eg, tc - 2), sevac.get((eg - 1, tc + 14), 0)))
            for k in range(8):
                P.emit("tensor", lambda E, eg=eg, tc=tc, k=k, ps=ps, slot=slot: E.matmul(
                    ps[:], XR[:, k * T + tc * 128: k * T + tc * 128 + 128],
                    W0[:, slot + k * 512: slot + k * 512 + 512], start=(k == 0), stop=(k == 7)))
            cnt = P.inc_last("tensor", s_mm)
            P.wait("scalar", s_mm)
            war = getattr(ctx, "_sdh_war", [None] * 4)
            if war[tc % 4] is not None:
                P.wait("scalar", *war[tc % 4])
            sevac[(eg, tc)] = P.inc("scalar", lambda E, tc=tc, ps=ps: E.copy(SDH[tc % 4][:], ps[:]), s_sc)
            ctx._sd_evac = sevac
            P.wait("gpsimd", s_sc)
            sp = stsd.load(tc, w["sd_scr"][tc, eg], SDH[tc % 4][:])
            war[tc % 4] = sp
            ctx._sdh_war = war
        stw.set_war(eg, "s_mm", P.cnt[s_mm])
    ctx.drain(stw)
    ctx.wcmark("c_det")
    for e in ("sync", "vector", "scalar", "tensor"):
        for sm in stsd.sems:
            P.wait(e, sm)
    P.barrier2(COMPUTE, s_bar)

    SDH4 = [TR1[:, 5120 + i * 512: 5632 + i * 512] for i in range(4)]

    def pass2_tc(tc):
        slot = tc % 2
        tok = stsdr.load(tc, SD[slot].rearrange("q (eg c) -> q eg c", c=512),
                         w["sd_scr"][tc].rearrange("eg q c -> q eg c"))
        stsdr.wait("vector", tok)
        for sm in stat.sems:
            P.wait("vector", sm)
        for h in range(4):
            et = ET[h % 2]
            col = tc * 4 + h
            ux = UV[:, tc * 512 + h * 64: tc * 512 + h * 64 + 64]
            vy = UV[:, tc * 512 + 256 + h * 64: tc * 512 + 256 + h * 64 + 64]
            P.inc("vector", lambda E, ux=ux, vy=vy, et=et: E.tensor_tensor(
                out=et.rearrange("q (a b) -> q a b", b=64),
                in0=ux.rearrange("q (a b) -> q a b", b=1).broadcast_to([128, 64, 64]),
                in1=vy.rearrange("q (a b) -> q a b", a=1).broadcast_to([128, 64, 64]),
                op=OP.mult), s_ve)
            P.fence("vector", s_ve)
            P.inc("vector", lambda E, et=et, col=col: E.scalar_tensor_tensor(
                out=et, in0=et, scalar=ctx.SC2[:].bitcast(BF16)[:, 256 + col: 257 + col], in1=et,
                op0=OP.is_ge, op1=OP.mult), s_ve)
            P.fence("vector", s_ve)
            P.inc("vector", lambda E, et=et, slot=slot: E.tensor_mul(
                et, SD[slot], et), s_ve)
            P.wait("scalar", s_ve)
            P.inc("scalar", lambda E, et=et: E.activation(out=et, in_=et, func=AF.Silu), s_sc)
            P.wait("vector", s_sc)
            P.fence("vector", s_ve)
            if h == 0:
                P.inc("vector", lambda E, et=et: E.tensor_copy(A[:], et), s_ve)
            else:
                P.inc("vector", lambda E, et=et: E.tensor_add(A[:], A[:], et), s_ve)
        stsdr.set_war(tc, "s_ve", P.cnt[s_ve])
        P.wait("tensor", s_ve)
        for tb in range(8):
            if tb > 0:
                P.wait("tensor", s_ve)
            for j in range(4):
                ec = tb * 4 + j
                P.emit("tensor", lambda E, ec=ec, j=j: E.transpose(
                    ctx.PSTR[:, ts(j, 128)], A[:, ts(ec, 128)], ctx.ident_bf[:]))
            P.inc_last("tensor", s_mm)
            P.wait("vector", s_mm)
            ai = tc * 8 + tb
            awar = getattr(ctx, "_atmp_war", [None] * 4)
            if awar[tb % 4] is not None:
                P.wait("vector", *awar[tb % 4])
            P.inc("vector", lambda E, tb=tb: E.tensor_copy(ATMP[tb % 4][:], ctx.PSTR[:, 0:512]), s_ve)
            P.wait("sync", s_ve)
            sp = stat.load(ai, w["at_scr"][tb * 4:(tb + 1) * 4, tc].rearrange("b q c -> q b c"),
                           ATMP[tb % 4][:].rearrange("q (b c) -> q b c", c=128))
            awar[tb % 4] = sp
            ctx._atmp_war = awar

    def p2b_tokg_gen(tokg, at_tots):
        for sm, tot in zip(stat.sems, at_tots):
            P.wait("sync", sm, tot)
        P.wait("sync", s_mm)
        P.wait("gpsimd", s_mm)
        for half in range(2):
            for sec in range(2):
                for ec in range(sec * 16, (sec + 1) * 16):
                    slot = ec % 4
                    t1 = stats_.load(ec, W0[:, slot * 512: slot * 512 + 512].rearrange("q (b c) -> q b c", c=128),
                                     w["at_scr"][ec, tokg * 4:(tokg + 1) * 4].rearrange("b q c -> q b c"))
                    ue_st = stue
                    if "c_ue" in ctx.wcf:
                        if not hasattr(ctx, "st_ue2"):
                            ctx.st_ue2 = DStream(P, "st_ue2", "sync", nslots=4)
                        ue_st = ctx.st_ue2
                        if not getattr(ctx, "_ue2_sync", False):
                            ue_st.war = list(stue.war)
                            stue.war = [None] * stue.n
                            ctx._ue2_sync = True
                    t2 = ctx.cload(ue_st, ec, W0[:, 2048 + slot * 512: 2048 + slot * 512 + 512],
                                   "c_ue", half * 32 + ec,
                                   w["ue"][ec * 128:(ec + 1) * 128, half * 512:(half + 1) * 512])
                    stats_.wait("tensor", t1)
                    ue_st.wait("tensor", t2)
                    for mq in range(4):
                        P.emit("tensor", lambda E, mq=mq, slot=slot, ec=ec: E.matmul(
                            ctx.PSBIG[:, ts(mq, NG)],
                            W0[:, 2048 + slot * 512 + mq * 128: 2048 + slot * 512 + (mq + 1) * 128],
                            W0[:, slot * 512: slot * 512 + 512],
                            start=(ec == 0), stop=(ec == 31)))
                    cnt = P.inc_last("tensor", s_mm)
                    stats_.set_war(ec, "s_mm", cnt)
                    ue_st.set_war(ec, "s_mm", cnt)
                yield
            P.wait("scalar", s_mm)
            for mq in range(4):
                md = half * 4 + mq
                P.inc("scalar", lambda E, mq=mq, md=md, tokg=tokg: E.copy(
                    Z[:, md * T + tokg * 512: md * T + (tokg + 1) * 512], ctx.PSBIG[:, ts(mq, NG)]), s_sc)
            P.wait("tensor", s_sc)
        if tokg == 0:
            ctx.wcmark("c_ue")
            ctx.drain(stue)

    gen = None
    for grp in range(4):
        for tc in range(grp * 4, (grp + 1) * 4):
            pass2_tc(tc)
            if gen is not None:
                next(gen, None)
        at_tots = tuple(P.cnt[sm] for sm in stat.sems)
        if gen is not None:
            for _ in gen:
                pass
        gen = p2b_tokg_gen(grp, at_tots)
    for _ in gen:
        pass
    P.barrier2(COMPUTE, s_bar)

    UP = lambda j: TR0[:, j * 512:(j + 1) * 512]
    if not hasattr(ctx, "st_w4"):
        ctx.st_w4 = DStream(P, "st_w4", "gpsimd", nslots=4)
    stw4 = ctx.st_w4
    for tt in range(4):
        P.wait("gpsimd", s_mm)
        stw4.reset()
        ctx.drain(stw)
        up_evac = {}
        for j in range(32):
            tok = ctx.cload(stw4, j, W0[:, (j % 4) * 1024: (j % 4) * 1024 + 1024], "c_wup", j, w["wup_t"][j],
                            view=W0[:, (j % 4) * 1024: (j % 4) * 1024 + 1024].rearrange("q (kc col) -> q kc col", col=128))
            stw4.wait("tensor", tok)
            ps = (ctx.PSA, ctx.PSB)[j % 2]
            if j >= 2:
                P.wait("tensor", s_sc, up_evac[j - 2])
            elif not (tt == 0 and j < 2):
                P.wait("tensor", s_sc)
                P.wait("tensor", s_ve)
            for k in range(8):
                P.emit("tensor", lambda E, j=j, k=k, tt=tt, ps=ps: E.matmul(
                    ps[:], W0[:, (j % 4) * 1024 + k * 128: (j % 4) * 1024 + (k + 1) * 128],
                    XR[:, k * T + tt * 512: k * T + (tt + 1) * 512], start=(k == 0), stop=(k == 7)))
            cnt = P.inc_last("tensor", s_mm)
            stw4.set_war(j, "s_mm", cnt)
            P.wait("scalar", s_mm)
            up_evac[j] = P.inc("scalar", lambda E, j=j, ps=ps: E.activation(out=UP(j)[:], in_=ps[:], func=AF.Silu), s_sc)
        P.wait("tensor", s_sc)
        P.wait("gpsimd", s_mm)
        stw.war = [None, None]
        ctx.drain(stw4)
        for j in range(8):
            tok = ctx.cload(stw, j, W0[:, (j % 2) * 4096: (j % 2) * 4096 + 4096], "c_wdn", j, w["wdn_t"][j],
                           view=W0[:, (j % 2) * 4096: (j % 2) * 4096 + 4096].rearrange("q (kc col) -> q kc col", col=128))
            stw.wait("tensor", tok)
            ps = (ctx.PSA, ctx.PSB)[j % 2]
            if j >= 2:
                P.wait("tensor", s_ve)
            for k in range(32):
                P.emit("tensor", lambda E, j=j, k=k, ps=ps: E.matmul(
                    ps[:], W0[:, (j % 2) * 4096 + k * 128: (j % 2) * 4096 + (k + 1) * 128],
                    UP(k)[:], start=(k == 0), stop=(k == 31)))
            cnt = P.inc_last("tensor", s_mm)
            stw.set_war(j, "s_mm", cnt)
            P.wait("vector", s_mm)
            P.inc("vector", lambda E, j=j, tt=tt, ps=ps: E.scalar_tensor_tensor(
                out=Z[:, j * T + tt * 512: j * T + (tt + 1) * 512], in0=ps[:], scalar=1.0,
                in1=Z[:, j * T + tt * 512: j * T + (tt + 1) * 512], op0=OP.mult, op1=OP.add), s_ve)
        if tt == 0:
            ctx.wcmark("c_wup")
            ctx.wcmark("c_wdn")
    P.barrier2(COMPUTE, s_bar)

    ctx.ln_stats(lambda c: Z[:, ts(c, T)])
    for e in COMPUTE:
        P.wait(e, ctx.s_ve)
    ctx.ln_residual_update(lambda c: Z[:, ts(c, T)])
    P.barrier2(COMPUTE, s_bar)
    if seq_aps.get("xout") is not None:
        P.wait("gpsimd", ctx.s_ve)
        for c in range(8):
            P.dma("gpsimd", ctx.s_out, seq_aps["xout"][ts(c, 128), :], X[:, ts(c, T)])


def build_l1f_standalone(nseq):
    nc = bass.Bass("TRN2", target_bir_lowering=False)
    xin = nc.dram_tensor("xin", [nseq, 1024, T], F32, kind="ExternalInput").ap()
    xout = nc.dram_tensor("xout", [nseq, 1024, T], F32, kind="ExternalOutput").ap()
    consts = nc.dram_tensor("consts", [128, 832], F32, kind="ExternalInput").ap()
    w = {
        "wup_t": nc.dram_tensor("wup_t", [32, 128, 8, 128], F32, kind="ExternalInput").ap(),
        "wdn_t": nc.dram_tensor("wdn_t", [8, 128, 32, 128], F32, kind="ExternalInput").ap(),
        "wq_t": nc.dram_tensor("wq_t", [4, 128, 8, 128], F32, kind="ExternalInput").ap(),
        "keysp": nc.dram_tensor("keysp", [2, 4, 64, 64], F32, kind="ExternalInput").ap(),
        "det": nc.dram_tensor("det", [8, 128, 8, 512], F32, kind="ExternalInput").ap(),
        "ue": nc.dram_tensor("ue", [4096, 1024], F32, kind="ExternalInput").ap(),
        "sd_scr": nc.dram_tensor("sd_scr", [16, 8, 128, 512], BF16, kind="Internal").ap(),
        "at_scr": nc.dram_tensor("at_scr", [32, 16, 128, 128], BF16, kind="Internal").ap(),
    }
    stack = contextlib.ExitStack()
    with stack:
        P = Prog(nc, stack)
        ctx = Ctx(P)
        ctx.use_wcache = True
        ctx.load_consts(consts)
        for s in range(nseq):
            emit_l1_ffn(ctx, {"xin": xin[s], "xout": xout[s]}, w)
        P.wait("gpsimd", ctx.s_out)
        print("n queue items:", P.n_instr())
        P.finalize()
    return nc


def l1f_in_maps(x_fm, weights):
    keys = weights["keys"]
    keysp = np.stack([np.stack([np.ascontiguousarray(keys[h, :, p, :].T) for h in range(4)])
                      for p in range(2)])
    det = np.ascontiguousarray(
        weights["down_embed"].T.reshape(8, 128, 8, 512).transpose(2, 1, 0, 3))
    return {
        "xin": x_fm,
        "consts": make_consts(),
        "wup_t": tile_w(weights["W_up"].T),
        "wdn_t": tile_w(weights["W_down"].T),
        "wq_t": tile_w(weights["W_q"].T),
        "keysp": keysp,
        "det": det,
        "ue": weights["up_embed"],
    }


def np_l1f_ref(x, w):
    B, Tn, H = x.shape
    HEADS, NK, K = 4, 64, 8
    xd = x.astype(np.float64)
    wd = {k: np.asarray(v, np.float64) for k, v in w.items()}

    def _ln(t, eps=1e-5):
        m = t.mean(-1, keepdims=True)
        v = t.var(-1, keepdims=True)
        return (t - m) / np.sqrt(v + eps)

    def silu(t):
        return t / (1 + np.exp(-t))

    h = _ln(xd)
    q = np.einsum('btd,ed->bte', h, wd["W_q"]).reshape(B, Tn, 2, HEADS, 64).transpose(2, 0, 1, 3, 4)
    sim = np.einsum('pbthn,hkpn->pbthk', q, wd["keys"])
    def topk(a, k):
        idx = np.argsort(-a, axis=-1, kind="stable")[..., :k]
        return np.take_along_axis(a, idx, -1), idx
    sx, ix = topk(sim[0], K)
    sy, iy = topk(sim[1], K)
    all_scores = (sx[..., :, None] + sy[..., None, :]).reshape(B, Tn, HEADS, K * K)
    all_idx = (ix[..., :, None] * NK + iy[..., None, :]).reshape(B, Tn, HEADS, K * K)
    scores, pk = topk(all_scores, K)
    indices = np.take_along_axis(all_idx, pk, axis=-1)
    de = wd["down_embed"][indices]
    ue = wd["up_embed"][indices]
    sm = np.exp(scores - scores.max(-1, keepdims=True))
    sm = sm / sm.sum(-1, keepdims=True)
    wgt = silu(np.einsum('btd,bthkd->bthk', h, de) * sm)
    experts = np.einsum('bthk,bthkd->btd', wgt, ue)
    dense = np.einsum('bti,di->btd', silu(np.einsum('btd,id->bti', h, wd["W_up"])), wd["W_down"])
    z = dense + experts
    return (xd + _ln(z)).astype(np.float32)



def emit_l0_attn(ctx, seq_aps, w):
    P = ctx.P
    X, Y, H = ctx.XB, ctx.A1, ctx.A2
    TR0, TR1, W0 = ctx.TR0, ctx.TR1, ctx.W0
    ATT = TR0
    s_dma, s_mm, s_ve, s_sc, s_gp, s_bar = ctx.s_dma, ctx.s_mm, ctx.s_ve, ctx.s_sc, ctx.s_gp, ctx.s_bar
    stw = ctx.st_w if hasattr(ctx, "st_w") else None
    if stw is None:
        ctx.st_w = DStream(P, "st_w", "gpsimd")
        stw = ctx.st_w
    VH = TR1[0:64, 0:2 * T]
    STG = TR1[64:128, 0:2 * T]
    QPH = TR1[0:64, 2 * T:4 * T]
    KPH = TR1[0:64, 4 * T:6 * T]
    VTOK = W0[:, 6144:6144 + 1040]
    KTOK = ctx.VT2[:, 0:1024]
    SMS = ctx.SMS
    KVS = ctx.KVS
    gcol = lambda slot: (lambda c, slot=slot: ctx.BIAS[:, slot * 8 + c: slot * 8 + c + 1])

    if not getattr(ctx, "_l0a_prepped", False):
        ctx._l0a_prepped = True
        P.dma("gpsimd", s_dma, ctx.PROJB[:], w["proj"][:, :])
    P.wait("sync", ctx.s_sc)
    P.wait("sync", ctx.s_ve)
    P.dma("sync", ctx.s_dmh, ctx.BIAS[:, 0:72], w["biaspack_l0a"][:, :])
    if seq_aps.get("xin") is not None:
        P.wait("gpsimd", ctx.s_out)
        for c in range(8):
            P.dma("gpsimd", s_dma, X[:, ts(c, T)], seq_aps["xin"][ts(c, 128), :])
        P.wait("gpsimd", s_dma)
        for e in COMPUTE:
            P.wait(e, s_dma)
            P.wait(e, ctx.s_dmh)
    ctx.ln_stats(lambda c: X[:, ts(c, T)])
    for e in COMPUTE:
        P.wait(e, ctx.s_ve)
    ctx.ln_apply(lambda c: X[:, ts(c, T)], lambda c: Y[:, ts(c, T)])
    P.barrier2(COMPUTE, s_bar)
    ctx.ln_stats(lambda c: Y[:, ts(c, T)])
    for e in COMPUTE:
        P.wait(e, ctx.s_ve)
    ctx.ln_apply_affine(lambda c: Y[:, ts(c, T)], lambda c: H[:, ts(c, T)],
                        gcol(0), gcol(1))
    P.barrier2(COMPUTE, s_bar)

    P.emit("vector", lambda E: E.memset(
        VTOK.rearrange("q (lc s) -> q lc s", s=65)[:, :, 64:65], 1.0))
    P.inc_last("vector", s_ve)

    P.wait("gpsimd", s_mm)
    stw.war = [None, None]
    ctx.wcache("a_qkv", [24, 128, 1024])
    ctx.wcache("a_wo", [8, 128, 1024])
    ctx.wcache("a_w1", [32, 128, 1024])
    ctx.wcache("a_w2", [8, 128, 4096])
    for p in range(8):
        slot = (p % 2) * 3072
        for wi, wn in enumerate(("wq_t", "wk_t", "wv_t")):
            tok = ctx.cload(stw, p, W0[:, slot + wi * 1024: slot + (wi + 1) * 1024], "a_qkv", p * 3 + wi, w[wn][p],
                            view=W0[:, slot + wi * 1024: slot + (wi + 1) * 1024]
                            .rearrange("q (kc col) -> q kc col", col=128))
        stw.wait("tensor", tok)
        for wi in range(3):
            if not (p == 0 and wi == 0):
                P.wait("tensor", s_sc)
            for g in range(NGRP):
                for k in range(8):
                    P.emit("tensor", lambda E, wi=wi, g=g, k=k, slot=slot: E.matmul(
                        ctx.PSBIG[:, ts(g, NG)], W0[:, slot + wi * 1024 + k * 128: slot + wi * 1024 + (k + 1) * 128],
                        H[:, k * T + g * NG: k * T + (g + 1) * NG], start=(k == 0), stop=(k == 7)))
            P.inc_last("tensor", s_mm)
            P.wait("scalar", s_mm)
            if wi == 2:
                P.emit("scalar", lambda E: E.copy(VH[:, 0:T], ctx.PSBIG[0:64, 0:T]))
                P.inc("scalar", lambda E: E.copy(VH[:, T:2 * T], ctx.PSBIG[64:128, 0:T]), s_sc)
            else:
                dsts = (QPH, KPH)[wi]
                P.emit("scalar", lambda E, dsts=dsts: E.copy(dsts[:, 0:T], ctx.PSBIG[0:64, 0:T]))
                P.inc("scalar", lambda E, dsts=dsts: E.copy(dsts[:, T:2 * T], ctx.PSBIG[64:128, 0:T]), s_sc)
                P.wait("tensor", s_sc)
                phi_done = {}
                for hh in range(2):
                    for g in range(NGRP):
                        gi = hh * NGRP + g
                        psn = (ctx.PSA, ctx.PSB)[gi % 2]
                        if gi >= 2:
                            P.wait("tensor", s_ve, phi_done[gi - 2])
                        P.emit("tensor", lambda E, hh=hh, g=g, dsts=dsts, psn=psn: E.matmul(
                            psn[0:64, 0:NG], ctx.PROJB[:],
                            dsts[:, hh * T + g * NG: hh * T + (g + 1) * NG],
                            start=True, stop=True))
                        P.inc_last("tensor", s_mm)
                        sl = (gi % 2) * 512
                        P.wait("vector", s_mm)
                        P.fence("vector", s_ve)
                        P.inc("vector", lambda E, psn=psn, sl=sl: E.tensor_scalar_min(
                            ctx.SC1[0:64, sl:sl + 512], psn[0:64, 0:NG], 0.0), s_ve)
                        P.wait("scalar", s_ve)
                        P.inc("scalar", lambda E, sl=sl: E.activation(
                            out=ctx.SC2[0:64, sl:sl + 512], in_=ctx.SC1[0:64, sl:sl + 512], func=AF.Exp), s_sc)
                        P.wait("vector", s_sc)
                        P.inc("vector", lambda E, psn=psn, sl=sl: E.tensor_scalar_add(
                            ctx.SC1[0:64, sl:sl + 512], psn[0:64, 0:NG], 1.0), s_ve)
                        P.fence("vector", s_ve)
                        phi_done[gi] = P.inc("vector", lambda E, dsts=dsts, hh=hh, g=g, sl=sl: E.tensor_max(
                            dsts[:, hh * T + g * NG:(hh * T) + (g + 1) * NG],
                            ctx.SC1[0:64, sl:sl + 512], ctx.SC2[0:64, sl:sl + 512]), s_ve)
                P.wait("tensor", s_ve)
        stw.set_war(p, "s_mm", P.cnt[s_mm])
        P.barrier2(COMPUTE, s_bar)
        for hh in range(2):
            h_global = p * 2 + hh
            P.wait("tensor", s_sc)
            P.wait("tensor", s_ve)
            for half in range(2):
                P.wait("tensor", s_ve)
                for j in range(8):
                    lc = half * 8 + j
                    P.emit("tensor", lambda E, lc=lc, j=j, hh=hh: E.transpose(
                        ctx.PSTR[:, j * 64:(j + 1) * 64],
                        KPH[0:64, hh * T + lc * 128: hh * T + (lc + 1) * 128], ctx.ident_bf[0:64, 0:64]))
                P.inc_last("tensor", s_mm)
                P.wait("vector", s_mm)
                P.inc("vector", lambda E, half=half: E.tensor_copy(
                    KTOK[:, half * 512:(half + 1) * 512], ctx.PSTR[:, 0:512]), s_ve)
                P.wait("tensor", s_ve)
                for j in range(8):
                    lc = half * 8 + j
                    P.emit("tensor", lambda E, lc=lc, j=j, hh=hh: E.transpose(
                        ctx.PSTR[:, j * 64:(j + 1) * 64],
                        VH[0:64, hh * T + lc * 128: hh * T + (lc + 1) * 128], ctx.ident_bf[0:64, 0:64]))
                P.inc_last("tensor", s_mm)
                P.wait("vector", s_mm)
                P.inc("vector", lambda E, half=half: E.tensor_copy(
                    VTOK.rearrange("q (lc s) -> q lc s", s=65)[:, half * 8:(half + 1) * 8, 0:64],
                    ctx.PSTR[:, 0:512].rearrange("q (lc s) -> q lc s", s=64)), s_ve)
            P.wait("tensor", s_ve)
            P.inc("vector", lambda E: E.memset(ctx.KVF[0:64, 0:65], 0.0), s_ve)
            for c in range(16):
                if c > 0:
                    P.wait("vector", s_mm)
                P.fence("vector", s_ve)
                P.inc("vector", lambda E: E.tensor_copy(KVS[0:64, 0:65], ctx.KVF[0:64, 0:65]), s_ve)
                P.wait("tensor", s_ve)
                P.emit("tensor", lambda E, c=c, hh=hh: E.matmul(
                    ctx.PSB[:, 0:128], KPH[0:64, hh * T + c * 128: hh * T + (c + 1) * 128],
                    QPH[0:64, hh * T + c * 128: hh * T + (c + 1) * 128], start=True, stop=True))
                P.inc_last("tensor", s_mm)
                P.emit("tensor", lambda E, c=c: E.matmul(
                    ctx.PSA[0:64, 0:65], KTOK[:, c * 64:(c + 1) * 64],
                    VTOK[:, c * 65:(c + 1) * 65], start=True, stop=True))
                P.inc_last("tensor", s_mm)
                P.wait("vector", s_mm)
                P.wait("vector", s_ve)
                P.inc("vector", lambda E: E.tensor_mul(SMS[:], ctx.PSB[:, 0:128], ctx.LTRI[:]), s_ve)
                P.wait("vector", s_ve)
                P.inc("vector", lambda E: E.tensor_add(ctx.KVF[0:64, 0:65], ctx.KVF[0:64, 0:65],
                                                       ctx.PSA[0:64, 0:65]), s_ve)
                P.wait("tensor", s_ve)
                P.emit("tensor", lambda E, c=c, hh=hh: E.matmul(
                    ctx.PSBIG[0:65, c * 128:(c + 1) * 128], KVS[0:64, 0:65],
                    QPH[0:64, hh * T + c * 128: hh * T + (c + 1) * 128], start=True, stop=False))
                P.emit("tensor", lambda E, c=c: E.matmul(
                    ctx.PSBIG[0:65, c * 128:(c + 1) * 128], VTOK[:, c * 65:(c + 1) * 65],
                    SMS[:], start=False, stop=True))
                P.inc_last("tensor", s_mm)
            P.wait("scalar", s_mm)
            P.emit("scalar", lambda E: E.activation(out=ctx.SC1[0:1, 0:T], in_=ctx.PSBIG[64:65, 0:T],
                                                    func=AF.Ln, bias=ctx.CCOL[0:1, 2:3]))
            P.fence("scalar", s_sc)
            P.inc("scalar", lambda E: E.activation(out=ctx.SC1[0:1, 0:T], in_=ctx.SC1[0:1, 0:T],
                                                   func=AF.Exp, scale=-1.0), s_sc)
            P.inc("scalar", lambda E: E.copy(ctx.SC2[0:64, 0:T], ctx.PSBIG[0:64, 0:T]), s_sc)
            P.wait("tensor", s_sc)
            P.wait("vector", s_sc)
            for g in range(NGRP):
                psn = (ctx.PSA, ctx.PSB)[g % 2]
                if g >= 2:
                    P.wait("tensor", s_ve)
                P.emit("tensor", lambda E, g=g, psn=psn: E.matmul(
                    psn[0:64, 0:NG], ctx.zp1_f32[:, 0:64], ctx.SC1[:, ts(g, NG)],
                    start=True, stop=True))
                P.inc_last("tensor", s_mm)
                P.wait("vector", s_mm)
                dst = ATT[hh * 64: hh * 64 + 64, p * T + g * NG: p * T + (g + 1) * NG]
                P.inc("vector", lambda E, g=g, dst=dst, psn=psn: E.tensor_mul(
                    dst, ctx.SC2[0:64, ts(g, NG)], psn[0:64, 0:NG]), s_ve)
            P.barrier2(COMPUTE, s_bar)
    ctx.wcmark("a_qkv")
    P.wait("gpsimd", s_mm)
    stw.war = [None, None]
    for j in range(8):
        tok = ctx.cload(stw, j, W0[:, (j % 2) * 1024: (j % 2) * 1024 + 1024], "a_wo", j, w["wo_t"][j],
                        view=W0[:, (j % 2) * 1024: (j % 2) * 1024 + 1024].rearrange("q (kc col) -> q kc col", col=128))
        stw.wait("tensor", tok)
        if j >= 1:
            P.wait("tensor", s_ve)
        for g in range(NGRP):
            for k in range(8):
                P.emit("tensor", lambda E, j=j, g=g, k=k: E.matmul(
                    ctx.PSBIG[:, ts(g, NG)], W0[:, (j % 2) * 1024 + k * 128: (j % 2) * 1024 + (k + 1) * 128],
                    ATT[:, k * T + g * NG: k * T + (g + 1) * NG], start=(k == 0), stop=(k == 7)))
        cnt = P.inc_last("tensor", s_mm)
        stw.set_war(j, "s_mm", cnt)
        P.wait("vector", s_mm)
        P.inc("vector", lambda E, j=j: E.scalar_tensor_tensor(
            out=Y[:, ts(j, T)], in0=ctx.PSBIG[:, 0:T], scalar=1.0, in1=Y[:, ts(j, T)],
            op0=OP.mult, op1=OP.add), s_ve)
    P.barrier2(COMPUTE, s_bar)
    ctx.ln_stats(lambda c: Y[:, ts(c, T)])
    for e in COMPUTE:
        P.wait(e, ctx.s_ve)
    ctx.ln_apply_affine(lambda c: Y[:, ts(c, T)], lambda c: H[:, ts(c, T)],
                        gcol(2), gcol(3))
    for c in range(8):
        P.emit("vector", lambda E, c=c: E.tensor_scalar_add(
            Y[:, ts(c, T)], Y[:, ts(c, T)], ctx.BIAS[:, 32 + c: 33 + c]))
    P.inc_last("vector", s_ve)
    P.barrier2(COMPUTE, s_bar)
    ctx.wcmark("a_wo")
    UPT = lambda j: TR0[:, j * 512:(j + 1) * 512]
    if not hasattr(ctx, "st_w4"):
        ctx.st_w4 = DStream(P, "st_w4", "gpsimd", nslots=4)
    stw4 = ctx.st_w4
    for tt in range(4):
        P.wait("gpsimd", s_mm)
        stw4.reset()
        ctx.drain(stw)
        up_evac = {}
        for j in range(32):
            tok = ctx.cload(stw4, j, W0[:, (j % 4) * 1024: (j % 4) * 1024 + 1024], "a_w1", j, w["w1_t"][j],
                            view=W0[:, (j % 4) * 1024: (j % 4) * 1024 + 1024].rearrange("q (kc col) -> q kc col", col=128))
            stw4.wait("tensor", tok)
            ps = (ctx.PSA, ctx.PSB)[j % 2]
            if j >= 2:
                P.wait("tensor", s_sc, up_evac[j - 2])
            elif not (tt == 0 and j < 2):
                P.wait("tensor", s_sc)
                P.wait("tensor", s_ve)
            for k in range(8):
                P.emit("tensor", lambda E, j=j, k=k, tt=tt, ps=ps: E.matmul(
                    ps[:], W0[:, (j % 4) * 1024 + k * 128: (j % 4) * 1024 + (k + 1) * 128],
                    H[:, k * T + tt * 512: k * T + (tt + 1) * 512], start=(k == 0), stop=(k == 7)))
            cnt = P.inc_last("tensor", s_mm)
            stw4.set_war(j, "s_mm", cnt)
            P.wait("scalar", s_mm)
            up_evac[j] = P.inc("scalar", lambda E, j=j, ps=ps: E.activation(
                out=UPT(j)[:], in_=ps[:], func=AF.Gelu, bias=ctx.BIAS[:, 40 + j: 41 + j]), s_sc)
        P.wait("tensor", s_sc)
        P.wait("gpsimd", s_mm)
        stw.war = [None, None]
        ctx.drain(stw4)
        for j in range(8):
            tok = ctx.cload(stw, j, W0[:, (j % 2) * 4096: (j % 2) * 4096 + 4096], "a_w2", j, w["w2_t"][j],
                           view=W0[:, (j % 2) * 4096: (j % 2) * 4096 + 4096].rearrange("q (kc col) -> q kc col", col=128))
            stw.wait("tensor", tok)
            ps = (ctx.PSA, ctx.PSB)[j % 2]
            if j >= 2:
                P.wait("tensor", s_ve)
            for k in range(32):
                P.emit("tensor", lambda E, j=j, k=k, ps=ps: E.matmul(
                    ps[:], W0[:, (j % 2) * 4096 + k * 128: (j % 2) * 4096 + (k + 1) * 128],
                    UPT(k)[:], start=(k == 0), stop=(k == 31)))
            cnt = P.inc_last("tensor", s_mm)
            stw.set_war(j, "s_mm", cnt)
            P.wait("vector", s_mm)
            P.inc("vector", lambda E, j=j, tt=tt, ps=ps: E.scalar_tensor_tensor(
                out=Y[:, j * T + tt * 512: j * T + (tt + 1) * 512], in0=ps[:], scalar=1.0,
                in1=Y[:, j * T + tt * 512: j * T + (tt + 1) * 512], op0=OP.mult, op1=OP.add), s_ve)
        if tt == 0:
            ctx.wcmark("a_w1")
            ctx.wcmark("a_w2")
    P.barrier2(COMPUTE, s_bar)
    ctx.ln_stats(lambda c: Y[:, ts(c, T)])
    for e in COMPUTE:
        P.wait(e, ctx.s_ve)
    ctx.ln_residual_update(lambda c: Y[:, ts(c, T)])
    P.barrier2(COMPUTE, s_bar)
    if seq_aps.get("xout") is not None:
        P.wait("gpsimd", ctx.s_ve)
        for c in range(8):
            P.dma("gpsimd", ctx.s_out, seq_aps["xout"][ts(c, 128), :], X[:, ts(c, T)])


def build_l0a_standalone(nseq):
    nc = bass.Bass("TRN2", target_bir_lowering=False)
    xin = nc.dram_tensor("xin", [nseq, 1024, T], F32, kind="ExternalInput").ap()
    xout = nc.dram_tensor("xout", [nseq, 1024, T], F32, kind="ExternalOutput").ap()
    consts = nc.dram_tensor("consts", [128, 832], F32, kind="ExternalInput").ap()
    w = {
        "wq_t": nc.dram_tensor("wq_t", [8, 128, 8, 128], F32, kind="ExternalInput").ap(),
        "wk_t": nc.dram_tensor("wk_t", [8, 128, 8, 128], F32, kind="ExternalInput").ap(),
        "wv_t": nc.dram_tensor("wv_t", [8, 128, 8, 128], F32, kind="ExternalInput").ap(),
        "wo_t": nc.dram_tensor("wo_t", [8, 128, 8, 128], F32, kind="ExternalInput").ap(),
        "w1_t": nc.dram_tensor("w1_t", [32, 128, 8, 128], F32, kind="ExternalInput").ap(),
        "w2_t": nc.dram_tensor("w2_t", [8, 128, 32, 128], F32, kind="ExternalInput").ap(),
        "proj": nc.dram_tensor("proj", [64, 64], F32, kind="ExternalInput").ap(),
        "biaspack_l0a": nc.dram_tensor("biaspack_l0a", [128, 72], F32, kind="ExternalInput").ap(),
    }
    stack = contextlib.ExitStack()
    with stack:
        P = Prog(nc, stack)
        ctx = Ctx(P)
        ctx.use_wcache = True
        ctx.load_consts(consts)
        for s in range(nseq):
            emit_l0_attn(ctx, {"xin": xin[s], "xout": xout[s]}, w)
        P.wait("gpsimd", ctx.s_out)
        print("n queue items:", P.n_instr())
        P.finalize()
    return nc


def l0a_in_maps(x_fm, weights):
    bp = pack_bias_cols(weights["g1"], weights["beta1"], weights["g2"], weights["beta2"],
                        weights["b2"], weights["b1"])
    return {
        "xin": x_fm,
        "consts": make_consts(),
        "wq_t": tile_w(weights["wq"]),
        "wk_t": tile_w(weights["wk"]),
        "wv_t": tile_w(weights["wv"]),
        "wo_t": tile_w(weights["wo"]),
        "w1_t": tile_w(weights["w1"]),
        "w2_t": tile_w(weights["w2"]),
        "proj": weights["proj"],
        "biaspack_l0a": bp,
    }


def np_l0a_ref(x, w):
    B, S, D, H, DH = x.shape[0], x.shape[1], 1024, 16, 64
    xd = x.astype(np.float64)
    wd = {k: np.asarray(v, np.float64) for k, v in w.items()}

    def _lnp(t, eps=1e-5):
        m = t.mean(-1, keepdims=True)
        v = t.var(-1, keepdims=True)
        return (t - m) / np.sqrt(v + eps)

    def _lna(t, g, b, eps=1e-5):
        return _lnp(t, eps) * g + b

    def _phi(t, proj):
        u = t @ proj
        return np.where(u > 0, u + 1.0, np.exp(u))

    y = _lnp(xd)
    h = _lna(y, wd["g1"], wd["beta1"])
    q = _phi((h @ wd["wq"]).reshape(B, S, H, DH), wd["proj"])
    k = _phi((h @ wd["wk"]).reshape(B, S, H, DH), wd["proj"])
    v = (h @ wd["wv"]).reshape(B, S, H, DH)
    out = np.zeros((B, S, H, DH))
    for b in range(B):
        for hh in range(H):
            kh = k[b, :, hh]
            vh = v[b, :, hh]
            qh = q[b, :, hh]
            kv = np.einsum('sd,se->sde', kh, vh).cumsum(axis=0)
            ksum = kh.cumsum(axis=0)
            num = np.einsum('sd,sde->se', qh, kv)
            den = np.einsum('sd,sd->s', qh, ksum) + 1e-6
            out[b, :, hh] = num / den[:, None]
    attn = out.reshape(B, S, H * DH) @ wd["wo"]
    u = y + attn
    h2 = _lna(u, wd["g2"], wd["beta2"])
    g = h2 @ wd["w1"] + wd["b1"]
    gel = 0.5 * g * (1 + np.vectorize(math_erf)(g / np.sqrt(2))) if False else None
    from scipy.special import erf as _erf
    gel = 0.5 * g * (1 + _erf(g / np.sqrt(2)))
    z = u + gel @ wd["w2"] + wd["b2"]
    return (xd + _lnp(z)).astype(np.float32)



def emit_l0_ffn(ctx, seq_aps, w):
    P = ctx.P
    X, XL, A2 = ctx.XB, ctx.A1, ctx.A2
    TR0, TR1, W0 = ctx.TR0, ctx.TR1, ctx.W0
    s_dma, s_mm, s_ve, s_sc, s_gp, s_bar = ctx.s_dma, ctx.s_mm, ctx.s_ve, ctx.s_sc, ctx.s_gp, ctx.s_bar
    s_dmh = ctx.s_dmh
    stw = ctx.st_w if hasattr(ctx, "st_w") else None
    if stw is None:
        ctx.st_w = DStream(P, "st_w", "gpsimd")
        stw = ctx.st_w
    if not hasattr(ctx, "st_w4"):
        ctx.st_w4 = DStream(P, "st_w4", "gpsimd", nslots=4)
    stw4 = ctx.st_w4
    QPH = TR1[0:64, 0:2 * T]
    KPH = TR1[0:64, 2 * T:4 * T]
    VH = TR1[0:64, 4 * T:6 * T]
    EB = TR0[:, 0:4096]
    OFM = TR0[:, 4096:4096 + 4 * T]
    MSK = TR0[:, 4096 + 4 * T:4096 + 4 * T + 4096]
    GL = TR0[:, 0:22 * 512]
    VT = ctx.VT2
    ROT = W0
    SH = lambda c: A2[:, c * T:(c + 1) * T]
    HM = lambda c: A2[:, ((c + 4) % 8) * T:(((c + 4) % 8) + 1) * T]

    if not getattr(ctx, "_l0f_prepped", False):
        ctx._l0f_prepped = True
    if seq_aps.get("xin") is not None:
        P.wait("gpsimd", ctx.s_out)
        for c in range(8):
            P.dma("gpsimd", s_dma, X[:, ts(c, T)], seq_aps["xin"][ts(c, 128), :])
        P.wait("gpsimd", s_dma)
        for e in COMPUTE:
            P.wait(e, s_dma)
            P.wait(e, s_dmh)
    ctx.ln_stats(lambda c: X[:, ts(c, T)])
    for e in COMPUTE:
        P.wait(e, ctx.s_ve)
    ctx.ln_apply(lambda c: X[:, ts(c, T)], lambda c: XL[:, ts(c, T)])
    P.barrier2(COMPUTE, s_bar)

    for layer in range(2):
        P.wait("sync", s_sc)
        P.wait("sync", s_ve)
        P.dma("sync", s_dmh, ctx.BIAS[:, 0:70], w["bp_l0f"][layer])
        for e in COMPUTE:
            P.wait(e, s_dmh)
        for c in range(4):
            P.emit("vector", lambda E, c=c: E.memset(SH(c)[:, 0:1], 0.0))
            P.emit("vector", lambda E, c=c: E.tensor_copy(SH(c)[:, 1:T], XL[:, (4 + c) * T:(5 + c) * T - 1]))
        P.inc_last("vector", s_ve)
        P.barrier2(COMPUTE, s_bar)
        shift_src = lambda c: XL[:, ts(c, T)] if c < 4 else SH(c - 4)
        P.wait("gpsimd", s_ve)
        P.wait("gpsimd", s_mm)
        P.dma("gpsimd", s_dma, A2[0:64, 4 * T:6 * T], w["rotq"][:, :])
        P.dma("gpsimd", s_dma, A2[0:64, 6 * T:8 * T], w["rotk"][:, :])
        P.dma("gpsimd", s_dma, MSK[:], w["msk"][:, :])
        P.wait("gpsimd", s_dma)
        for e in COMPUTE:
            P.wait(e, s_dma)
        P.inc("vector", lambda E: E.memset(
            VT[:].rearrange("q (lc s) -> q lc s", s=65)[:, :, 64:65], 1.0), s_ve)
        P.wait("gpsimd", s_mm)
        stw.war = [None, None]
        ctx.wcache("f_qkv%d" % layer, [12, 128, 1024])
        ctx.wcache("f_wo2%d" % layer, [8, 128, 512])
        ctx.wcache("f_w1p%d" % layer, [44, 128, 1024])
        ctx.wcache("f_w2p%d" % layer, [8, 128, 2816])
        for p in range(4):
            slot = (p % 2) * 3072
            for wi in range(3):
                tok = ctx.cload(stw, p, W0[:, slot + wi * 1024: slot + (wi + 1) * 1024],
                                "f_qkv%d" % layer, wi * 4 + p, w["wqkv_t"][layer, wi * 4 + p],
                                view=W0[:, slot + wi * 1024: slot + (wi + 1) * 1024]
                                .rearrange("q (kc col) -> q kc col", col=128))
            stw.wait("tensor", tok)
            for wi in range(3):
                if not (p == 0 and wi == 0):
                    P.wait("tensor", s_sc)
                for g in range(NGRP):
                    for k in range(8):
                        P.emit("tensor", lambda E, wi=wi, g=g, k=k, slot=slot: E.matmul(
                            ctx.PSBIG[:, ts(g, NG)],
                            W0[:, slot + wi * 1024 + k * 128: slot + wi * 1024 + (k + 1) * 128],
                            shift_src(k)[:, ts(g, NG)], start=(k == 0), stop=(k == 7)))
                P.inc_last("tensor", s_mm)
                P.wait("scalar", s_mm)
                dst = (QPH, KPH, VH)[wi]
                P.emit("scalar", lambda E, dst=dst: E.copy(dst[:, 0:T], ctx.PSBIG[0:64, 0:T]))
                P.inc("scalar", lambda E, dst=dst: E.copy(dst[:, T:2 * T], ctx.PSBIG[64:128, 0:T]), s_sc)
            stw.set_war(p, "s_mm", P.cnt[s_mm])
            P.barrier2(COMPUTE, s_bar)
            if p == 0:
                pass
            for hh in range(2):
                h_global = p * 2 + hh
                emit_l0f_head(ctx, w, layer, hh, QPH, KPH, VH, EB, OFM, MSK, VT, h_global)
        P.barrier2(COMPUTE, s_bar)
        ctx.wcmark("f_qkv%d" % layer)
        P.wait("gpsimd", s_mm)
        stw.war = [None, None]
        for j in range(8):
            tok = ctx.cload(stw, j, W0[:, (j % 2) * 512: (j % 2) * 512 + 512],
                            "f_wo2%d" % layer, j, w["wo2_t"][layer, j],
                            view=W0[:, (j % 2) * 512: (j % 2) * 512 + 512].rearrange("q (kc col) -> q kc col", col=128))
            stw.wait("tensor", tok)
            if j >= 1:
                P.wait("tensor", s_ve)
            for g in range(NGRP):
                for k in range(4):
                    P.emit("tensor", lambda E, j=j, g=g, k=k: E.matmul(
                        ctx.PSBIG[:, ts(g, NG)], W0[:, (j % 2) * 512 + k * 128: (j % 2) * 512 + (k + 1) * 128],
                        OFM[:, k * T + g * NG: k * T + (g + 1) * NG], start=(k == 0), stop=(k == 3)))
            cnt = P.inc_last("tensor", s_mm)
            stw.set_war(j, "s_mm", cnt)
            P.wait("vector", s_mm)
            P.inc("vector", lambda E, j=j: E.scalar_tensor_tensor(
                out=XL[:, ts(j, T)], in0=ctx.PSBIG[:, 0:T], scalar=1.0, in1=XL[:, ts(j, T)],
                op0=OP.mult, op1=OP.add), s_ve)
        ctx.wcmark("f_wo2%d" % layer)
        P.barrier2(COMPUTE, s_bar)
        for c in range(4):
            P.emit("vector", lambda E, c=c: E.memset(SH(c)[:, 0:1], 0.0))
            P.emit("vector", lambda E, c=c: E.tensor_copy(SH(c)[:, 1:T], XL[:, (4 + c) * T:(5 + c) * T - 1]))
        P.inc_last("vector", s_ve)
        P.barrier2(COMPUTE, s_bar)
        ctx.ln_stats(shift_src)
        for e in COMPUTE:
            P.wait(e, ctx.s_ve)
        ctx.ln_apply_affine(shift_src, HM, lambda c: ctx.BIAS[:, c:c + 1],
                            lambda c: ctx.BIAS[:, 8 + c:9 + c])
        for c in range(8):
            P.emit("vector", lambda E, c=c: E.tensor_scalar_add(
                XL[:, ts(c, T)], XL[:, ts(c, T)], ctx.BIAS[:, 16 + c:17 + c]))
        P.inc_last("vector", s_ve)
        P.barrier2(COMPUTE, s_bar)
        for tt in range(4):
            P.wait("gpsimd", s_mm)
            stw4.reset()
            ctx.drain(stw)
            glu_done = {}
            for i in range(22):
                slot = (i % 4) * 2048
                pb = (i % 2) * 1024
                tok = ctx.cload(stw4, i, W0[:, slot: slot + 1024], "f_w1p%d" % layer, i,
                                w["w1p_t"][layer, i],
                                view=W0[:, slot: slot + 1024].rearrange("q (kc col) -> q kc col", col=128))
                tok = ctx.cload(stw4, i, W0[:, slot + 1024: slot + 2048], "f_w1p%d" % layer, 22 + i,
                                w["w1p_t"][layer, 22 + i],
                                view=W0[:, slot + 1024: slot + 2048].rearrange("q (kc col) -> q kc col", col=128))
                stw4.wait("tensor", tok)
                if i >= 2:
                    P.wait("tensor", s_ve, glu_done[i - 2])
                elif tt > 0 and i < 2:
                    P.wait("tensor", s_ve)
                for k in range(8):
                    P.emit("tensor", lambda E, i=i, k=k, tt=tt, slot=slot, pb=pb: E.matmul(
                        ctx.PSBIG[:, pb:pb + 512], W0[:, slot + k * 128: slot + (k + 1) * 128],
                        HM(k)[:, tt * 512:(tt + 1) * 512], start=(k == 0), stop=(k == 7)))
                for k in range(8):
                    P.emit("tensor", lambda E, i=i, k=k, tt=tt, slot=slot, pb=pb: E.matmul(
                        ctx.PSBIG[:, pb + 512:pb + 1024], W0[:, slot + 1024 + k * 128: slot + 1024 + (k + 1) * 128],
                        HM(k)[:, tt * 512:(tt + 1) * 512], start=(k == 0), stop=(k == 7)))
                cnt = P.inc_last("tensor", s_mm)
                stw4.set_war(i, "s_mm", cnt)
                P.wait("scalar", s_mm)
                P.inc("scalar", lambda E, i=i, pb=pb: E.activation(
                    out=ctx.SC1[:, (i % 2) * 512:(i % 2) * 512 + 512], in_=ctx.PSBIG[:, pb + 512:pb + 1024],
                    func=AF.Gelu, bias=ctx.BIAS[:, 46 + i:47 + i]), s_sc)
                P.wait("vector", s_sc)
                glu_done[i] = P.inc("vector", lambda E, i=i, pb=pb: E.scalar_tensor_tensor(
                    out=GL[:, ts(i, 512)], in0=ctx.PSBIG[:, pb:pb + 512], scalar=ctx.BIAS[:, 24 + i:25 + i],
                    in1=ctx.SC1[:, (i % 2) * 512:(i % 2) * 512 + 512], op0=OP.add, op1=OP.mult), s_ve)
            P.wait("tensor", s_ve)
            P.wait("gpsimd", s_mm)
            stw.war = [None, None]
            ctx.drain(stw4)
            for j in range(8):
                slot = (j % 2) * 2816
                tok = ctx.cload(stw, j, W0[:, slot: slot + 2816], "f_w2p%d" % layer, j,
                                w["w2p_t"][layer, j],
                                view=W0[:, slot: slot + 2816].rearrange("q (kc col) -> q kc col", col=128))
                stw.wait("tensor", tok)
                if j >= 2:
                    P.wait("tensor", s_ve)
                for k in range(22):
                    P.emit("tensor", lambda E, j=j, k=k, slot=slot: E.matmul(
                        ctx.PSA[:] if j % 2 == 0 else ctx.PSB[:],
                        W0[:, slot + k * 128: slot + (k + 1) * 128],
                        GL[:, ts(k, 512)], start=(k == 0), stop=(k == 21)))
                cnt = P.inc_last("tensor", s_mm)
                stw.set_war(j, "s_mm", cnt)
                P.wait("vector", s_mm)
                ps = ctx.PSA if j % 2 == 0 else ctx.PSB
                P.inc("vector", lambda E, j=j, tt=tt, ps=ps: E.scalar_tensor_tensor(
                    out=XL[:, j * T + tt * 512: j * T + (tt + 1) * 512], in0=ps[:], scalar=1.0,
                    in1=XL[:, j * T + tt * 512: j * T + (tt + 1) * 512], op0=OP.mult, op1=OP.add), s_ve)
            if tt == 0:
                ctx.wcmark("f_w1p%d" % layer)
                ctx.wcmark("f_w2p%d" % layer)
        P.barrier2(COMPUTE, s_bar)
    ctx.ln_stats(lambda c: XL[:, ts(c, T)])
    for e in COMPUTE:
        P.wait(e, ctx.s_ve)
    ctx.ln_residual_update(lambda c: XL[:, ts(c, T)])
    P.barrier2(COMPUTE, s_bar)
    if seq_aps.get("xout") is not None:
        P.wait("gpsimd", ctx.s_ve)
        for c in range(8):
            P.dma("gpsimd", ctx.s_out, seq_aps["xout"][ts(c, 128), :], X[:, ts(c, T)])


def emit_l0f_head(ctx, w, layer, hh, QPH, KPH, VH, EB, OFM, MSK, VT, h_global):
    P = ctx.P
    T_ = T
    A2 = ctx.A2
    s_mm, s_ve, s_sc, s_bar = ctx.s_mm, ctx.s_ve, ctx.s_sc, ctx.s_bar
    Aq = A2[0:64, 4 * T_:5 * T_]
    Bq = A2[0:64, 5 * T_:6 * T_]
    Ck = A2[0:64, 6 * T_:7 * T_]
    Dk = A2[0:64, 7 * T_:8 * T_]
    qs_col = ctx.BIAS[0:64, 68:69]
    ks_col = ctx.BIAS[0:64, 69:70]

    for side in range(2):
        src = (QPH, KPH)[side][0:64, hh * T_:(hh + 1) * T_]
        At, Bt = ((Aq, Bq), (Ck, Dk))[side]
        scol = (qs_col, ks_col)[side]
        P.wait("scalar", s_ve)
        P.inc("scalar", lambda E, src=src: E.activation(out=ctx.SC1[0:64, :], in_=src, func=AF.Square), s_sc)
        P.wait("tensor", s_sc)
        for g in range(NGRP):
            if g > 0 or side > 0:
                P.wait("tensor", s_ve)
            P.emit("tensor", lambda E, g=g: E.matmul(
                ctx.PSA[0:1, 0:NG], ctx.ones_f32[0:64, 0:1], ctx.SC1[0:64, ts(g, NG)],
                start=True, stop=True))
            P.inc_last("tensor", s_mm)
            P.wait("vector", s_mm)
            P.inc("vector", lambda E, g=g: E.tensor_copy(ctx.SC2[0:1, ts(g, NG)], ctx.PSA[0:1, 0:NG]), s_ve)
        P.wait("scalar", s_ve)
        P.emit("scalar", lambda E: E.activation(out=ctx.SC2[0:1, 0:T_], in_=ctx.SC2[0:1, 0:T_],
                                                func=AF.Ln, bias=ctx.CCOL[0:1, 3:4]))
        P.fence("scalar", s_sc)
        P.inc("scalar", lambda E: E.activation(out=ctx.SC2[0:1, 0:T_], in_=ctx.SC2[0:1, 0:T_],
                                               func=AF.Exp, scale=-0.5), s_sc)
        P.wait("tensor", s_sc)
        for g in range(NGRP):
            if True:
                P.wait("tensor", s_ve)
            P.emit("tensor", lambda E, g=g: E.matmul(
                ctx.PSB[0:64, 0:NG], ctx.zp1_f32[:, 0:64], ctx.SC2[:, ts(g, NG)],
                start=True, stop=True))
            P.inc_last("tensor", s_mm)
            P.wait("vector", s_mm)
            P.inc("vector", lambda E, g=g, src=src, scol=scol: E.scalar_tensor_tensor(
                out=src[:, ts(g, NG)], in0=src[:, ts(g, NG)], scalar=scol,
                in1=ctx.PSB[0:64, 0:NG], op0=OP.mult, op1=OP.mult), s_ve)
        P.wait("vector", s_ve)
        P.emit("vector", lambda E, src=src, At=At: E.tensor_mul(ctx.SC1[0:64, :], src, At))
        P.emit("vector", lambda E, src=src: E.tensor_copy(ctx.SC2[0:32, :], src[32:64, :]))
        P.emit("vector", lambda E, src=src: E.tensor_copy(ctx.SC2[32:64, :], src[0:32, :]))
        P.fence("vector", s_ve)
        P.emit("vector", lambda E, Bt=Bt: E.tensor_mul(ctx.SC2[0:64, :], ctx.SC2[0:64, :], Bt))
        P.fence("vector", s_ve)
        P.inc("vector", lambda E, src=src: E.tensor_add(src, ctx.SC1[0:64, :], ctx.SC2[0:64, :]), s_ve)
        P.wait("tensor", s_ve)
        P.wait("scalar", s_ve)
    for half in range(2):
        P.wait("tensor", s_ve)
        for j in range(8):
            lc = half * 8 + j
            P.emit("tensor", lambda E, lc=lc, j=j: E.transpose(
                ctx.PSTR[:, j * 64:(j + 1) * 64],
                VH[0:64, hh * T_ + lc * 128: hh * T_ + (lc + 1) * 128], ctx.ident_bf[0:64, 0:64]))
        P.inc_last("tensor", s_mm)
        P.wait("vector", s_mm)
        P.inc("vector", lambda E, half=half: E.tensor_copy(
            VT[:].rearrange("q (lc s) -> q lc s", s=65)[:, half * 8:(half + 1) * 8, 0:64],
            ctx.PSTR[:, 0:512].rearrange("q (lc s) -> q lc s", s=64)), s_ve)
    EBP = [EB, ctx.SC2[:].bitcast(BF16)]
    if not hasattr(ctx, "_l0f_evac_last"):
        ctx._l0f_evac_last = {}
    evl = {0: 0, 1: 0}
    exp_done = {}
    o_wait = {}

    def ebuild(wdw, ktcs):
        par = wdw % 2
        eb = EBP[par]
        mc0 = 2032 + par * 8
        for li, ktc in enumerate(ktcs):
            rel_slot = (ktc * 128 - wdw * 512 + 512) // 128
            psl = (ctx.PSA, ctx.PSB)[li % 2]
            if evl[li % 2]:
                P.wait("tensor", s_sc, evl[li % 2])
            P.emit("tensor", lambda E, ktc=ktc, wdw=wdw, psl=psl: E.matmul(
                psl[:], KPH[0:64, hh * T_ + ktc * 128: hh * T_ + (ktc + 1) * 128],
                QPH[0:64, hh * T_ + wdw * 512: hh * T_ + (wdw + 1) * 512], start=True, stop=False))
            P.emit("tensor", lambda E, rel_slot=rel_slot, psl=psl: E.matmul(
                psl[:], ctx.ident_bf[:], MSK[:, rel_slot * 512:(rel_slot + 1) * 512],
                start=False, stop=True))
            P.inc_last("tensor", s_mm)
            P.wait("scalar", s_mm)
            evl[li % 2] = P.inc("scalar", lambda E, li=li, psl=psl, eb=eb: E.copy(
                eb[:, ts(li, NG)], psl[:]), s_sc)
            P.wait("vector", s_sc)
            P.inc("vector", lambda E, li=li, eb=eb, mc0=mc0: E.tensor_reduce(
                ctx.SC1[:, mc0 + li:mc0 + li + 1], eb[:, ts(li, NG)], axis=AX.X, op=OP.max), s_ve)
        if len(ktcs) < 8:
            P.inc("vector", lambda E, n=len(ktcs), mc0=mc0: E.memset(
                ctx.SC1[:, mc0 + n:mc0 + 8], -1e30), s_ve)

    def maxpath_exps(wdw, ktcs):
        par = wdw % 2
        eb = EBP[par]
        mc0 = 2032 + par * 8
        mcell = 6 if par == 0 else 4
        P.fence("vector", s_ve)
        P.inc("vector", lambda E, par=par, mc0=mc0: E.tensor_reduce(
            ctx.SMS[:, par:par + 1], ctx.SC1[:, mc0:mc0 + 8], axis=AX.X, op=OP.max), s_ve)
        P.wait("tensor", s_ve)
        P.emit("tensor", lambda E, par=par: E.transpose(
            ctx.PSTR[0:1, par * 128:(par + 1) * 128], ctx.SMS[:, par:par + 1], ctx.ident_bf[:]))
        P.inc_last("tensor", s_mm)
        P.wait("vector", s_mm)
        P.inc("vector", lambda E, par=par: E.tensor_copy(
            ctx.SC1[0:1, 1760 + par * 128: 1888 + par * 128],
            ctx.PSTR[0:1, par * 128:(par + 1) * 128]), s_ve)
        P.fence("vector", s_ve)
        P.inc("vector", lambda E, par=par, mcell=mcell: E.tensor_reduce(
            ctx.CCOL[0:1, mcell:mcell + 1], ctx.SC1[0:1, 1760 + par * 128: 1888 + par * 128],
            axis=AX.X, op=OP.max), s_ve)
        P.wait("tensor", s_ve)
        P.emit("tensor", lambda E, par=par, mcell=mcell: E.matmul(
            ctx.PSM[:, par:par + 1],
            ctx.zp1_f32[:, 0:128], ctx.CCOL[:, mcell:mcell + 1], start=True, stop=True))
        P.inc_last("tensor", s_mm)
        P.wait("vector", s_mm)
        P.inc("vector", lambda E, par=par, mcell=mcell: E.tensor_scalar_mul(
            ctx.CCOL[:, mcell + 1:mcell + 2], ctx.PSM[:, par:par + 1], -1.0), s_ve)
        P.wait("scalar", s_ve)
        P.fence("scalar", s_sc)
        for li in range(len(ktcs)):
            P.inc("scalar", lambda E, li=li, eb=eb, mcell=mcell: E.activation(
                out=eb[:, ts(li, NG)], in_=eb[:, ts(li, NG)], func=AF.Exp,
                bias=ctx.CCOL[:, mcell + 1:mcell + 2]), s_sc)
        exp_done[wdw] = P.cnt[s_sc]

    def ommse(wdw, ktcs):
        par = wdw % 2
        eb = EBP[par]
        P.wait("tensor", s_sc, exp_done[wdw])
        for li, ktc in enumerate(ktcs):
            P.emit("tensor", lambda E, li=li, ktc=ktc, wdw=wdw, nl=len(ktcs), eb=eb: E.matmul(
                ctx.PSBIG[0:65, wdw * 512:(wdw + 1) * 512], VT[:, ktc * 65:(ktc + 1) * 65],
                eb[:, ts(li, NG)], start=(li == 0), stop=(li == nl - 1)))
        P.inc_last("tensor", s_mm)

    prev = None
    for wdw in range(4):
        ktcs = list(range(max(0, (wdw - 1) * 4), (wdw + 1) * 4))
        ebuild(wdw, ktcs)
        if prev is not None:
            ommse(*prev)
        maxpath_exps(wdw, ktcs)
        prev = (wdw, ktcs)
    ommse(*prev)
    P.wait("scalar", s_mm)
    P.emit("scalar", lambda E: E.activation(out=ctx.SC1[0:1, 0:T_], in_=ctx.PSBIG[64:65, 0:T_],
                                            func=AF.Ln))
    P.fence("scalar", s_sc)
    P.inc("scalar", lambda E: E.activation(out=ctx.SC1[0:1, 0:T_], in_=ctx.SC1[0:1, 0:T_],
                                           func=AF.Exp, scale=-1.0), s_sc)
    P.inc("scalar", lambda E: E.copy(ctx.SC2[0:64, 0:T_], ctx.PSBIG[0:64, 0:T_]), s_sc)
    P.wait("tensor", s_sc)
    P.wait("vector", s_sc)
    for g in range(NGRP):
        psn = (ctx.PSA, ctx.PSB)[g % 2]
        if g >= 2:
            P.wait("tensor", s_ve)
        P.emit("tensor", lambda E, g=g, psn=psn: E.matmul(
            psn[0:64, 0:NG], ctx.zp1_f32[:, 0:64], ctx.SC1[:, ts(g, NG)],
            start=True, stop=True))
        P.inc_last("tensor", s_mm)
        P.wait("vector", s_mm)
        dst = OFM[(h_global % 2) * 64:(h_global % 2) * 64 + 64,
                  (h_global // 2) * T_ + g * NG:(h_global // 2) * T_ + (g + 1) * NG]
        P.inc("vector", lambda E, g=g, dst=dst, psn=psn: E.tensor_mul(
            dst, ctx.SC2[0:64, ts(g, NG)], psn[0:64, 0:NG]), s_ve)
    P.barrier2(COMPUTE, s_bar)


def build_l0f_standalone(nseq):
    nc = bass.Bass("TRN2", target_bir_lowering=False)
    xin = nc.dram_tensor("xin", [nseq, 1024, T], F32, kind="ExternalInput").ap()
    xout = nc.dram_tensor("xout", [nseq, 1024, T], F32, kind="ExternalOutput").ap()
    consts = nc.dram_tensor("consts", [128, 832], F32, kind="ExternalInput").ap()
    w = {
        "wqkv_t": nc.dram_tensor("wqkv_t", [2, 12, 128, 8, 128], F32, kind="ExternalInput").ap(),
        "wo2_t": nc.dram_tensor("wo2_t", [2, 8, 128, 4, 128], F32, kind="ExternalInput").ap(),
        "w1p_t": nc.dram_tensor("w1p_t", [2, 44, 128, 8, 128], F32, kind="ExternalInput").ap(),
        "w2p_t": nc.dram_tensor("w2p_t", [2, 8, 128, 22, 128], F32, kind="ExternalInput").ap(),
        "rotq": nc.dram_tensor("rotq", [64, 4096], F32, kind="ExternalInput").ap(),
        "rotk": nc.dram_tensor("rotk", [64, 4096], F32, kind="ExternalInput").ap(),
        "msk": nc.dram_tensor("msk", [128, 4096], F32, kind="ExternalInput").ap(),
        "bp_l0f": nc.dram_tensor("bp_l0f", [2, 128, 70], F32, kind="ExternalInput").ap(),
    }
    stack = contextlib.ExitStack()
    with stack:
        P = Prog(nc, stack)
        ctx = Ctx(P)
        ctx.use_wcache = True
        ctx.load_consts(consts)
        for s in range(nseq):
            emit_l0_ffn(ctx, {"xin": xin[s], "xout": xout[s]}, w)
        P.wait("gpsimd", ctx.s_out)
        print("n queue items:", P.n_instr())
        P.finalize()
    return nc


def l0f_tables():
    DH, WINDOW = 64, 512
    n = T
    half = np.arange(0, DH, 2, dtype=np.float64)
    inv_freq = 1.0 / 10000.0 ** (half / DH)
    t = np.arange(n, dtype=np.float64)
    freqs = t[:, None] * inv_freq[None, :]
    freqs = np.concatenate([freqs, freqs], axis=1)
    base = (half + 0.4 * DH) / (1.4 * DH)
    power = (t - n // 2) / (WINDOW // 2)
    scale = base[None, :] ** power[:, None]
    scale = np.concatenate([scale, scale], axis=1)
    cos, sin = np.cos(freqs), np.sin(freqs)
    sgn = np.concatenate([-np.ones(32), np.ones(32)])[None, :]
    QK = 8.0
    Aq = (QK * cos * scale).T
    Bq = (QK * sin * scale * sgn).T
    Ck = (cos / scale).T
    Dk = (sin / scale * sgn).T
    rotq = np.concatenate([Aq, Bq], axis=1).astype(np.float32)
    rotk = np.concatenate([Ck, Dk], axis=1).astype(np.float32)
    msk = np.zeros((128, 8 * 512), np.float32)
    kj = np.arange(128)[:, None]
    qi = np.arange(512)[None, :]
    for r in range(8):
        rel = r * 128 - 512
        d = qi - kj - rel
        msk[:, r * 512:(r + 1) * 512] = np.where((d >= 0) & (d <= 512), 0.0, -1e9)
    return rotq, rotk, msk


def pack_l0f_weights(weights):
    HID = 2730
    w1p = np.zeros((2, 1024, 5632), np.float32)
    w2p = np.zeros((2, 2816, 1024), np.float32)
    b1p = np.zeros((2, 5632), np.float32)
    for l in range(2):
        w1p[l, :, 0:HID] = weights["w1"][l][:, 0:HID]
        w1p[l, :, 2816:2816 + HID] = weights["w1"][l][:, HID:]
        w2p[l, 0:HID] = weights["w2"][l]
        b1p[l, 0:HID] = weights["b1"][l][0:HID]
        b1p[l, 2816:2816 + HID] = weights["b1"][l][HID:]
    rotq, rotk, msk = l0f_tables()
    bp = np.zeros((2, 128, 70), np.float32)
    for l in range(2):
        bp[l, :, 0:8] = weights["ln_g"][l].reshape(8, 128).T
        bp[l, :, 8:16] = weights["ln_b"][l].reshape(8, 128).T
        bp[l, :, 16:24] = weights["b2"][l].reshape(8, 128).T
        bp[l, :, 24:68] = b1p[l].reshape(44, 128).T
        bp[l, 0:64, 68] = weights["q_scale"][l]
        bp[l, 0:64, 69] = weights["k_scale"][l]
    return {
        "wqkv_t": np.stack([tile_w(weights["wqkv"][l]) for l in range(2)]),
        "wo2_t": np.stack([tile_w(weights["wo"][l]) for l in range(2)]),
        "w1p_t": np.stack([tile_w(w1p[l]) for l in range(2)]),
        "w2p_t": np.stack([tile_w(w2p[l]) for l in range(2)]),
        "rotq": rotq, "rotk": rotk, "msk": msk, "bp_l0f": bp,
    }


def np_l0f_ref(x, w):
    from scipy.special import erf as _erf
    B, N, DIM = x.shape
    HEADS, DH, WINDOW = 8, 64, 512
    xd = x.astype(np.float64)
    wd = {k: np.asarray(v, np.float64) for k, v in w.items()}

    def _lnp(t, eps=1e-5):
        m = t.mean(-1, keepdims=True)
        v = t.var(-1, keepdims=True)
        return (t - m) / np.sqrt(v + eps)

    def shift_token(t):
        t1, t2 = t[..., :512], t[..., 512:]
        t2 = np.concatenate([np.zeros_like(t2[:, :1]), t2[:, :-1]], axis=1)
        return np.concatenate([t1, t2], axis=-1)

    def l2norm(t):
        return t / np.clip(np.linalg.norm(t, axis=-1, keepdims=True), 1e-12, None)

    def rotate_half(t):
        x1, x2 = t[..., :32], t[..., 32:]
        return np.concatenate([-x2, x1], axis=-1)

    def rotary_xpos(q, k):
        n = q.shape[-2]
        half = np.arange(0, DH, 2, dtype=np.float64)
        inv_freq = 1.0 / 10000.0 ** (half / DH)
        t = np.arange(n, dtype=np.float64)
        freqs = t[:, None] * inv_freq[None, :]
        freqs = np.concatenate([freqs, freqs], axis=-1)
        base = (half + 0.4 * DH) / (1.4 * DH)
        power = (t - n // 2) / (WINDOW // 2)
        scale = base[None, :] ** power[:, None]
        scale = np.concatenate([scale, scale], axis=-1)
        cos, sin = np.cos(freqs), np.sin(freqs)
        q = q * cos * scale + rotate_half(q) * sin * scale
        k = k * cos / scale + rotate_half(k) * sin / scale
        return q, k

    def local_attention(q, k, v):
        b, h, n, d = q.shape
        wn = n // WINDOW
        bq = q.reshape(b, h, wn, WINDOW, d) * 8.0
        bk = k.reshape(b, h, wn, WINDOW, d)
        bv = v.reshape(b, h, wn, WINDOW, d)

        def look_around(t):
            p = np.concatenate([np.zeros_like(t[:, :, :1]), t], axis=2)
            return np.concatenate([p[:, :, :wn], p[:, :, 1:]], axis=3)
        bk, bv = look_around(bk), look_around(bv)
        bt = np.arange(n).reshape(wn, WINDOW)
        btp = np.concatenate([np.full((1, WINDOW), -1), bt], axis=0)
        btk = np.concatenate([btp[:wn], btp[1:]], axis=1)
        sim = np.einsum('bhwid,bhwjd->bhwij', bq, bk)
        i = bt[:, :, None]
        j = btk[:, None, :]
        mask = (i < j) | (i > j + WINDOW) | (j == -1)
        sim = np.where(mask, -1e30, sim)
        sim = sim - sim.max(-1, keepdims=True)
        attn = np.exp(sim)
        attn = attn / attn.sum(-1, keepdims=True)
        out = np.einsum('bhwij,bhwjd->bhwid', attn, bv)
        return out.reshape(b, h, n, d)

    def layer(xx, l):
        b, n, _ = xx.shape
        h = shift_token(xx)
        qkv = h @ wd["wqkv"][l]
        q, k, v = qkv[..., 0:512], qkv[..., 512:1024], qkv[..., 1024:]
        to_heads = lambda t: t.reshape(b, n, HEADS, DH).transpose(0, 2, 1, 3)
        q, k, v = to_heads(q), to_heads(k), to_heads(v)
        q = l2norm(q) * wd["q_scale"][l]
        k = l2norm(k) * wd["k_scale"][l]
        q, k = rotary_xpos(q, k)
        o = local_attention(q, k, v)
        o = o.transpose(0, 2, 1, 3).reshape(b, n, 512)
        xx = xx + o @ wd["wo"][l]
        h = shift_token(xx)
        mu = h.mean(-1, keepdims=True)
        var = h.var(-1, keepdims=True)
        h = (h - mu) / np.sqrt(var + 1e-05) * wd["ln_g"][l] + wd["ln_b"][l]
        h = h @ wd["w1"][l] + wd["b1"][l]
        a, g = h[..., :2730], h[..., 2730:]
        h = a * (0.5 * g * (1 + _erf(g / np.sqrt(2))))
        xx = xx + h @ wd["w2"][l] + wd["b2"][l]
        return xx

    y = _lnp(xd)
    for l in range(2):
        y = layer(y, l)
    return (xd + _lnp(y)).astype(np.float32)



def kernel_multilaunch(**inputs):
    g = {k: np.ascontiguousarray(np.asarray(v, np.float32)) for k, v in inputs.items()}
    cores = list(range(8))
    x = g["L0_attn_x"]
    xfm = np.ascontiguousarray(x.reshape(8, 4, 2048, 1024).transpose(0, 1, 3, 2))
    consts = make_consts()

    w_l0a = l0a_in_maps(None, {
        "wq": g["L0_attn_wq"], "wk": g["L0_attn_wk"], "wv": g["L0_attn_wv"],
        "wo": g["L0_attn_wo"], "w1": g["L0_attn_w1"], "b1": g["L0_attn_b1"],
        "w2": g["L0_attn_w2"], "b2": g["L0_attn_b2"], "g1": g["L0_attn_g1"],
        "beta1": g["L0_attn_beta1"], "g2": g["L0_attn_g2"], "beta2": g["L0_attn_beta2"],
        "proj": g["L0_attn_proj"]})
    del w_l0a["xin"]
    w_l0f = pack_l0f_weights({
        "wqkv": g["L0_ffn_wqkv"], "q_scale": g["L0_ffn_q_scale"], "k_scale": g["L0_ffn_k_scale"],
        "wo": g["L0_ffn_wo"], "ln_g": g["L0_ffn_ln_g"], "ln_b": g["L0_ffn_ln_b"],
        "w1": g["L0_ffn_w1"], "b1": g["L0_ffn_b1"], "w2": g["L0_ffn_w2"], "b2": g["L0_ffn_b2"]})
    w_l1a = l1_in_maps(None, {
        "Wq": g["L1_attn_Wq"], "bq": g["L1_attn_bq"], "Wk": g["L1_attn_Wk"], "bk": g["L1_attn_bk"],
        "Wv": g["L1_attn_Wv"], "bv": g["L1_attn_bv"], "Wo": g["L1_attn_Wo"], "bo": g["L1_attn_bo"],
        "orf": g["L1_attn_orf"]})
    del w_l1a["xin"]
    del w_l1a["consts"]
    w_l1f = l1f_in_maps(None, {
        "W_up": g["L1_ffn_W_up"], "W_down": g["L1_ffn_W_down"], "W_q": g["L1_ffn_W_q"],
        "keys": g["L1_ffn_keys"], "down_embed": g["L1_ffn_down_embed"],
        "up_embed": g["L1_ffn_up_embed"]})
    del w_l1f["xin"]
    del w_l1f["consts"]

    shards = [xfm[i] for i in range(8)]
    for build, wmap in ((build_l0a_standalone, w_l0a), (build_l0f_standalone, w_l0f),
                        (build_l1_standalone, w_l1a), (build_l1f_standalone, w_l1f)):
        nc = build(4)
        in_maps = [dict(wmap, consts=consts, xin=shards[i]) for i in range(8)]
        res = run_bass_kernel_spmd(nc, in_maps, core_ids=cores)
        shards = [np.asarray(r["xout"]) for r in res.results]
    out = np.stack(shards)
    return np.ascontiguousarray(out.transpose(0, 1, 3, 2).reshape(32, 2048, 1024))



def build_fused(nseq=4):
    nc = bass.Bass("TRN2", target_bir_lowering=False)
    xin = nc.dram_tensor("xin", [nseq, 1024, T], F32, kind="ExternalInput").ap()
    xout = nc.dram_tensor("xout", [nseq, 1024, T], F32, kind="ExternalOutput").ap()
    consts = nc.dram_tensor("consts", [128, 832], F32, kind="ExternalInput").ap()

    def dt(name, shape):
        return nc.dram_tensor(name, shape, F32, kind="ExternalInput").ap()

    w_l0a = {
        "wq_t": dt("a_wq_t", [8, 128, 8, 128]), "wk_t": dt("a_wk_t", [8, 128, 8, 128]),
        "wv_t": dt("a_wv_t", [8, 128, 8, 128]), "wo_t": dt("a_wo_t", [8, 128, 8, 128]),
        "w1_t": dt("a_w1_t", [32, 128, 8, 128]), "w2_t": dt("a_w2_t", [8, 128, 32, 128]),
        "proj": dt("a_proj", [64, 64]), "biaspack_l0a": dt("a_bp", [128, 72]),
    }
    w_l0f = {
        "wqkv_t": dt("f_wqkv_t", [2, 12, 128, 8, 128]), "wo2_t": dt("f_wo2_t", [2, 8, 128, 4, 128]),
        "w1p_t": dt("f_w1p_t", [2, 44, 128, 8, 128]), "w2p_t": dt("f_w2p_t", [2, 8, 128, 22, 128]),
        "rotq": dt("f_rotq", [64, 4096]), "rotk": dt("f_rotk", [64, 4096]),
        "msk": dt("f_msk", [128, 4096]), "bp_l0f": dt("f_bp", [2, 128, 70]),
    }
    w_l1a = {
        "wq_t": dt("b_wq_t", [8, 128, 8, 128]), "wk_t": dt("b_wk_t", [8, 128, 8, 128]),
        "wv_t": dt("b_wv_t", [8, 128, 8, 128]), "wo_t": dt("b_wo_t", [8, 128, 8, 128]),
        "biaspack": dt("b_bp", [128, 32]), "orft": dt("b_orft", [64, 256]),
    }
    w_l1f = {
        "wup_t": dt("c_wup_t", [32, 128, 8, 128]), "wdn_t": dt("c_wdn_t", [8, 128, 32, 128]),
        "wq_t": dt("c_wq_t", [4, 128, 8, 128]), "keysp": dt("c_keysp", [2, 4, 64, 64]),
        "det": dt("c_det", [8, 128, 8, 512]), "ue": dt("c_ue", [4096, 1024]),
        "sd_scr": nc.dram_tensor("sd_scr", [16, 8, 128, 512], BF16, kind="Internal").ap(),
        "at_scr": nc.dram_tensor("at_scr", [32, 16, 128, 128], BF16, kind="Internal").ap(),
    }
    stack = contextlib.ExitStack()
    with stack:
        P = Prog(nc, stack)
        ctx = Ctx(P)
        ctx.use_wcache = True
        ctx.load_consts(consts)
        for s in range(nseq):
            emit_l0_attn(ctx, {"xin": xin[s], "xout": None}, w_l0a)
            emit_l0_ffn(ctx, {"xin": None, "xout": None}, w_l0f)
            emit_l1_attn(ctx, {"xin": None, "xout": None}, w_l1a)
            emit_l1_ffn(ctx, {"xin": None, "xout": xout[s]}, w_l1f)
        P.wait("gpsimd", ctx.s_out)
        P.finalize()
    return nc


def kernel_fused(**inputs):
    g = {k: np.ascontiguousarray(np.asarray(v, np.float32)) for k, v in inputs.items()}
    x = g["L0_attn_x"]
    xfm = np.ascontiguousarray(x.reshape(8, 4, 2048, 1024).transpose(0, 1, 3, 2))
    wmap = {"consts": make_consts()}
    m = l0a_in_maps(None, {
        "wq": g["L0_attn_wq"], "wk": g["L0_attn_wk"], "wv": g["L0_attn_wv"],
        "wo": g["L0_attn_wo"], "w1": g["L0_attn_w1"], "b1": g["L0_attn_b1"],
        "w2": g["L0_attn_w2"], "b2": g["L0_attn_b2"], "g1": g["L0_attn_g1"],
        "beta1": g["L0_attn_beta1"], "g2": g["L0_attn_g2"], "beta2": g["L0_attn_beta2"],
        "proj": g["L0_attn_proj"]})
    wmap.update({"a_wq_t": m["wq_t"], "a_wk_t": m["wk_t"], "a_wv_t": m["wv_t"],
                 "a_wo_t": m["wo_t"], "a_w1_t": m["w1_t"], "a_w2_t": m["w2_t"],
                 "a_proj": m["proj"], "a_bp": m["biaspack_l0a"]})
    m = pack_l0f_weights({
        "wqkv": g["L0_ffn_wqkv"], "q_scale": g["L0_ffn_q_scale"], "k_scale": g["L0_ffn_k_scale"],
        "wo": g["L0_ffn_wo"], "ln_g": g["L0_ffn_ln_g"], "ln_b": g["L0_ffn_ln_b"],
        "w1": g["L0_ffn_w1"], "b1": g["L0_ffn_b1"], "w2": g["L0_ffn_w2"], "b2": g["L0_ffn_b2"]})
    wmap.update({"f_wqkv_t": m["wqkv_t"], "f_wo2_t": m["wo2_t"], "f_w1p_t": m["w1p_t"],
                 "f_w2p_t": m["w2p_t"], "f_rotq": m["rotq"], "f_rotk": m["rotk"],
                 "f_msk": m["msk"], "f_bp": m["bp_l0f"]})
    m = l1_in_maps(None, {
        "Wq": g["L1_attn_Wq"], "bq": g["L1_attn_bq"], "Wk": g["L1_attn_Wk"], "bk": g["L1_attn_bk"],
        "Wv": g["L1_attn_Wv"], "bv": g["L1_attn_bv"], "Wo": g["L1_attn_Wo"], "bo": g["L1_attn_bo"],
        "orf": g["L1_attn_orf"]})
    wmap.update({"b_wq_t": m["wq_t"], "b_wk_t": m["wk_t"], "b_wv_t": m["wv_t"],
                 "b_wo_t": m["wo_t"], "b_bp": m["biaspack"], "b_orft": m["orft"]})
    m = l1f_in_maps(None, {
        "W_up": g["L1_ffn_W_up"], "W_down": g["L1_ffn_W_down"], "W_q": g["L1_ffn_W_q"],
        "keys": g["L1_ffn_keys"], "down_embed": g["L1_ffn_down_embed"],
        "up_embed": g["L1_ffn_up_embed"]})
    wmap.update({"c_wup_t": m["wup_t"], "c_wdn_t": m["wdn_t"], "c_wq_t": m["wq_t"],
                 "c_keysp": m["keysp"], "c_det": m["det"], "c_ue": m["ue"]})
    nc = build_fused(4)
    in_maps = [dict(wmap, xin=xfm[i]) for i in range(8)]
    res = run_bass_kernel_spmd(nc, in_maps, core_ids=list(range(8)))
    out = np.stack([np.asarray(r["xout"]) for r in res.results])
    return np.ascontiguousarray(out.transpose(0, 1, 3, 2).reshape(32, 2048, 1024))


def kernel(**inputs):
    return kernel_fused(**inputs)


# version 14 (final)
# speedup vs baseline: 1.1970x; 1.1970x over previous
import contextlib
import math
import numpy as np
import ml_dtypes
import concourse.bass as bass
import concourse.mybir as mybir
from concourse.bass_utils import run_bass_kernel_spmd

F32 = mybir.dt.float32
BF16 = mybir.dt.bfloat16
AF = mybir.ActivationFunctionType
OP = mybir.AluOpType
AX = mybir.AxisListType
BF = ml_dtypes.bfloat16

ENGINES = ("sync", "tensor", "vector", "scalar", "gpsimd")
T = 2048
NG = 512
NGRP = T // NG
LN16 = math.log(16.0)


def ts(i, n):
    return slice(i * n, (i + 1) * n)


class Prog:
    def __init__(self, nc, stack):
        self.nc = nc
        self.stack = stack
        self.qs = {e: [] for e in ENGINES}
        self.semnames = []
        self.cnt = {}

    def sem(self, name):
        assert name not in self.cnt
        self.semnames.append(name)
        self.cnt[name] = 0
        return name

    def sbuf(self, name, shape, dtype):
        return self.stack.enter_context(self.nc.sbuf_tensor(name, shape, dtype))

    def psum(self, name, shape, dtype=F32):
        return self.stack.enter_context(self.nc.psum_tensor(name, shape, dtype))

    def emit(self, eng, fn):
        self.qs[eng].append(("op", fn))

    def wait(self, eng, sem, val=None):
        v = self.cnt[sem] if val is None else val
        self.qs[eng].append(("wait", sem, v))

    def inc(self, eng, fn, sem, n=1):
        self.cnt[sem] += n
        self.qs[eng].append(("inc", fn, sem, n))
        return self.cnt[sem]

    def tick(self, eng, sem):
        return self.inc(eng, lambda E: E.nop(), sem)

    def fence(self, eng, sem):
        q = self.qs[eng]
        if q and q[-1][0] == "op":
            self.inc_last(eng, sem)
        self.wait(eng, sem)

    def inc_last(self, eng, sem, n=1):
        q = self.qs[eng]
        if q and q[-1][0] == "op":
            fn = q[-1][1]
            self.cnt[sem] += n
            q[-1] = ("inc", fn, sem, n)
            return self.cnt[sem]
        return self.inc(eng, lambda E: E.nop(), sem, n)

    def barrier2(self, engines, sem):
        for e in engines:
            q = self.qs[e]
            if q and q[-1][0] == "op":
                self.inc_last(e, sem)
            else:
                if q and q[-1][0] == "inc":
                    self.wait(e, q[-1][2], self.cnt[q[-1][2]])
                self.tick(e, sem)
        tot = self.cnt[sem]
        for e in engines:
            self.wait(e, sem, tot)

    def dma(self, eng, sem, out, in_):
        self.cnt[sem] += 16
        self.qs[eng].append(("dma", out, in_, sem))
        return self.cnt[sem]

    def barrier(self, engines, sem):
        for e in engines:
            self.tick(e, sem)
        tot = self.cnt[sem]
        for e in engines:
            self.wait(e, sem, tot)

    def n_instr(self):
        return sum(len(q) for q in self.qs.values())

    def finalize(self):
        semh = {n: self.stack.enter_context(self.nc.semaphore(n)) for n in self.semnames}
        with self.nc.Block() as block:
            for e in ENGINES:
                if not self.qs[e]:
                    continue
                items = self.qs[e]

                def body(E, items=items):
                    for item in items:
                        kind = item[0]
                        if kind == "wait":
                            E.wait_ge(semh[item[1]], item[2])
                        elif kind == "op":
                            item[1](E)
                        elif kind == "inc":
                            item[1](E).then_inc(semh[item[2]], item[3])
                        elif kind == "dma":
                            E.dma_start(out=item[1], in_=item[2]).then_inc(semh[item[3]], 16)

                getattr(block, e)(body)


COMPUTE = ("tensor", "vector", "scalar")


class Ctx:

    def __init__(self, P):
        self.P = P
        self.wcf = set()
        self.wct = {}
        self.use_wcache = False
        self.s_dma = P.sem("s_dma")
        self.s_dmh = P.sem("s_dmh")
        self.s_w0 = P.sem("s_w0")
        self.s_w1 = P.sem("s_w1")
        self.s_mm = P.sem("s_mm")
        self.s_ve = P.sem("s_ve")
        self.s_sc = P.sem("s_sc")
        self.s_gp = P.sem("s_gp")
        self.s_out = P.sem("s_out")
        self.s_bar = P.sem("s_bar")
        self.ones_bf = P.sbuf("ones_bf", [128, 128], BF16)
        self.ones_f32 = P.sbuf("ones_f32", [128, 128], F32)
        self.ident_bf = P.sbuf("ident_bf", [128, 128], BF16)
        self.zp1_f32 = P.sbuf("zp1_f32", [128, 128], F32)
        self.negk_f32 = P.sbuf("negk_f32", [64, 256], F32)
        self.CCOL = P.sbuf("CCOL", [128, 8], F32)
        self.LTRI = P.sbuf("LTRI", [128, 128], BF16)
        self.PROJB = P.sbuf("PROJB", [64, 64], BF16)
        self.XB = P.sbuf("XB", [128, 8 * T], BF16)
        self.A1 = P.sbuf("A1", [128, 8 * T], BF16)
        self.A2 = P.sbuf("A2", [128, 8 * T], BF16)
        self.RB = P.sbuf("RB", [128, T], F32)
        self.RBbf = P.sbuf("RBbf", [128, T], BF16)
        self.MRBbf = P.sbuf("MRBbf", [128, T], BF16)
        self.SC1 = P.sbuf("SC1", [128, T], F32)
        self.SC2 = P.sbuf("SC2", [128, T], F32)
        self.W0 = P.sbuf("W0", [128, 8 * 1024], BF16)
        self.TR0 = P.sbuf("TR0", [128, 8 * T], BF16)
        self.TR1 = P.sbuf("TR1", [128, 6 * T], BF16)
        self.BIAS = P.sbuf("BIAS", [128, 128], F32)
        self.orfTs = P.sbuf("orfTs", [64, 256], BF16)
        self.KEYS = P.sbuf("KEYS", [64, 512], BF16)
        self.SMS = P.sbuf("SMS", [128, 128], BF16)
        self.KVS = P.sbuf("KVS", [64, 66], BF16)
        self.KVF = P.sbuf("KVF", [64, 66], F32)
        self.VT2 = P.sbuf("VT2", [128, 1040], BF16)
        self.PSBIG = P.psum("PSBIG", [128, 4 * NG], F32)
        self.PSA = P.psum("PSA", [128, NG], F32)
        self.PSB = P.psum("PSB", [128, NG], F32)
        self.PSTR = P.psum("PSTR", [128, NG], BF16)
        self.PSM = P.psum("PSM", [128, 128], F32)

    def dbg_dump(self, slot, ap, rows=128):
        if getattr(self, "dbg", None) is None:
            return
        P = self.P
        P.barrier2(COMPUTE, self.s_bar)
        for e in COMPUTE:
            P.inc_last(e, self.s_bar) if False else None
        P.wait("gpsimd", self.s_bar)
        P.dma("gpsimd", self.s_out, self.dbg[slot][0:rows, 0:ap.shape[-1]], ap)
        P.wait("gpsimd", self.s_out)
        P.tick("gpsimd", self.s_gp)
        for e in COMPUTE:
            P.wait(e, self.s_gp)

    def wcache(self, name, shape):
        if not self.use_wcache:
            return None
        if name not in self.wct:
            self.wct[name] = self.P.nc.dram_tensor("wc_" + name, shape, BF16,
                                                   kind="Internal").ap()
        return self.wct[name]

    def wcmark(self, name):
        if name in self.wct:
            self.wcf.add(name)

    def drain(self, st):
        if st.pending is not None:
            st._flush()
        for i, t in enumerate(st.dwar):
            if t is not None:
                self.P.wait(st.queue, *t)
                if st.queue != "sync":
                    self.P.wait("sync", *t)
                st.dwar[i] = None

    def cload(self, st, idx, slot_ap, name, entry, f32_ap, view=None):
        wc = self.wct.get(name)
        if wc is None:
            return st.load(idx, slot_ap if view is None else view, f32_ap)
        if name in self.wcf:
            return st.load(idx, slot_ap, wc[entry])
        return st.load(idx, slot_ap if view is None else view, f32_ap,
                       spill=(wc[entry], slot_ap))

    def load_consts(self, consts_ap):
        P = self.P
        P.dma("gpsimd", self.s_dma, self.ones_bf[:], consts_ap[:, 0:128])
        P.dma("sync", self.s_dmh, self.ones_f32[:], consts_ap[:, 0:128])
        P.dma("gpsimd", self.s_dma, self.ident_bf[:], consts_ap[:, 128:256])
        P.wait("gpsimd", self.s_dma)
        P.dma("sync", self.s_dmh, self.zp1_f32[:], consts_ap[:, 256:384])
        P.dma("sync", self.s_dmh, self.negk_f32[:], consts_ap[0:64, 384:640])
        P.dma("sync", self.s_dmh, self.CCOL[:], consts_ap[:, 640:648])
        P.dma("gpsimd", self.s_dma, self.LTRI[:], consts_ap[:, 704:832])

    def ln_stats(self, get_chunk, d_total=1024, eps=1e-5, n_chunks=8):
        P = self.P
        for g in range(NGRP):
            for c in range(n_chunks):
                P.emit("tensor", lambda E, c=c, g=g, a=get_chunk(c): E.matmul(
                    self.PSBIG[:, ts(g, NG)], self.ones_bf[:], a[:, ts(g, NG)],
                    start=(c == 0), stop=(c == n_chunks - 1)))
        P.inc_last("tensor", self.s_mm)
        P.wait("vector", self.s_mm)
        P.inc("vector", lambda E: E.tensor_scalar_mul(self.SC2[:], self.PSBIG[:, 0:T], 1.0 / d_total),
              self.s_ve)
        P.wait("scalar", self.s_ve)
        mm_done = []
        for hc in range(2 * n_chunks):
            c, h = divmod(hc, 2)
            if hc >= 2:
                P.wait("scalar", self.s_mm, mm_done[hc - 2])
            SC1bf = self.SC1[:].bitcast(BF16)
            P.inc("scalar", lambda E, c=c, h=h, a=get_chunk(c), SC1bf=SC1bf: E.activation(
                out=SC1bf[:, h * 1024:(h + 1) * 1024], in_=a[:, h * 1024:(h + 1) * 1024],
                func=AF.Square), self.s_sc)
            P.wait("tensor", self.s_sc)
            for g in range(2):
                gg = h * 2 + g
                P.emit("tensor", lambda E, c=c, gg=gg, SC1bf=SC1bf: E.matmul(
                    self.PSBIG[:, ts(gg, NG)], self.ones_bf[:], SC1bf[:, ts(gg, NG)],
                    start=(c == 0), stop=(c == n_chunks - 1)))
            mm_done.append(P.inc_last("tensor", self.s_mm))
        P.wait("vector", self.s_mm)
        P.emit("vector", lambda E: E.tensor_scalar_mul(self.RB[:], self.PSBIG[:, 0:T], 1.0 / d_total))
        P.emit("vector", lambda E: E.tensor_mul(self.SC1[:], self.SC2[:], self.SC2[:]))
        P.fence("vector", self.s_ve)
        P.inc("vector", lambda E: E.tensor_sub(self.RB[:], self.RB[:], self.SC1[:]), self.s_ve)
        P.wait("scalar", self.s_ve)
        P.emit("scalar", lambda E: E.activation(out=self.SC1[:], in_=self.RB[:], func=AF.Ln, bias=self.CCOL[:, 0:1]))
        P.fence("scalar", self.s_sc)
        P.inc("scalar", lambda E: E.activation(out=self.RB[:], in_=self.SC1[:], func=AF.Exp, scale=-0.5),
              self.s_sc)
        P.wait("vector", self.s_sc)
        P.emit("vector", lambda E: E.tensor_copy(self.RBbf[:], self.RB[:]))
        P.inc("vector", lambda E: E.scalar_tensor_tensor(
            out=self.MRBbf[:], in0=self.SC2[:], scalar=1.0, in1=self.RB[:],
            op0=OP.mult, op1=OP.mult), self.s_ve)
        P.fence("vector", self.s_ve)

    def ln_apply(self, get_src, get_dst, n_chunks=8):
        P = self.P
        for c in range(n_chunks):
            P.emit("vector", lambda E, c=c: E.tensor_mul(get_dst(c), get_src(c), self.RBbf[:]))
            P.fence("vector", self.s_ve)
            P.emit("vector", lambda E, c=c: E.tensor_sub(get_dst(c), get_dst(c), self.MRBbf[:]))
        P.inc_last("vector", self.s_ve)

    def ln_apply_affine(self, get_src, get_dst, g_col, b_col, n_chunks=8):
        P = self.P
        for c in range(n_chunks):
            P.emit("vector", lambda E, c=c: E.tensor_mul(get_dst(c), get_src(c), self.RBbf[:]))
            P.fence("vector", self.s_ve)
            P.emit("vector", lambda E, c=c: E.tensor_sub(get_dst(c), get_dst(c), self.MRBbf[:]))
            P.fence("vector", self.s_ve)
            P.emit("vector", lambda E, c=c: E.tensor_scalar(
                out=get_dst(c), in0=get_dst(c), scalar1=g_col(c), scalar2=b_col(c),
                op0=OP.mult, op1=OP.add))
        P.inc_last("vector", self.s_ve)

    def ln_residual_update(self, get_z, n_chunks=8):
        P = self.P
        for c in range(n_chunks):
            xc = self.XB[:, ts(c, T)]
            P.emit("vector", lambda E, c=c, xc=xc: E.tensor_sub(xc, xc, self.MRBbf[:]))
            P.emit("vector", lambda E, c=c: E.tensor_mul(self.SC1[:], get_z(c), self.RBbf[:]))
            P.fence("vector", self.s_ve)
            P.emit("vector", lambda E, c=c, xc=xc: E.tensor_add(xc, xc, self.SC1[:]))
            P.fence("vector", self.s_ve)
        P.inc_last("vector", self.s_ve)

    def fc(self, w_slices, rhs_chunks, evac, nk=8, bias_col=None,
           evac_eng="scalar", extra_sem_wait=None):
        P = self.P
        sem_e = {"scalar": self.s_sc, "vector": self.s_ve}[evac_eng]
        for j in range(len(w_slices)):
            if j >= 1:
                P.wait("tensor", sem_e)
            for g in range(NGRP):
                for k in range(nk):
                    P.emit("tensor", lambda E, j=j, g=g, k=k: E.matmul(
                        self.PSBIG[:, ts(g, NG)], w_slices[j](k), rhs_chunks(k)[:, ts(g, NG)],
                        start=(k == 0), stop=(k == nk - 1)))
            P.inc_last("tensor", self.s_mm)
            P.wait(evac_eng, self.s_mm)
            P.inc(evac_eng, lambda E, j=j: evac(E, j), sem_e)



def emit_l1_attn(ctx, seq_aps, w):
    P = ctx.P
    X, Y, ATT = ctx.XB, ctx.A1, ctx.A2
    TR0, TR1, W0 = ctx.TR0, ctx.TR1, ctx.W0
    PHQ = lambda mh: TR0[:, mh * T: (mh + 1) * T]
    PKT_half = lambda hf: TR0[:, 2 * T + hf * T: 2 * T + (hf + 1) * T]
    VT = TR0[:, 5 * T: 5 * T + 16 * 130]
    Q2 = TR1[:, 0:T]
    K2 = TR1[:, T:2 * T]
    V2 = TR1[:, 2 * T:3 * T]
    KV = TR1[:, 3 * T: 3 * T + 130]
    QSTG = TR1[0:64, 3 * T + 256: 4 * T + 256]
    KSTG = TR1[0:64, 4 * T + 256: 5 * T + 256]
    s_dma, s_mm, s_ve, s_sc, s_bar = ctx.s_dma, ctx.s_mm, ctx.s_ve, ctx.s_sc, ctx.s_bar
    if not getattr(ctx, "_l1_prepped", False):
        ctx._l1_prepped = True
        P.wait("sync", ctx.s_ve)
        P.dma("sync", ctx.s_dmh, ctx.SC1[0:64, 0:256], w["orft"][:, :])
        P.wait("vector", ctx.s_dmh)
        P.inc("vector", lambda E: E.tensor_scalar_mul(ctx.orfTs[:], ctx.SC1[0:64, 0:256], 0.35355339059327373), ctx.s_ve)
    P.wait("sync", ctx.s_sc)
    P.wait("sync", ctx.s_ve)
    P.dma("sync", ctx.s_dmh, ctx.BIAS[:, 0:32], w["biaspack"][:, :])
    for e in COMPUTE:
        P.wait(e, ctx.s_ve)
        P.wait(e, ctx.s_dmh)
        P.wait(e, s_dma)

    if seq_aps.get("xin") is not None:
        P.wait("gpsimd", ctx.s_out)
        for c in range(8):
            P.dma("gpsimd", s_dma, X[:, ts(c, T)], seq_aps["xin"][ts(c, 128), :])
        P.wait("gpsimd", s_dma)
        P.wait("tensor", s_dma)
        P.wait("vector", s_dma)
        P.wait("scalar", s_dma)

    ctx.ln_stats(lambda c: X[:, ts(c, T)])
    for e in COMPUTE:
        P.wait(e, s_ve)
    ctx.ln_apply(lambda c: X[:, ts(c, T)], lambda c: Y[:, ts(c, T)])
    P.barrier2(COMPUTE, s_bar)
    ctx.dbg_dump(0, X[:, ts(0, T)])
    ctx.dbg_dump(1, Y[:, ts(0, T)])
    ctx.dbg_dump(15, ctx.RB[:, :])

    P.wait("gpsimd", s_mm)
    pair_mm_done = []
    for p in range(8):
        slot = (p % 2) * 3072
        s_w = (ctx.s_w0, ctx.s_w1)[p % 2]
        if p >= 2:
            P.wait("gpsimd", s_mm, pair_mm_done[p - 2])
        for wi, wn in enumerate(("wq_t", "wk_t", "wv_t")):
            P.dma("gpsimd", s_w, W0[:, slot + wi * 1024: slot + (wi + 1) * 1024]
                  .rearrange("q (kc col) -> q kc col", col=128),
                  w[wn][p])
        P.wait("tensor", s_w)
        wsl = lambda wi, k, slot=slot: W0[:, slot + wi * 1024 + k * 128: slot + wi * 1024 + (k + 1) * 128]
        for wi in range(3):
            if not (p == 0 and wi == 0):
                P.wait("tensor", s_sc)
            for g in range(NGRP):
                for k in range(8):
                    P.emit("tensor", lambda E, wi=wi, g=g, k=k, wsl=wsl: E.matmul(
                        ctx.PSBIG[:, ts(g, NG)], wsl(wi, k), Y[:, k * T + ts(g, NG).start: k * T + ts(g, NG).stop],
                        start=(k == 0), stop=(k == 7)))
            P.inc_last("tensor", s_mm)
            P.wait("scalar", s_mm)
            bcol = wi * 8 + p
            dst = (Q2, K2, V2)[wi]
            P.inc("scalar", lambda E, dst=dst, bcol=bcol: E.add(
                dst[:], ctx.PSBIG[:, 0:T], ctx.BIAS[:, bcol:bcol + 1]), s_sc)
        P.barrier2(COMPUTE, s_bar)
        if p == 0:
            ctx.dbg_dump(2, Q2[:])
            ctx.dbg_dump(3, K2[:])
            ctx.dbg_dump(4, V2[:])
        P.emit("vector", lambda E: E.memset(
            VT.rearrange("q (lc s) -> q lc s", s=130)[:, :, 64:65], 1.0))
        P.inc("vector", lambda E: E.memset(
            VT.rearrange("q (lc s) -> q lc s", s=130)[:, :, 129:130], 1.0), s_ve)
        for lb in range(4):
            if lb > 0:
                P.wait("tensor", s_ve)
            for j in range(4):
                lc = lb * 4 + j
                P.emit("tensor", lambda E, lc=lc, j=j: E.transpose(
                    ctx.PSTR[:, ts(j, 128)], V2[:, ts(lc, 128)], ctx.ident_bf[:]))
            P.inc_last("tensor", s_mm)
            P.wait("vector", s_mm)
            for j in range(4):
                lc = lb * 4 + j
                P.inc("vector", lambda E, lc=lc, j=j: E.tensor_copy(
                    VT[:, lc * 130: lc * 130 + 130].rearrange("q (two s) -> q two s", s=65)[:, :, 0:64],
                    ctx.PSTR[:, ts(j, 128)].rearrange("q (two s) -> q two s", s=64)), s_ve)
        P.barrier2(COMPUTE, s_bar)

        for hh in range(2):
            if hh == 0:
                qsrc, ksrc = Q2[0:64, :], K2[0:64, :]
            else:
                P.emit("vector", lambda E: E.tensor_copy(QSTG, Q2[64:128, :]))
                P.inc("vector", lambda E: E.tensor_copy(KSTG, K2[64:128, :]), s_ve)
                P.wait("scalar", s_ve)
                P.wait("tensor", s_ve)
                qsrc, ksrc = QSTG, KSTG
            P.inc("scalar", lambda E, qsrc=qsrc: E.activation(out=ctx.SC1[0:64, :], in_=qsrc, func=AF.Square), s_sc)
            for mh in range(2):
                P.wait("tensor", s_sc)
                for g in range(NGRP):
                    P.emit("tensor", lambda E, mh=mh, g=g, qsrc=qsrc: E.matmul(
                        ctx.PSBIG[:, ts(g, NG)], ctx.orfTs[:, ts(mh, 128)], qsrc[:, ts(g, NG)],
                        start=True, stop=False))
                    P.emit("tensor", lambda E, g=g: E.matmul(
                        ctx.PSBIG[:, ts(g, NG)], ctx.negk_f32[:, 0:128], ctx.SC1[0:64, ts(g, NG)],
                        start=False, stop=True))
                P.inc_last("tensor", s_mm)
                P.wait("scalar", s_mm)
                P.inc("scalar", lambda E, mh=mh: E.activation(
                    out=PHQ(mh), in_=ctx.PSBIG[:, 0:T], func=AF.Exp, bias=ctx.CCOL[:, 1:2]), s_sc)
                P.wait("tensor", s_sc)
            P.inc("scalar", lambda E, ksrc=ksrc: E.activation(out=ctx.SC2[0:64, :], in_=ksrc, func=AF.Square), s_sc)
            for hf in range(2):
                P.wait("tensor", s_sc)
                for j in range(8):
                    lc = hf * 8 + j
                    P.emit("tensor", lambda E, lc=lc, j=j, ksrc=ksrc: E.matmul(
                        ctx.PSBIG[:, j * 256:(j + 1) * 256], ksrc[:, ts(lc, 128)], ctx.orfTs[:],
                        start=True, stop=False))
                    P.emit("tensor", lambda E, lc=lc, j=j: E.matmul(
                        ctx.PSBIG[:, j * 256:(j + 1) * 256], ctx.SC2[0:64, ts(lc, 128)], ctx.negk_f32[:],
                        start=False, stop=True))
                P.inc_last("tensor", s_mm)
                P.wait("scalar", s_mm)
                P.inc("scalar", lambda E, hf=hf: E.activation(
                    out=PKT_half(hf), in_=ctx.PSBIG[:, 0:T], func=AF.Exp, bias=ctx.CCOL[:, 1:2]), s_sc)
                P.wait("tensor", s_sc)
            for hm in range(2):
                for lc in range(16):
                    hf, j = divmod(lc, 8)
                    P.emit("tensor", lambda E, hm=hm, hf=hf, j=j, lc=lc, hh=hh: E.matmul(
                        ctx.PSA[:, hm * 65:(hm + 1) * 65],
                        PKT_half(hf)[:, j * 256 + hm * 128: j * 256 + hm * 128 + 128],
                        VT[:, lc * 130 + hh * 65: lc * 130 + hh * 65 + 65],
                        start=(lc == 0), stop=(lc == 15)))
            P.inc_last("tensor", s_mm)
            P.wait("vector", s_mm)
            P.inc("vector", lambda E: E.tensor_copy(KV[:], ctx.PSA[:, 0:130]), s_ve)
            P.wait("tensor", s_ve)
            for hm in range(2):
                for g in range(NGRP):
                    P.emit("tensor", lambda E, hm=hm, g=g: E.matmul(
                        ctx.PSBIG[0:65, ts(g, NG)], KV[:, hm * 65:(hm + 1) * 65],
                        PHQ(hm)[:, ts(g, NG)], start=(hm == 0), stop=(hm == 1)))
            P.inc_last("tensor", s_mm)
            P.wait("vector", s_mm)
            P.wait("scalar", s_mm)
            P.emit("scalar", lambda E: E.activation(out=ctx.SC1[0:1, 0:T], in_=ctx.PSBIG[64:65, 0:T],
                                                    func=AF.Ln))
            P.fence("scalar", s_sc)
            P.inc("scalar", lambda E: E.activation(out=ctx.SC1[0:1, 0:T], in_=ctx.SC1[0:1, 0:T],
                                                   func=AF.Exp, scale=-1.0), s_sc)
            P.inc("scalar", lambda E: E.copy(ctx.SC2[0:64, :], ctx.PSBIG[0:64, 0:T]), s_sc)
            P.wait("tensor", s_sc)
            P.wait("vector", s_sc)
            for g in range(NGRP):
                psn = (ctx.PSA, ctx.PSB)[g % 2]
                if g >= 2:
                    P.wait("tensor", s_ve)
                P.emit("tensor", lambda E, g=g, psn=psn: E.matmul(
                    psn[0:64, 0:NG], ctx.zp1_f32[:, 0:64], ctx.SC1[:, ts(g, NG)],
                    start=True, stop=True))
                P.inc_last("tensor", s_mm)
                P.wait("vector", s_mm)
                dst = ATT[hh * 64: hh * 64 + 64, p * T + g * NG: p * T + (g + 1) * NG]
                P.inc("vector", lambda E, g=g, dst=dst, psn=psn: E.tensor_mul(
                    dst, ctx.SC2[0:64, ts(g, NG)], psn[0:64, 0:NG]), s_ve)
            if p == 0 and hh == 0:
                ctx.dbg_dump(5, PHQ(0))
                ctx.dbg_dump(6, PKT_half(0))
                ctx.dbg_dump(7, KV[:, 0:130])
                ctx.dbg_dump(8, ctx.SC2[0:64, :], rows=64)
                ctx.dbg_dump(9, ctx.SC1[0:1, :], rows=1)
                ctx.dbg_dump(10, ATT[0:64, 0:T], rows=64)
                ctx.dbg_dump(14, VT[:, 0:2048])
                ctx.dbg_dump(12, PKT_half(1))
                ctx.dbg_dump(13, VT[:, 2048:2080])
            P.barrier2(COMPUTE, s_bar)
        pair_mm_done.append(P.inc_last("tensor", s_mm))
    wo_done_counts = []
    for j in range(8):
        slot = (j % 2) * 1024
        s_w = (ctx.s_w0, ctx.s_w1)[j % 2]
        if j < 2:
            P.wait("gpsimd", s_mm, pair_mm_done[6 + j])
        else:
            P.wait("gpsimd", s_mm, wo_done_counts[j - 2])
        P.dma("gpsimd", s_w, W0[:, slot: slot + 1024].rearrange("q (kc col) -> q kc col", col=128),
              w["wo_t"][j])
        P.wait("tensor", s_w)
        if j > 0:
            P.wait("tensor", s_sc)
        for g in range(NGRP):
            for k in range(8):
                P.emit("tensor", lambda E, j=j, g=g, k=k, slot=slot: E.matmul(
                    ctx.PSBIG[:, ts(g, NG)], W0[:, slot + k * 128: slot + (k + 1) * 128],
                    ATT[:, k * T + g * NG: k * T + (g + 1) * NG], start=(k == 0), stop=(k == 7)))
        wo_done_counts.append(P.inc_last("tensor", s_mm))
        P.wait("scalar", s_mm)
        P.inc("scalar", lambda E, j=j: E.add(
            ctx.A1[:, ts(j, T)], ctx.PSBIG[:, 0:T], ctx.BIAS[:, 24 + j: 25 + j]), s_sc)
    P.barrier2(COMPUTE, s_bar)
    ctx.dbg_dump(11, ctx.A1[:, ts(0, T)])
    ctx.ln_stats(lambda c: ctx.A1[:, ts(c, T)])
    for e in COMPUTE:
        P.wait(e, s_ve)
    ctx.ln_residual_update(lambda c: ctx.A1[:, ts(c, T)])
    P.barrier2(COMPUTE, s_bar)
    if seq_aps.get("xout") is not None:
        P.wait("gpsimd", s_ve)
        for c in range(8):
            P.dma("gpsimd", ctx.s_out, seq_aps["xout"][ts(c, 128), :], X[:, ts(c, T)])



def tile_w(Wt):
    K, M = Wt.shape
    assert K % 128 == 0 and M % 128 == 0
    return np.ascontiguousarray(
        Wt.reshape(K // 128, 128, M // 128, 128).transpose(2, 1, 0, 3))


def make_consts():
    c = np.zeros((128, 832), np.float32)
    c[:, 0:128] = 1.0
    c[:, 128:256] = np.eye(128, dtype=np.float32)
    c[0, 256:384] = 1.0
    c[0:64, 384:640] = -1.0 / 16.0
    c[:, 640] = 1e-5
    c[:, 641] = -LN16
    c[:, 642] = 1e-6
    c[:, 704:832] = np.tril(np.ones((128, 128), np.float32)).T
    return c


def pack_bias_cols(*biases):
    return np.ascontiguousarray(
        np.concatenate([np.asarray(b, np.float32).reshape(-1, 128).T for b in biases], axis=1))



def build_l1_standalone(nseq):
    nc = bass.Bass("TRN2", target_bir_lowering=False)
    xin = nc.dram_tensor("xin", [nseq, 1024, T], F32, kind="ExternalInput").ap()
    xout = nc.dram_tensor("xout", [nseq, 1024, T], F32, kind="ExternalOutput").ap()
    consts = nc.dram_tensor("consts", [128, 832], F32, kind="ExternalInput").ap()
    w = {}
    for n in ("wq_t", "wk_t", "wv_t", "wo_t"):
        w[n] = nc.dram_tensor(n, [8, 128, 8, 128], F32, kind="ExternalInput").ap()
    w["biaspack"] = nc.dram_tensor("biaspack", [128, 32], F32, kind="ExternalInput").ap()
    w["orft"] = nc.dram_tensor("orft", [64, 256], F32, kind="ExternalInput").ap()
    dbg = nc.dram_tensor("dbg", [16, 128, T], F32, kind="ExternalOutput").ap()
    stack = contextlib.ExitStack()
    with stack:
        P = Prog(nc, stack)
        ctx = Ctx(P)
        ctx.dbg = dbg
        ctx.load_consts(consts)
        for s in range(nseq):
            emit_l1_attn(ctx, {"xin": xin[s], "xout": xout[s]}, w)
        P.wait("gpsimd", ctx.s_out)
        print("n queue items:", P.n_instr())
        P.finalize()
    return nc


def l1_in_maps(x_fm, weights):
    m = {
        "xin": x_fm,
        "consts": make_consts(),
        "wq_t": tile_w(weights["Wq"].T),
        "wk_t": tile_w(weights["Wk"].T),
        "wv_t": tile_w(weights["Wv"].T),
        "wo_t": tile_w(weights["Wo"].T),
        "biaspack": pack_bias_cols(weights["bq"], weights["bk"], weights["bv"], weights["bo"]),
        "orft": np.ascontiguousarray(weights["orf"].T),
    }
    return m



class DStream:
    def __init__(self, P, name, queue, nslots=2):
        self.P = P
        self.queue = queue
        self.n = nslots
        self.sems = [P.sem(name + str(i)) for i in range(nslots)]
        self.war = [None] * nslots
        self.dwar = [None] * nslots
        self.pending = None

    def load(self, i, out_ap, in_ap, spill=None):
        slot = i % self.n
        if self.war[slot] is not None:
            self.P.wait(self.queue, *self.war[slot])
            self.war[slot] = None
        if self.dwar[slot] is not None:
            self.P.wait(self.queue, *self.dwar[slot])
            self.dwar[slot] = None
        c = self.P.dma(self.queue, self.sems[slot], out_ap, in_ap)
        if self.pending is not None:
            self._flush()
        if spill is not None:
            self.pending = (slot, spill[1], spill[0])
        return (self.sems[slot], c)

    def _flush(self):
        slot, srcap, dstap = self.pending
        self.pending = None
        sem = self.sems[slot]
        self.P.wait(self.queue, sem, self.P.cnt[sem])
        c2 = self.P.dma(self.queue, sem, dstap, srcap)
        self.dwar[slot] = (sem, c2)

    def wait(self, eng, tok):
        self.P.wait(eng, tok[0], tok[1])

    def set_war(self, i, sem, count):
        self.war[i % self.n] = (sem, count)

    def reset(self):
        if self.pending is not None:
            self._flush()
        self.war = [None] * self.n



def emit_l1_ffn(ctx, seq_aps, w):
    P = ctx.P
    X, XR, Z = ctx.XB, ctx.A1, ctx.A2
    TR0, TR1, W0 = ctx.TR0, ctx.TR1, ctx.W0
    s_dma, s_mm, s_ve, s_sc, s_gp, s_bar = ctx.s_dma, ctx.s_mm, ctx.s_ve, ctx.s_sc, ctx.s_gp, ctx.s_bar
    UV = TR0[:, 0:16 * 512]
    SD = [TR0[:, 16 * 512 + i * 4096: 16 * 512 + (i + 1) * 4096] for i in range(2)]
    A = TR1[:, 0:4096]
    ET = [TR1[:, 4096:8192], TR1[:, 8192:12288]]
    QSTG = TR1[0:64, 4096:6144]
    ATMP = [TR1[:, 4096 + i * 512: 4608 + i * 512] for i in range(4)]
    SDH = [TR1[:, 6144 + i * 512: 6656 + i * 512] for i in range(4)]
    ATS = W0[:, 0:1024]
    UES = W0[:, 1024:2048]
    QF = lambda c: TR1[:, c * T:(c + 1) * T]
    if not hasattr(ctx, "st_w"):
        ctx.st_w = DStream(P, "st_w", "gpsimd")
    if not hasattr(ctx, "st_sd"):
        ctx.st_sd = DStream(P, "st_sd", "gpsimd", nslots=4)
        ctx.st_sdr = DStream(P, "st_sdr", "sync")
        ctx.st_at = DStream(P, "st_at", "sync", nslots=4)
        ctx.st_ats = DStream(P, "st_ats", "sync", nslots=4)
        ctx.st_ue = DStream(P, "st_ue", "gpsimd", nslots=4)
    stw, stsd, stsdr, stat, stats_, stue = (ctx.st_w, ctx.st_sd, ctx.st_sdr,
                                            ctx.st_at, ctx.st_ats, ctx.st_ue)

    if seq_aps.get("xin") is not None:
        P.wait("gpsimd", ctx.s_out)
        for c in range(8):
            P.dma("gpsimd", s_dma, X[:, ts(c, T)], seq_aps["xin"][ts(c, 128), :])
        P.wait("gpsimd", s_dma)
        for e in COMPUTE:
            P.wait(e, s_dma)
            P.wait(e, ctx.s_dmh)
    ctx.ln_stats(lambda c: X[:, ts(c, T)])
    for e in COMPUTE:
        P.wait(e, ctx.s_ve)
    ctx.ln_apply(lambda c: X[:, ts(c, T)], lambda c: XR[:, ts(c, T)])
    P.barrier2(COMPUTE, s_bar)

    if not getattr(ctx, "_l1f_prepped", False):
        ctx._l1f_prepped = True
        for p in range(2):
            for h in range(4):
                P.dma("gpsimd", s_dma, ctx.KEYS[:, (p * 4 + h) * 64:(p * 4 + h) * 64 + 64],
                      w["keysp"][p, h])
        P.wait("gpsimd", s_dma)
        for e in COMPUTE:
            P.wait(e, s_dma)
    P.wait("gpsimd", s_mm)
    ctx.wcache("c_wq", [4, 128, 1024])
    ctx.wcache("c_det", [8, 128, 4096])
    ctx.wcache("c_ue", [64, 128, 512])
    ctx.wcache("c_wup", [32, 128, 1024])
    ctx.wcache("c_wdn", [8, 128, 4096])
    for j in range(4):
        tok = ctx.cload(stw, j, W0[:, (j % 2) * 4096: (j % 2) * 4096 + 1024], "c_wq", j, w["wq_t"][j],
                        view=W0[:, (j % 2) * 4096: (j % 2) * 4096 + 1024].rearrange("q (kc col) -> q kc col", col=128))
        stw.wait("tensor", tok)
        for g in range(NGRP):
            for k in range(8):
                P.emit("tensor", lambda E, j=j, g=g, k=k: E.matmul(
                    ctx.PSBIG[:, ts(g, NG)], W0[:, (j % 2) * 4096 + k * 128: (j % 2) * 4096 + (k + 1) * 128],
                    XR[:, k * T + g * NG: k * T + (g + 1) * NG], start=(k == 0), stop=(k == 7)))
        cnt = P.inc_last("tensor", s_mm)
        stw.set_war(j, "s_mm", cnt)
        P.wait("scalar", s_mm)
        P.inc("scalar", lambda E, j=j: E.copy(QF(j)[:], ctx.PSBIG[:, 0:T]), s_sc)
        P.wait("tensor", s_sc)
    ctx.wcmark("c_wq")
    P.barrier2(COMPUTE, s_bar)
    for p in range(2):
        for h in range(4):
            c = p * 2 + h // 2
            if h % 2 == 0:
                qsrc = QF(c)[0:64, :]
            else:
                P.inc("vector", lambda E, c=c: E.tensor_copy(QSTG, QF(c)[64:128, :]), s_ve)
                P.wait("tensor", s_ve)
                qsrc = QSTG
            for half in range(2):
                ps = (ctx.PSA, ctx.PSB)[half]
                if not (p == 0 and h == 0):
                    P.wait("tensor", s_ve)
                for q8 in range(8):
                    tc = half * 8 + q8
                    P.emit("tensor", lambda E, tc=tc, q8=q8, p=p, h=h, qsrc=qsrc, ps=ps: E.matmul(
                        ps[:, q8 * 64:(q8 + 1) * 64], qsrc[:, ts(tc, 128)],
                        ctx.KEYS[:, (p * 4 + h) * 64:(p * 4 + h) * 64 + 64],
                        start=True, stop=True))
                P.inc_last("tensor", s_mm)
                P.wait("vector", s_mm)
                P.inc("vector", lambda E, half=half, p=p, h=h, ps=ps: E.tensor_copy(
                    UV.rearrange("q (tc w) -> q tc w", w=512)[:, half * 8:(half + 1) * 8,
                                 (p * 4 + h) * 64:(p * 4 + h) * 64 + 64],
                    ps[:].rearrange("q (t w) -> q t w", w=64)), s_ve)
    P.barrier2(COMPUTE, s_bar)

    for tc in range(16):
        for h in range(4):
            simx = UV[:, tc * 512 + h * 64: tc * 512 + h * 64 + 64]
            simy = UV[:, tc * 512 + 256 + h * 64: tc * 512 + 256 + h * 64 + 64]
            col = tc * 4 + h
            P.emit("vector", lambda E, simx=simx: E.max(out=ctx.SC1[:, 0:8], in_=simx))
            P.emit("vector", lambda E, simy=simy: E.max(out=ctx.SC1[:, 8:16], in_=simy))
            P.fence("vector", s_ve)
            P.emit("vector", lambda E: E.tensor_tensor(
                out=ctx.SC1[:, 16:80].rearrange("q (a b) -> q a b", b=8),
                in0=ctx.SC1[:, 0:8].rearrange("q (a b) -> q a b", b=1).broadcast_to([128, 8, 8]),
                in1=ctx.SC1[:, 8:16].rearrange("q (a b) -> q a b", a=1).broadcast_to([128, 8, 8]),
                op=OP.add))
            P.fence("vector", s_ve)
            P.emit("vector", lambda E: E.max(out=ctx.SC1[:, 80:88], in_=ctx.SC1[:, 16:80]))
            P.fence("vector", s_ve)
            P.inc("vector", lambda E: E.tensor_scalar_mul(ctx.SC1[:, 96:97], ctx.SC1[:, 80:81], -1.0), s_ve)
            P.wait("scalar", s_ve)
            P.inc("scalar", lambda E: E.activation(out=ctx.SC1[:, 88:96], in_=ctx.SC1[:, 80:88],
                                                   func=AF.Exp, bias=ctx.SC1[:, 96:97]), s_sc)
            P.wait("vector", s_sc)
            P.emit("vector", lambda E: E.reduce_sum(ctx.SC1[:, 97:98], ctx.SC1[:, 88:96], axis=AX.X))
            P.fence("vector", s_ve)
            P.emit("vector", lambda E, col=col: E.reciprocal(ctx.SC2[:, col:col + 1], ctx.SC1[:, 97:98]))
            P.inc("vector", lambda E, col=col: E.tensor_scalar_mul(
                ctx.SC2[:, 64 + col: 65 + col], ctx.SC1[:, 95:96], 1.0 - 1.0 / 32.0), s_ve)
            P.wait("scalar", s_ve)
            P.emit("scalar", lambda E, simx=simx: E.activation(out=simx, in_=simx, func=AF.Exp,
                                                               bias=ctx.SC1[:, 96:97]))
            P.inc("scalar", lambda E, simy=simy: E.activation(out=simy, in_=simy, func=AF.Exp), s_sc)
            P.wait("vector", s_sc)
            SC2bf = ctx.SC2[:].bitcast(BF16)
            P.emit("vector", lambda E, simy=simy, col=col: E.tensor_tensor(
                out=simy, in0=simy, in1=ctx.SC2[:, col:col + 1].broadcast_to([128, 64]),
                op=OP.mult))
            P.fence("vector", s_ve)
            P.inc("vector", lambda E, col=col, SC2bf=SC2bf: E.tensor_tensor(
                out=SC2bf[:, 256 + col: 257 + col], in0=ctx.SC2[:, 64 + col: 65 + col],
                in1=ctx.SC2[:, col: col + 1], op=OP.mult), s_ve)
    P.barrier2(COMPUTE, s_bar)

    P.wait("gpsimd", s_mm)
    stw.war = [None, None]
    for eg in range(8):
        slot = (eg % 2) * 4096
        tok = ctx.cload(stw, eg, W0[:, slot: slot + 4096], "c_det", eg, w["det"][eg],
                        view=W0[:, slot: slot + 4096].rearrange("q (kc c) -> q kc c", c=512))
        stw.wait("tensor", tok)
        sevac = getattr(ctx, "_sd_evac", {})
        for tc in range(16):
            ps = (ctx.PSA, ctx.PSB)[tc % 2]
            if (eg, tc) != (0, 0) and (eg, tc) != (0, 1):
                P.wait("tensor", s_sc, sevac.get((eg, tc - 2), sevac.get((eg - 1, tc + 14), 0)))
            for k in range(8):
                P.emit("tensor", lambda E, eg=eg, tc=tc, k=k, ps=ps, slot=slot: E.matmul(
                    ps[:], XR[:, k * T + tc * 128: k * T + tc * 128 + 128],
                    W0[:, slot + k * 512: slot + k * 512 + 512], start=(k == 0), stop=(k == 7)))
            cnt = P.inc_last("tensor", s_mm)
            P.wait("scalar", s_mm)
            war = getattr(ctx, "_sdh_war", [None] * 4)
            if war[tc % 4] is not None:
                P.wait("scalar", *war[tc % 4])
            sevac[(eg, tc)] = P.inc("scalar", lambda E, tc=tc, ps=ps: E.copy(SDH[tc % 4][:], ps[:]), s_sc)
            ctx._sd_evac = sevac
            P.wait("gpsimd", s_sc)
            sp = stsd.load(tc, w["sd_scr"][tc, eg], SDH[tc % 4][:])
            war[tc % 4] = sp
            ctx._sdh_war = war
        stw.set_war(eg, "s_mm", P.cnt[s_mm])
    ctx.drain(stw)
    ctx.wcmark("c_det")
    for e in ("sync", "vector", "scalar", "tensor"):
        for sm in stsd.sems:
            P.wait(e, sm)
    P.barrier2(COMPUTE, s_bar)

    SDH4 = [TR1[:, 5120 + i * 512: 5632 + i * 512] for i in range(4)]

    def pass2_tc(tc):
        slot = tc % 2
        tok = stsdr.load(tc, SD[slot].rearrange("q (eg c) -> q eg c", c=512),
                         w["sd_scr"][tc].rearrange("eg q c -> q eg c"))
        stsdr.wait("vector", tok)
        for sm in stat.sems:
            P.wait("vector", sm)
        for h in range(4):
            et = ET[h % 2]
            col = tc * 4 + h
            ux = UV[:, tc * 512 + h * 64: tc * 512 + h * 64 + 64]
            vy = UV[:, tc * 512 + 256 + h * 64: tc * 512 + 256 + h * 64 + 64]
            P.inc("vector", lambda E, ux=ux, vy=vy, et=et: E.tensor_tensor(
                out=et.rearrange("q (a b) -> q a b", b=64),
                in0=ux.rearrange("q (a b) -> q a b", b=1).broadcast_to([128, 64, 64]),
                in1=vy.rearrange("q (a b) -> q a b", a=1).broadcast_to([128, 64, 64]),
                op=OP.mult), s_ve)
            P.fence("vector", s_ve)
            P.inc("vector", lambda E, et=et, col=col: E.scalar_tensor_tensor(
                out=et, in0=et, scalar=ctx.SC2[:].bitcast(BF16)[:, 256 + col: 257 + col], in1=et,
                op0=OP.is_ge, op1=OP.mult), s_ve)
            P.fence("vector", s_ve)
            P.inc("vector", lambda E, et=et, slot=slot: E.tensor_mul(
                et, SD[slot], et), s_ve)
            P.wait("scalar", s_ve)
            P.inc("scalar", lambda E, et=et: E.activation(out=et, in_=et, func=AF.Silu), s_sc)
            P.wait("vector", s_sc)
            P.fence("vector", s_ve)
            if h == 0:
                P.inc("vector", lambda E, et=et: E.tensor_copy(A[:], et), s_ve)
            else:
                P.inc("vector", lambda E, et=et: E.tensor_add(A[:], A[:], et), s_ve)
        stsdr.set_war(tc, "s_ve", P.cnt[s_ve])
        P.wait("tensor", s_ve)
        for tb in range(8):
            if tb > 0:
                P.wait("tensor", s_ve)
            for j in range(4):
                ec = tb * 4 + j
                P.emit("tensor", lambda E, ec=ec, j=j: E.transpose(
                    ctx.PSTR[:, ts(j, 128)], A[:, ts(ec, 128)], ctx.ident_bf[:]))
            P.inc_last("tensor", s_mm)
            P.wait("vector", s_mm)
            ai = tc * 8 + tb
            awar = getattr(ctx, "_atmp_war", [None] * 4)
            if awar[tb % 4] is not None:
                P.wait("vector", *awar[tb % 4])
            P.inc("vector", lambda E, tb=tb: E.tensor_copy(ATMP[tb % 4][:], ctx.PSTR[:, 0:512]), s_ve)
            P.wait("sync", s_ve)
            sp = stat.load(ai, w["at_scr"][tb * 4:(tb + 1) * 4, tc].rearrange("b q c -> q b c"),
                           ATMP[tb % 4][:].rearrange("q (b c) -> q b c", c=128))
            awar[tb % 4] = sp
            ctx._atmp_war = awar

    def p2b_tokg_gen(tokg, at_tots):
        for sm, tot in zip(stat.sems, at_tots):
            P.wait("sync", sm, tot)
        P.wait("sync", s_mm)
        P.wait("gpsimd", s_mm)
        for half in range(2):
            for sec in range(2):
                for ec in range(sec * 16, (sec + 1) * 16):
                    slot = ec % 4
                    t1 = stats_.load(ec, W0[:, slot * 512: slot * 512 + 512].rearrange("q (b c) -> q b c", c=128),
                                     w["at_scr"][ec, tokg * 4:(tokg + 1) * 4].rearrange("b q c -> q b c"))
                    t2 = ctx.cload(stue, ec, W0[:, 2048 + slot * 512: 2048 + slot * 512 + 512],
                                   "c_ue", half * 32 + ec,
                                   w["ue"][ec * 128:(ec + 1) * 128, half * 512:(half + 1) * 512])
                    stats_.wait("tensor", t1)
                    stue.wait("tensor", t2)
                    for mq in range(4):
                        P.emit("tensor", lambda E, mq=mq, slot=slot, ec=ec: E.matmul(
                            ctx.PSBIG[:, ts(mq, NG)],
                            W0[:, 2048 + slot * 512 + mq * 128: 2048 + slot * 512 + (mq + 1) * 128],
                            W0[:, slot * 512: slot * 512 + 512],
                            start=(ec == 0), stop=(ec == 31)))
                    cnt = P.inc_last("tensor", s_mm)
                    stats_.set_war(ec, "s_mm", cnt)
                    stue.set_war(ec, "s_mm", cnt)
                yield
            P.wait("scalar", s_mm)
            for mq in range(4):
                md = half * 4 + mq
                P.inc("scalar", lambda E, mq=mq, md=md, tokg=tokg: E.copy(
                    Z[:, md * T + tokg * 512: md * T + (tokg + 1) * 512], ctx.PSBIG[:, ts(mq, NG)]), s_sc)
            P.wait("tensor", s_sc)
        if tokg == 0:
            ctx.wcmark("c_ue")

    gen = None
    for grp in range(4):
        for tc in range(grp * 4, (grp + 1) * 4):
            pass2_tc(tc)
            if gen is not None:
                next(gen, None)
        at_tots = tuple(P.cnt[sm] for sm in stat.sems)
        if gen is not None:
            for _ in gen:
                pass
        gen = p2b_tokg_gen(grp, at_tots)
    for _ in gen:
        pass
    P.barrier2(COMPUTE, s_bar)

    UP = lambda j: TR0[:, j * 512:(j + 1) * 512]
    if not hasattr(ctx, "st_w4"):
        ctx.st_w4 = DStream(P, "st_w4", "gpsimd", nslots=4)
    stw4 = ctx.st_w4
    for tt in range(4):
        P.wait("gpsimd", s_mm)
        stw4.reset()
        ctx.drain(stw)
        up_evac = {}
        for j in range(32):
            tok = ctx.cload(stw4, j, W0[:, (j % 4) * 1024: (j % 4) * 1024 + 1024], "c_wup", j, w["wup_t"][j],
                            view=W0[:, (j % 4) * 1024: (j % 4) * 1024 + 1024].rearrange("q (kc col) -> q kc col", col=128))
            stw4.wait("tensor", tok)
            ps = (ctx.PSA, ctx.PSB)[j % 2]
            if j >= 2:
                P.wait("tensor", s_sc, up_evac[j - 2])
            elif not (tt == 0 and j < 2):
                P.wait("tensor", s_sc)
                P.wait("tensor", s_ve)
            for k in range(8):
                P.emit("tensor", lambda E, j=j, k=k, tt=tt, ps=ps: E.matmul(
                    ps[:], W0[:, (j % 4) * 1024 + k * 128: (j % 4) * 1024 + (k + 1) * 128],
                    XR[:, k * T + tt * 512: k * T + (tt + 1) * 512], start=(k == 0), stop=(k == 7)))
            cnt = P.inc_last("tensor", s_mm)
            stw4.set_war(j, "s_mm", cnt)
            P.wait("scalar", s_mm)
            up_evac[j] = P.inc("scalar", lambda E, j=j, ps=ps: E.activation(out=UP(j)[:], in_=ps[:], func=AF.Silu), s_sc)
        P.wait("tensor", s_sc)
        P.wait("gpsimd", s_mm)
        stw.war = [None, None]
        ctx.drain(stw4)
        for j in range(8):
            tok = ctx.cload(stw, j, W0[:, (j % 2) * 4096: (j % 2) * 4096 + 4096], "c_wdn", j, w["wdn_t"][j],
                           view=W0[:, (j % 2) * 4096: (j % 2) * 4096 + 4096].rearrange("q (kc col) -> q kc col", col=128))
            stw.wait("tensor", tok)
            ps = (ctx.PSA, ctx.PSB)[j % 2]
            if j >= 2:
                P.wait("tensor", s_ve)
            for k in range(32):
                P.emit("tensor", lambda E, j=j, k=k, ps=ps: E.matmul(
                    ps[:], W0[:, (j % 2) * 4096 + k * 128: (j % 2) * 4096 + (k + 1) * 128],
                    UP(k)[:], start=(k == 0), stop=(k == 31)))
            cnt = P.inc_last("tensor", s_mm)
            stw.set_war(j, "s_mm", cnt)
            P.wait("vector", s_mm)
            P.inc("vector", lambda E, j=j, tt=tt, ps=ps: E.scalar_tensor_tensor(
                out=Z[:, j * T + tt * 512: j * T + (tt + 1) * 512], in0=ps[:], scalar=1.0,
                in1=Z[:, j * T + tt * 512: j * T + (tt + 1) * 512], op0=OP.mult, op1=OP.add), s_ve)
        if tt == 0:
            ctx.wcmark("c_wup")
            ctx.wcmark("c_wdn")
    P.barrier2(COMPUTE, s_bar)

    ctx.ln_stats(lambda c: Z[:, ts(c, T)])
    for e in COMPUTE:
        P.wait(e, ctx.s_ve)
    ctx.ln_residual_update(lambda c: Z[:, ts(c, T)])
    P.barrier2(COMPUTE, s_bar)
    if seq_aps.get("xout") is not None:
        P.wait("gpsimd", ctx.s_ve)
        for c in range(8):
            P.dma("gpsimd", ctx.s_out, seq_aps["xout"][ts(c, 128), :], X[:, ts(c, T)])


def build_l1f_standalone(nseq):
    nc = bass.Bass("TRN2", target_bir_lowering=False)
    xin = nc.dram_tensor("xin", [nseq, 1024, T], F32, kind="ExternalInput").ap()
    xout = nc.dram_tensor("xout", [nseq, 1024, T], F32, kind="ExternalOutput").ap()
    consts = nc.dram_tensor("consts", [128, 832], F32, kind="ExternalInput").ap()
    w = {
        "wup_t": nc.dram_tensor("wup_t", [32, 128, 8, 128], F32, kind="ExternalInput").ap(),
        "wdn_t": nc.dram_tensor("wdn_t", [8, 128, 32, 128], F32, kind="ExternalInput").ap(),
        "wq_t": nc.dram_tensor("wq_t", [4, 128, 8, 128], F32, kind="ExternalInput").ap(),
        "keysp": nc.dram_tensor("keysp", [2, 4, 64, 64], F32, kind="ExternalInput").ap(),
        "det": nc.dram_tensor("det", [8, 128, 8, 512], F32, kind="ExternalInput").ap(),
        "ue": nc.dram_tensor("ue", [4096, 1024], F32, kind="ExternalInput").ap(),
        "sd_scr": nc.dram_tensor("sd_scr", [16, 8, 128, 512], BF16, kind="Internal").ap(),
        "at_scr": nc.dram_tensor("at_scr", [32, 16, 128, 128], BF16, kind="Internal").ap(),
    }
    stack = contextlib.ExitStack()
    with stack:
        P = Prog(nc, stack)
        ctx = Ctx(P)
        ctx.use_wcache = True
        ctx.load_consts(consts)
        for s in range(nseq):
            emit_l1_ffn(ctx, {"xin": xin[s], "xout": xout[s]}, w)
        P.wait("gpsimd", ctx.s_out)
        print("n queue items:", P.n_instr())
        P.finalize()
    return nc


def l1f_in_maps(x_fm, weights):
    keys = weights["keys"]
    keysp = np.stack([np.stack([np.ascontiguousarray(keys[h, :, p, :].T) for h in range(4)])
                      for p in range(2)])
    det = np.ascontiguousarray(
        weights["down_embed"].T.reshape(8, 128, 8, 512).transpose(2, 1, 0, 3))
    return {
        "xin": x_fm,
        "consts": make_consts(),
        "wup_t": tile_w(weights["W_up"].T),
        "wdn_t": tile_w(weights["W_down"].T),
        "wq_t": tile_w(weights["W_q"].T),
        "keysp": keysp,
        "det": det,
        "ue": weights["up_embed"],
    }


def np_l1f_ref(x, w):
    B, Tn, H = x.shape
    HEADS, NK, K = 4, 64, 8
    xd = x.astype(np.float64)
    wd = {k: np.asarray(v, np.float64) for k, v in w.items()}

    def _ln(t, eps=1e-5):
        m = t.mean(-1, keepdims=True)
        v = t.var(-1, keepdims=True)
        return (t - m) / np.sqrt(v + eps)

    def silu(t):
        return t / (1 + np.exp(-t))

    h = _ln(xd)
    q = np.einsum('btd,ed->bte', h, wd["W_q"]).reshape(B, Tn, 2, HEADS, 64).transpose(2, 0, 1, 3, 4)
    sim = np.einsum('pbthn,hkpn->pbthk', q, wd["keys"])
    def topk(a, k):
        idx = np.argsort(-a, axis=-1, kind="stable")[..., :k]
        return np.take_along_axis(a, idx, -1), idx
    sx, ix = topk(sim[0], K)
    sy, iy = topk(sim[1], K)
    all_scores = (sx[..., :, None] + sy[..., None, :]).reshape(B, Tn, HEADS, K * K)
    all_idx = (ix[..., :, None] * NK + iy[..., None, :]).reshape(B, Tn, HEADS, K * K)
    scores, pk = topk(all_scores, K)
    indices = np.take_along_axis(all_idx, pk, axis=-1)
    de = wd["down_embed"][indices]
    ue = wd["up_embed"][indices]
    sm = np.exp(scores - scores.max(-1, keepdims=True))
    sm = sm / sm.sum(-1, keepdims=True)
    wgt = silu(np.einsum('btd,bthkd->bthk', h, de) * sm)
    experts = np.einsum('bthk,bthkd->btd', wgt, ue)
    dense = np.einsum('bti,di->btd', silu(np.einsum('btd,id->bti', h, wd["W_up"])), wd["W_down"])
    z = dense + experts
    return (xd + _ln(z)).astype(np.float32)



def emit_l0_attn(ctx, seq_aps, w):
    P = ctx.P
    X, Y, H = ctx.XB, ctx.A1, ctx.A2
    TR0, TR1, W0 = ctx.TR0, ctx.TR1, ctx.W0
    ATT = TR0
    s_dma, s_mm, s_ve, s_sc, s_gp, s_bar = ctx.s_dma, ctx.s_mm, ctx.s_ve, ctx.s_sc, ctx.s_gp, ctx.s_bar
    stw = ctx.st_w if hasattr(ctx, "st_w") else None
    if stw is None:
        ctx.st_w = DStream(P, "st_w", "gpsimd")
        stw = ctx.st_w
    VH = TR1[0:64, 0:2 * T]
    STG = TR1[64:128, 0:2 * T]
    QPH = TR1[0:64, 2 * T:4 * T]
    KPH = TR1[0:64, 4 * T:6 * T]
    VTOK = W0[:, 6144:6144 + 1040]
    KTOK = ctx.VT2[:, 0:1024]
    SMS = ctx.SMS
    KVS = ctx.KVS
    gcol = lambda slot: (lambda c, slot=slot: ctx.BIAS[:, slot * 8 + c: slot * 8 + c + 1])

    if not getattr(ctx, "_l0a_prepped", False):
        ctx._l0a_prepped = True
        P.dma("gpsimd", s_dma, ctx.PROJB[:], w["proj"][:, :])
    P.wait("sync", ctx.s_sc)
    P.wait("sync", ctx.s_ve)
    P.dma("sync", ctx.s_dmh, ctx.BIAS[:, 0:72], w["biaspack_l0a"][:, :])
    if seq_aps.get("xin") is not None:
        P.wait("gpsimd", ctx.s_out)
        for c in range(8):
            P.dma("gpsimd", s_dma, X[:, ts(c, T)], seq_aps["xin"][ts(c, 128), :])
        P.wait("gpsimd", s_dma)
        for e in COMPUTE:
            P.wait(e, s_dma)
            P.wait(e, ctx.s_dmh)
    ctx.ln_stats(lambda c: X[:, ts(c, T)])
    for e in COMPUTE:
        P.wait(e, ctx.s_ve)
    ctx.ln_apply(lambda c: X[:, ts(c, T)], lambda c: Y[:, ts(c, T)])
    P.barrier2(COMPUTE, s_bar)
    ctx.ln_stats(lambda c: Y[:, ts(c, T)])
    for e in COMPUTE:
        P.wait(e, ctx.s_ve)
    ctx.ln_apply_affine(lambda c: Y[:, ts(c, T)], lambda c: H[:, ts(c, T)],
                        gcol(0), gcol(1))
    P.barrier2(COMPUTE, s_bar)

    P.emit("vector", lambda E: E.memset(
        VTOK.rearrange("q (lc s) -> q lc s", s=65)[:, :, 64:65], 1.0))
    P.inc_last("vector", s_ve)

    P.wait("gpsimd", s_mm)
    stw.war = [None, None]
    ctx.wcache("a_qkv", [24, 128, 1024])
    ctx.wcache("a_wo", [8, 128, 1024])
    ctx.wcache("a_w1", [32, 128, 1024])
    ctx.wcache("a_w2", [8, 128, 4096])
    for p in range(8):
        slot = (p % 2) * 3072
        for wi, wn in enumerate(("wq_t", "wk_t", "wv_t")):
            tok = ctx.cload(stw, p, W0[:, slot + wi * 1024: slot + (wi + 1) * 1024], "a_qkv", p * 3 + wi, w[wn][p],
                            view=W0[:, slot + wi * 1024: slot + (wi + 1) * 1024]
                            .rearrange("q (kc col) -> q kc col", col=128))
        stw.wait("tensor", tok)
        for wi in range(3):
            if not (p == 0 and wi == 0):
                P.wait("tensor", s_sc)
            for g in range(NGRP):
                for k in range(8):
                    P.emit("tensor", lambda E, wi=wi, g=g, k=k, slot=slot: E.matmul(
                        ctx.PSBIG[:, ts(g, NG)], W0[:, slot + wi * 1024 + k * 128: slot + wi * 1024 + (k + 1) * 128],
                        H[:, k * T + g * NG: k * T + (g + 1) * NG], start=(k == 0), stop=(k == 7)))
            P.inc_last("tensor", s_mm)
            P.wait("scalar", s_mm)
            if wi == 2:
                P.emit("scalar", lambda E: E.copy(VH[:, 0:T], ctx.PSBIG[0:64, 0:T]))
                P.inc("scalar", lambda E: E.copy(VH[:, T:2 * T], ctx.PSBIG[64:128, 0:T]), s_sc)
            else:
                dsts = (QPH, KPH)[wi]
                P.emit("scalar", lambda E, dsts=dsts: E.copy(dsts[:, 0:T], ctx.PSBIG[0:64, 0:T]))
                P.inc("scalar", lambda E, dsts=dsts: E.copy(dsts[:, T:2 * T], ctx.PSBIG[64:128, 0:T]), s_sc)
                P.wait("tensor", s_sc)
                phi_done = {}
                for hh in range(2):
                    for g in range(NGRP):
                        gi = hh * NGRP + g
                        psn = (ctx.PSA, ctx.PSB)[gi % 2]
                        if gi >= 2:
                            P.wait("tensor", s_ve, phi_done[gi - 2])
                        P.emit("tensor", lambda E, hh=hh, g=g, dsts=dsts, psn=psn: E.matmul(
                            psn[0:64, 0:NG], ctx.PROJB[:],
                            dsts[:, hh * T + g * NG: hh * T + (g + 1) * NG],
                            start=True, stop=True))
                        P.inc_last("tensor", s_mm)
                        sl = (gi % 2) * 512
                        P.wait("vector", s_mm)
                        P.fence("vector", s_ve)
                        P.inc("vector", lambda E, psn=psn, sl=sl: E.tensor_scalar_min(
                            ctx.SC1[0:64, sl:sl + 512], psn[0:64, 0:NG], 0.0), s_ve)
                        P.wait("scalar", s_ve)
                        P.inc("scalar", lambda E, sl=sl: E.activation(
                            out=ctx.SC2[0:64, sl:sl + 512], in_=ctx.SC1[0:64, sl:sl + 512], func=AF.Exp), s_sc)
                        P.wait("vector", s_sc)
                        P.inc("vector", lambda E, psn=psn, sl=sl: E.tensor_scalar_add(
                            ctx.SC1[0:64, sl:sl + 512], psn[0:64, 0:NG], 1.0), s_ve)
                        P.fence("vector", s_ve)
                        phi_done[gi] = P.inc("vector", lambda E, dsts=dsts, hh=hh, g=g, sl=sl: E.tensor_max(
                            dsts[:, hh * T + g * NG:(hh * T) + (g + 1) * NG],
                            ctx.SC1[0:64, sl:sl + 512], ctx.SC2[0:64, sl:sl + 512]), s_ve)
                P.wait("tensor", s_ve)
        stw.set_war(p, "s_mm", P.cnt[s_mm])
        P.barrier2(COMPUTE, s_bar)
        for hh in range(2):
            h_global = p * 2 + hh
            P.wait("tensor", s_sc)
            P.wait("tensor", s_ve)
            for half in range(2):
                P.wait("tensor", s_ve)
                for j in range(8):
                    lc = half * 8 + j
                    P.emit("tensor", lambda E, lc=lc, j=j, hh=hh: E.transpose(
                        ctx.PSTR[:, j * 64:(j + 1) * 64],
                        KPH[0:64, hh * T + lc * 128: hh * T + (lc + 1) * 128], ctx.ident_bf[0:64, 0:64]))
                P.inc_last("tensor", s_mm)
                P.wait("vector", s_mm)
                P.inc("vector", lambda E, half=half: E.tensor_copy(
                    KTOK[:, half * 512:(half + 1) * 512], ctx.PSTR[:, 0:512]), s_ve)
                P.wait("tensor", s_ve)
                for j in range(8):
                    lc = half * 8 + j
                    P.emit("tensor", lambda E, lc=lc, j=j, hh=hh: E.transpose(
                        ctx.PSTR[:, j * 64:(j + 1) * 64],
                        VH[0:64, hh * T + lc * 128: hh * T + (lc + 1) * 128], ctx.ident_bf[0:64, 0:64]))
                P.inc_last("tensor", s_mm)
                P.wait("vector", s_mm)
                P.inc("vector", lambda E, half=half: E.tensor_copy(
                    VTOK.rearrange("q (lc s) -> q lc s", s=65)[:, half * 8:(half + 1) * 8, 0:64],
                    ctx.PSTR[:, 0:512].rearrange("q (lc s) -> q lc s", s=64)), s_ve)
            P.wait("tensor", s_ve)
            P.inc("vector", lambda E: E.memset(ctx.KVF[0:64, 0:65], 0.0), s_ve)
            for c in range(16):
                if c > 0:
                    P.wait("vector", s_mm)
                P.fence("vector", s_ve)
                P.inc("vector", lambda E: E.tensor_copy(KVS[0:64, 0:65], ctx.KVF[0:64, 0:65]), s_ve)
                P.wait("tensor", s_ve)
                P.emit("tensor", lambda E, c=c, hh=hh: E.matmul(
                    ctx.PSB[:, 0:128], KPH[0:64, hh * T + c * 128: hh * T + (c + 1) * 128],
                    QPH[0:64, hh * T + c * 128: hh * T + (c + 1) * 128], start=True, stop=True))
                P.inc_last("tensor", s_mm)
                P.emit("tensor", lambda E, c=c: E.matmul(
                    ctx.PSA[0:64, 0:65], KTOK[:, c * 64:(c + 1) * 64],
                    VTOK[:, c * 65:(c + 1) * 65], start=True, stop=True))
                P.inc_last("tensor", s_mm)
                P.wait("vector", s_mm)
                P.wait("vector", s_ve)
                P.inc("vector", lambda E: E.tensor_mul(SMS[:], ctx.PSB[:, 0:128], ctx.LTRI[:]), s_ve)
                P.wait("vector", s_ve)
                P.inc("vector", lambda E: E.tensor_add(ctx.KVF[0:64, 0:65], ctx.KVF[0:64, 0:65],
                                                       ctx.PSA[0:64, 0:65]), s_ve)
                P.wait("tensor", s_ve)
                P.emit("tensor", lambda E, c=c, hh=hh: E.matmul(
                    ctx.PSBIG[0:65, c * 128:(c + 1) * 128], KVS[0:64, 0:65],
                    QPH[0:64, hh * T + c * 128: hh * T + (c + 1) * 128], start=True, stop=False))
                P.emit("tensor", lambda E, c=c: E.matmul(
                    ctx.PSBIG[0:65, c * 128:(c + 1) * 128], VTOK[:, c * 65:(c + 1) * 65],
                    SMS[:], start=False, stop=True))
                P.inc_last("tensor", s_mm)
            P.wait("scalar", s_mm)
            P.emit("scalar", lambda E: E.activation(out=ctx.SC1[0:1, 0:T], in_=ctx.PSBIG[64:65, 0:T],
                                                    func=AF.Ln, bias=ctx.CCOL[0:1, 2:3]))
            P.fence("scalar", s_sc)
            P.inc("scalar", lambda E: E.activation(out=ctx.SC1[0:1, 0:T], in_=ctx.SC1[0:1, 0:T],
                                                   func=AF.Exp, scale=-1.0), s_sc)
            P.inc("scalar", lambda E: E.copy(ctx.SC2[0:64, 0:T], ctx.PSBIG[0:64, 0:T]), s_sc)
            P.wait("tensor", s_sc)
            P.wait("vector", s_sc)
            for g in range(NGRP):
                psn = (ctx.PSA, ctx.PSB)[g % 2]
                if g >= 2:
                    P.wait("tensor", s_ve)
                P.emit("tensor", lambda E, g=g, psn=psn: E.matmul(
                    psn[0:64, 0:NG], ctx.zp1_f32[:, 0:64], ctx.SC1[:, ts(g, NG)],
                    start=True, stop=True))
                P.inc_last("tensor", s_mm)
                P.wait("vector", s_mm)
                dst = ATT[hh * 64: hh * 64 + 64, p * T + g * NG: p * T + (g + 1) * NG]
                P.inc("vector", lambda E, g=g, dst=dst, psn=psn: E.tensor_mul(
                    dst, ctx.SC2[0:64, ts(g, NG)], psn[0:64, 0:NG]), s_ve)
            P.barrier2(COMPUTE, s_bar)
    ctx.wcmark("a_qkv")
    P.wait("gpsimd", s_mm)
    stw.war = [None, None]
    for j in range(8):
        tok = ctx.cload(stw, j, W0[:, (j % 2) * 1024: (j % 2) * 1024 + 1024], "a_wo", j, w["wo_t"][j],
                        view=W0[:, (j % 2) * 1024: (j % 2) * 1024 + 1024].rearrange("q (kc col) -> q kc col", col=128))
        stw.wait("tensor", tok)
        if j >= 1:
            P.wait("tensor", s_ve)
        for g in range(NGRP):
            for k in range(8):
                P.emit("tensor", lambda E, j=j, g=g, k=k: E.matmul(
                    ctx.PSBIG[:, ts(g, NG)], W0[:, (j % 2) * 1024 + k * 128: (j % 2) * 1024 + (k + 1) * 128],
                    ATT[:, k * T + g * NG: k * T + (g + 1) * NG], start=(k == 0), stop=(k == 7)))
        cnt = P.inc_last("tensor", s_mm)
        stw.set_war(j, "s_mm", cnt)
        P.wait("vector", s_mm)
        P.inc("vector", lambda E, j=j: E.scalar_tensor_tensor(
            out=Y[:, ts(j, T)], in0=ctx.PSBIG[:, 0:T], scalar=1.0, in1=Y[:, ts(j, T)],
            op0=OP.mult, op1=OP.add), s_ve)
    P.barrier2(COMPUTE, s_bar)
    ctx.ln_stats(lambda c: Y[:, ts(c, T)])
    for e in COMPUTE:
        P.wait(e, ctx.s_ve)
    ctx.ln_apply_affine(lambda c: Y[:, ts(c, T)], lambda c: H[:, ts(c, T)],
                        gcol(2), gcol(3))
    for c in range(8):
        P.emit("vector", lambda E, c=c: E.tensor_scalar_add(
            Y[:, ts(c, T)], Y[:, ts(c, T)], ctx.BIAS[:, 32 + c: 33 + c]))
    P.inc_last("vector", s_ve)
    P.barrier2(COMPUTE, s_bar)
    ctx.wcmark("a_wo")
    UPT = lambda j: TR0[:, j * 512:(j + 1) * 512]
    if not hasattr(ctx, "st_w4"):
        ctx.st_w4 = DStream(P, "st_w4", "gpsimd", nslots=4)
    stw4 = ctx.st_w4
    for tt in range(4):
        P.wait("gpsimd", s_mm)
        stw4.reset()
        ctx.drain(stw)
        up_evac = {}
        for j in range(32):
            tok = ctx.cload(stw4, j, W0[:, (j % 4) * 1024: (j % 4) * 1024 + 1024], "a_w1", j, w["w1_t"][j],
                            view=W0[:, (j % 4) * 1024: (j % 4) * 1024 + 1024].rearrange("q (kc col) -> q kc col", col=128))
            stw4.wait("tensor", tok)
            ps = (ctx.PSA, ctx.PSB)[j % 2]
            if j >= 2:
                P.wait("tensor", s_sc, up_evac[j - 2])
            elif not (tt == 0 and j < 2):
                P.wait("tensor", s_sc)
                P.wait("tensor", s_ve)
            for k in range(8):
                P.emit("tensor", lambda E, j=j, k=k, tt=tt, ps=ps: E.matmul(
                    ps[:], W0[:, (j % 4) * 1024 + k * 128: (j % 4) * 1024 + (k + 1) * 128],
                    H[:, k * T + tt * 512: k * T + (tt + 1) * 512], start=(k == 0), stop=(k == 7)))
            cnt = P.inc_last("tensor", s_mm)
            stw4.set_war(j, "s_mm", cnt)
            P.wait("scalar", s_mm)
            up_evac[j] = P.inc("scalar", lambda E, j=j, ps=ps: E.activation(
                out=UPT(j)[:], in_=ps[:], func=AF.Gelu, bias=ctx.BIAS[:, 40 + j: 41 + j]), s_sc)
        P.wait("tensor", s_sc)
        P.wait("gpsimd", s_mm)
        stw.war = [None, None]
        ctx.drain(stw4)
        for j in range(8):
            tok = ctx.cload(stw, j, W0[:, (j % 2) * 4096: (j % 2) * 4096 + 4096], "a_w2", j, w["w2_t"][j],
                           view=W0[:, (j % 2) * 4096: (j % 2) * 4096 + 4096].rearrange("q (kc col) -> q kc col", col=128))
            stw.wait("tensor", tok)
            ps = (ctx.PSA, ctx.PSB)[j % 2]
            if j >= 2:
                P.wait("tensor", s_ve)
            for k in range(32):
                P.emit("tensor", lambda E, j=j, k=k, ps=ps: E.matmul(
                    ps[:], W0[:, (j % 2) * 4096 + k * 128: (j % 2) * 4096 + (k + 1) * 128],
                    UPT(k)[:], start=(k == 0), stop=(k == 31)))
            cnt = P.inc_last("tensor", s_mm)
            stw.set_war(j, "s_mm", cnt)
            P.wait("vector", s_mm)
            P.inc("vector", lambda E, j=j, tt=tt, ps=ps: E.scalar_tensor_tensor(
                out=Y[:, j * T + tt * 512: j * T + (tt + 1) * 512], in0=ps[:], scalar=1.0,
                in1=Y[:, j * T + tt * 512: j * T + (tt + 1) * 512], op0=OP.mult, op1=OP.add), s_ve)
        if tt == 0:
            ctx.wcmark("a_w1")
            ctx.wcmark("a_w2")
    P.barrier2(COMPUTE, s_bar)
    ctx.ln_stats(lambda c: Y[:, ts(c, T)])
    for e in COMPUTE:
        P.wait(e, ctx.s_ve)
    ctx.ln_residual_update(lambda c: Y[:, ts(c, T)])
    P.barrier2(COMPUTE, s_bar)
    if seq_aps.get("xout") is not None:
        P.wait("gpsimd", ctx.s_ve)
        for c in range(8):
            P.dma("gpsimd", ctx.s_out, seq_aps["xout"][ts(c, 128), :], X[:, ts(c, T)])


def build_l0a_standalone(nseq):
    nc = bass.Bass("TRN2", target_bir_lowering=False)
    xin = nc.dram_tensor("xin", [nseq, 1024, T], F32, kind="ExternalInput").ap()
    xout = nc.dram_tensor("xout", [nseq, 1024, T], F32, kind="ExternalOutput").ap()
    consts = nc.dram_tensor("consts", [128, 832], F32, kind="ExternalInput").ap()
    w = {
        "wq_t": nc.dram_tensor("wq_t", [8, 128, 8, 128], F32, kind="ExternalInput").ap(),
        "wk_t": nc.dram_tensor("wk_t", [8, 128, 8, 128], F32, kind="ExternalInput").ap(),
        "wv_t": nc.dram_tensor("wv_t", [8, 128, 8, 128], F32, kind="ExternalInput").ap(),
        "wo_t": nc.dram_tensor("wo_t", [8, 128, 8, 128], F32, kind="ExternalInput").ap(),
        "w1_t": nc.dram_tensor("w1_t", [32, 128, 8, 128], F32, kind="ExternalInput").ap(),
        "w2_t": nc.dram_tensor("w2_t", [8, 128, 32, 128], F32, kind="ExternalInput").ap(),
        "proj": nc.dram_tensor("proj", [64, 64], F32, kind="ExternalInput").ap(),
        "biaspack_l0a": nc.dram_tensor("biaspack_l0a", [128, 72], F32, kind="ExternalInput").ap(),
    }
    stack = contextlib.ExitStack()
    with stack:
        P = Prog(nc, stack)
        ctx = Ctx(P)
        ctx.use_wcache = True
        ctx.load_consts(consts)
        for s in range(nseq):
            emit_l0_attn(ctx, {"xin": xin[s], "xout": xout[s]}, w)
        P.wait("gpsimd", ctx.s_out)
        print("n queue items:", P.n_instr())
        P.finalize()
    return nc


def l0a_in_maps(x_fm, weights):
    bp = pack_bias_cols(weights["g1"], weights["beta1"], weights["g2"], weights["beta2"],
                        weights["b2"], weights["b1"])
    return {
        "xin": x_fm,
        "consts": make_consts(),
        "wq_t": tile_w(weights["wq"]),
        "wk_t": tile_w(weights["wk"]),
        "wv_t": tile_w(weights["wv"]),
        "wo_t": tile_w(weights["wo"]),
        "w1_t": tile_w(weights["w1"]),
        "w2_t": tile_w(weights["w2"]),
        "proj": weights["proj"],
        "biaspack_l0a": bp,
    }


def np_l0a_ref(x, w):
    B, S, D, H, DH = x.shape[0], x.shape[1], 1024, 16, 64
    xd = x.astype(np.float64)
    wd = {k: np.asarray(v, np.float64) for k, v in w.items()}

    def _lnp(t, eps=1e-5):
        m = t.mean(-1, keepdims=True)
        v = t.var(-1, keepdims=True)
        return (t - m) / np.sqrt(v + eps)

    def _lna(t, g, b, eps=1e-5):
        return _lnp(t, eps) * g + b

    def _phi(t, proj):
        u = t @ proj
        return np.where(u > 0, u + 1.0, np.exp(u))

    y = _lnp(xd)
    h = _lna(y, wd["g1"], wd["beta1"])
    q = _phi((h @ wd["wq"]).reshape(B, S, H, DH), wd["proj"])
    k = _phi((h @ wd["wk"]).reshape(B, S, H, DH), wd["proj"])
    v = (h @ wd["wv"]).reshape(B, S, H, DH)
    out = np.zeros((B, S, H, DH))
    for b in range(B):
        for hh in range(H):
            kh = k[b, :, hh]
            vh = v[b, :, hh]
            qh = q[b, :, hh]
            kv = np.einsum('sd,se->sde', kh, vh).cumsum(axis=0)
            ksum = kh.cumsum(axis=0)
            num = np.einsum('sd,sde->se', qh, kv)
            den = np.einsum('sd,sd->s', qh, ksum) + 1e-6
            out[b, :, hh] = num / den[:, None]
    attn = out.reshape(B, S, H * DH) @ wd["wo"]
    u = y + attn
    h2 = _lna(u, wd["g2"], wd["beta2"])
    g = h2 @ wd["w1"] + wd["b1"]
    gel = 0.5 * g * (1 + np.vectorize(math_erf)(g / np.sqrt(2))) if False else None
    from scipy.special import erf as _erf
    gel = 0.5 * g * (1 + _erf(g / np.sqrt(2)))
    z = u + gel @ wd["w2"] + wd["b2"]
    return (xd + _lnp(z)).astype(np.float32)



def emit_l0_ffn(ctx, seq_aps, w):
    P = ctx.P
    X, XL, A2 = ctx.XB, ctx.A1, ctx.A2
    TR0, TR1, W0 = ctx.TR0, ctx.TR1, ctx.W0
    s_dma, s_mm, s_ve, s_sc, s_gp, s_bar = ctx.s_dma, ctx.s_mm, ctx.s_ve, ctx.s_sc, ctx.s_gp, ctx.s_bar
    s_dmh = ctx.s_dmh
    stw = ctx.st_w if hasattr(ctx, "st_w") else None
    if stw is None:
        ctx.st_w = DStream(P, "st_w", "gpsimd")
        stw = ctx.st_w
    if not hasattr(ctx, "st_w4"):
        ctx.st_w4 = DStream(P, "st_w4", "gpsimd", nslots=4)
    stw4 = ctx.st_w4
    QPH = TR1[0:64, 0:2 * T]
    KPH = TR1[0:64, 2 * T:4 * T]
    VH = TR1[0:64, 4 * T:6 * T]
    EB = TR0[:, 0:4096]
    OFM = TR0[:, 4096:4096 + 4 * T]
    MSK = TR0[:, 4096 + 4 * T:4096 + 4 * T + 4096]
    GL = TR0[:, 0:22 * 512]
    VT = ctx.VT2
    ROT = W0
    SH = lambda c: A2[:, c * T:(c + 1) * T]
    HM = lambda c: A2[:, ((c + 4) % 8) * T:(((c + 4) % 8) + 1) * T]

    if not getattr(ctx, "_l0f_prepped", False):
        ctx._l0f_prepped = True
    if seq_aps.get("xin") is not None:
        P.wait("gpsimd", ctx.s_out)
        for c in range(8):
            P.dma("gpsimd", s_dma, X[:, ts(c, T)], seq_aps["xin"][ts(c, 128), :])
        P.wait("gpsimd", s_dma)
        for e in COMPUTE:
            P.wait(e, s_dma)
            P.wait(e, s_dmh)
    ctx.ln_stats(lambda c: X[:, ts(c, T)])
    for e in COMPUTE:
        P.wait(e, ctx.s_ve)
    ctx.ln_apply(lambda c: X[:, ts(c, T)], lambda c: XL[:, ts(c, T)])
    P.barrier2(COMPUTE, s_bar)

    for layer in range(2):
        P.wait("sync", s_sc)
        P.wait("sync", s_ve)
        P.dma("sync", s_dmh, ctx.BIAS[:, 0:70], w["bp_l0f"][layer])
        for e in COMPUTE:
            P.wait(e, s_dmh)
        for c in range(4):
            P.emit("vector", lambda E, c=c: E.memset(SH(c)[:, 0:1], 0.0))
            P.emit("vector", lambda E, c=c: E.tensor_copy(SH(c)[:, 1:T], XL[:, (4 + c) * T:(5 + c) * T - 1]))
        P.inc_last("vector", s_ve)
        P.barrier2(COMPUTE, s_bar)
        shift_src = lambda c: XL[:, ts(c, T)] if c < 4 else SH(c - 4)
        P.wait("gpsimd", s_ve)
        P.wait("gpsimd", s_mm)
        P.dma("gpsimd", s_dma, A2[0:64, 4 * T:6 * T], w["rotq"][:, :])
        P.dma("gpsimd", s_dma, A2[0:64, 6 * T:8 * T], w["rotk"][:, :])
        P.dma("gpsimd", s_dma, MSK[:], w["msk"][:, :])
        P.wait("gpsimd", s_dma)
        for e in COMPUTE:
            P.wait(e, s_dma)
        P.inc("vector", lambda E: E.memset(
            VT[:].rearrange("q (lc s) -> q lc s", s=65)[:, :, 64:65], 1.0), s_ve)
        P.wait("gpsimd", s_mm)
        stw.war = [None, None]
        ctx.wcache("f_qkv%d" % layer, [12, 128, 1024])
        ctx.wcache("f_wo2%d" % layer, [8, 128, 512])
        ctx.wcache("f_w1p%d" % layer, [44, 128, 1024])
        ctx.wcache("f_w2p%d" % layer, [8, 128, 2816])
        for p in range(4):
            slot = (p % 2) * 3072
            for wi in range(3):
                tok = ctx.cload(stw, p, W0[:, slot + wi * 1024: slot + (wi + 1) * 1024],
                                "f_qkv%d" % layer, wi * 4 + p, w["wqkv_t"][layer, wi * 4 + p],
                                view=W0[:, slot + wi * 1024: slot + (wi + 1) * 1024]
                                .rearrange("q (kc col) -> q kc col", col=128))
            stw.wait("tensor", tok)
            for wi in range(3):
                if not (p == 0 and wi == 0):
                    P.wait("tensor", s_sc)
                for g in range(NGRP):
                    for k in range(8):
                        P.emit("tensor", lambda E, wi=wi, g=g, k=k, slot=slot: E.matmul(
                            ctx.PSBIG[:, ts(g, NG)],
                            W0[:, slot + wi * 1024 + k * 128: slot + wi * 1024 + (k + 1) * 128],
                            shift_src(k)[:, ts(g, NG)], start=(k == 0), stop=(k == 7)))
                P.inc_last("tensor", s_mm)
                P.wait("scalar", s_mm)
                dst = (QPH, KPH, VH)[wi]
                P.emit("scalar", lambda E, dst=dst: E.copy(dst[:, 0:T], ctx.PSBIG[0:64, 0:T]))
                P.inc("scalar", lambda E, dst=dst: E.copy(dst[:, T:2 * T], ctx.PSBIG[64:128, 0:T]), s_sc)
            stw.set_war(p, "s_mm", P.cnt[s_mm])
            P.barrier2(COMPUTE, s_bar)
            if p == 0:
                pass
            for hh in range(2):
                h_global = p * 2 + hh
                emit_l0f_head(ctx, w, layer, hh, QPH, KPH, VH, EB, OFM, MSK, VT, h_global)
        P.barrier2(COMPUTE, s_bar)
        ctx.wcmark("f_qkv%d" % layer)
        P.wait("gpsimd", s_mm)
        stw.war = [None, None]
        for j in range(8):
            tok = ctx.cload(stw, j, W0[:, (j % 2) * 512: (j % 2) * 512 + 512],
                            "f_wo2%d" % layer, j, w["wo2_t"][layer, j],
                            view=W0[:, (j % 2) * 512: (j % 2) * 512 + 512].rearrange("q (kc col) -> q kc col", col=128))
            stw.wait("tensor", tok)
            if j >= 1:
                P.wait("tensor", s_ve)
            for g in range(NGRP):
                for k in range(4):
                    P.emit("tensor", lambda E, j=j, g=g, k=k: E.matmul(
                        ctx.PSBIG[:, ts(g, NG)], W0[:, (j % 2) * 512 + k * 128: (j % 2) * 512 + (k + 1) * 128],
                        OFM[:, k * T + g * NG: k * T + (g + 1) * NG], start=(k == 0), stop=(k == 3)))
            cnt = P.inc_last("tensor", s_mm)
            stw.set_war(j, "s_mm", cnt)
            P.wait("vector", s_mm)
            P.inc("vector", lambda E, j=j: E.scalar_tensor_tensor(
                out=XL[:, ts(j, T)], in0=ctx.PSBIG[:, 0:T], scalar=1.0, in1=XL[:, ts(j, T)],
                op0=OP.mult, op1=OP.add), s_ve)
        ctx.wcmark("f_wo2%d" % layer)
        P.barrier2(COMPUTE, s_bar)
        for c in range(4):
            P.emit("vector", lambda E, c=c: E.memset(SH(c)[:, 0:1], 0.0))
            P.emit("vector", lambda E, c=c: E.tensor_copy(SH(c)[:, 1:T], XL[:, (4 + c) * T:(5 + c) * T - 1]))
        P.inc_last("vector", s_ve)
        P.barrier2(COMPUTE, s_bar)
        ctx.ln_stats(shift_src)
        for e in COMPUTE:
            P.wait(e, ctx.s_ve)
        ctx.ln_apply_affine(shift_src, HM, lambda c: ctx.BIAS[:, c:c + 1],
                            lambda c: ctx.BIAS[:, 8 + c:9 + c])
        for c in range(8):
            P.emit("vector", lambda E, c=c: E.tensor_scalar_add(
                XL[:, ts(c, T)], XL[:, ts(c, T)], ctx.BIAS[:, 16 + c:17 + c]))
        P.inc_last("vector", s_ve)
        P.barrier2(COMPUTE, s_bar)
        for tt in range(4):
            P.wait("gpsimd", s_mm)
            stw4.reset()
            ctx.drain(stw)
            glu_done = {}
            for i in range(22):
                slot = (i % 4) * 2048
                pb = (i % 2) * 1024
                tok = ctx.cload(stw4, i, W0[:, slot: slot + 1024], "f_w1p%d" % layer, i,
                                w["w1p_t"][layer, i],
                                view=W0[:, slot: slot + 1024].rearrange("q (kc col) -> q kc col", col=128))
                tok = ctx.cload(stw4, i, W0[:, slot + 1024: slot + 2048], "f_w1p%d" % layer, 22 + i,
                                w["w1p_t"][layer, 22 + i],
                                view=W0[:, slot + 1024: slot + 2048].rearrange("q (kc col) -> q kc col", col=128))
                stw4.wait("tensor", tok)
                if i >= 2:
                    P.wait("tensor", s_ve, glu_done[i - 2])
                elif tt > 0 and i < 2:
                    P.wait("tensor", s_ve)
                for k in range(8):
                    P.emit("tensor", lambda E, i=i, k=k, tt=tt, slot=slot, pb=pb: E.matmul(
                        ctx.PSBIG[:, pb:pb + 512], W0[:, slot + k * 128: slot + (k + 1) * 128],
                        HM(k)[:, tt * 512:(tt + 1) * 512], start=(k == 0), stop=(k == 7)))
                for k in range(8):
                    P.emit("tensor", lambda E, i=i, k=k, tt=tt, slot=slot, pb=pb: E.matmul(
                        ctx.PSBIG[:, pb + 512:pb + 1024], W0[:, slot + 1024 + k * 128: slot + 1024 + (k + 1) * 128],
                        HM(k)[:, tt * 512:(tt + 1) * 512], start=(k == 0), stop=(k == 7)))
                cnt = P.inc_last("tensor", s_mm)
                stw4.set_war(i, "s_mm", cnt)
                P.wait("scalar", s_mm)
                P.inc("scalar", lambda E, i=i, pb=pb: E.activation(
                    out=ctx.SC1[:, (i % 2) * 512:(i % 2) * 512 + 512], in_=ctx.PSBIG[:, pb + 512:pb + 1024],
                    func=AF.Gelu, bias=ctx.BIAS[:, 46 + i:47 + i]), s_sc)
                P.wait("vector", s_sc)
                glu_done[i] = P.inc("vector", lambda E, i=i, pb=pb: E.scalar_tensor_tensor(
                    out=GL[:, ts(i, 512)], in0=ctx.PSBIG[:, pb:pb + 512], scalar=ctx.BIAS[:, 24 + i:25 + i],
                    in1=ctx.SC1[:, (i % 2) * 512:(i % 2) * 512 + 512], op0=OP.add, op1=OP.mult), s_ve)
            P.wait("tensor", s_ve)
            P.wait("gpsimd", s_mm)
            stw.war = [None, None]
            ctx.drain(stw4)
            for j in range(8):
                slot = (j % 2) * 2816
                tok = ctx.cload(stw, j, W0[:, slot: slot + 2816], "f_w2p%d" % layer, j,
                                w["w2p_t"][layer, j],
                                view=W0[:, slot: slot + 2816].rearrange("q (kc col) -> q kc col", col=128))
                stw.wait("tensor", tok)
                if j >= 2:
                    P.wait("tensor", s_ve)
                for k in range(22):
                    P.emit("tensor", lambda E, j=j, k=k, slot=slot: E.matmul(
                        ctx.PSA[:] if j % 2 == 0 else ctx.PSB[:],
                        W0[:, slot + k * 128: slot + (k + 1) * 128],
                        GL[:, ts(k, 512)], start=(k == 0), stop=(k == 21)))
                cnt = P.inc_last("tensor", s_mm)
                stw.set_war(j, "s_mm", cnt)
                P.wait("vector", s_mm)
                ps = ctx.PSA if j % 2 == 0 else ctx.PSB
                P.inc("vector", lambda E, j=j, tt=tt, ps=ps: E.scalar_tensor_tensor(
                    out=XL[:, j * T + tt * 512: j * T + (tt + 1) * 512], in0=ps[:], scalar=1.0,
                    in1=XL[:, j * T + tt * 512: j * T + (tt + 1) * 512], op0=OP.mult, op1=OP.add), s_ve)
            if tt == 0:
                ctx.wcmark("f_w1p%d" % layer)
                ctx.wcmark("f_w2p%d" % layer)
        P.barrier2(COMPUTE, s_bar)
    ctx.ln_stats(lambda c: XL[:, ts(c, T)])
    for e in COMPUTE:
        P.wait(e, ctx.s_ve)
    ctx.ln_residual_update(lambda c: XL[:, ts(c, T)])
    P.barrier2(COMPUTE, s_bar)
    if seq_aps.get("xout") is not None:
        P.wait("gpsimd", ctx.s_ve)
        for c in range(8):
            P.dma("gpsimd", ctx.s_out, seq_aps["xout"][ts(c, 128), :], X[:, ts(c, T)])


def emit_l0f_head(ctx, w, layer, hh, QPH, KPH, VH, EB, OFM, MSK, VT, h_global):
    P = ctx.P
    T_ = T
    A2 = ctx.A2
    s_mm, s_ve, s_sc, s_bar = ctx.s_mm, ctx.s_ve, ctx.s_sc, ctx.s_bar
    Aq = A2[0:64, 4 * T_:5 * T_]
    Bq = A2[0:64, 5 * T_:6 * T_]
    Ck = A2[0:64, 6 * T_:7 * T_]
    Dk = A2[0:64, 7 * T_:8 * T_]
    qs_col = ctx.BIAS[0:64, 68:69]
    ks_col = ctx.BIAS[0:64, 69:70]

    for side in range(2):
        src = (QPH, KPH)[side][0:64, hh * T_:(hh + 1) * T_]
        At, Bt = ((Aq, Bq), (Ck, Dk))[side]
        scol = (qs_col, ks_col)[side]
        P.wait("scalar", s_ve)
        P.inc("scalar", lambda E, src=src: E.activation(out=ctx.SC1[0:64, :], in_=src, func=AF.Square), s_sc)
        P.wait("tensor", s_sc)
        for g in range(NGRP):
            if g > 0 or side > 0:
                P.wait("tensor", s_ve)
            P.emit("tensor", lambda E, g=g: E.matmul(
                ctx.PSA[0:1, 0:NG], ctx.ones_f32[0:64, 0:1], ctx.SC1[0:64, ts(g, NG)],
                start=True, stop=True))
            P.inc_last("tensor", s_mm)
            P.wait("vector", s_mm)
            P.inc("vector", lambda E, g=g: E.tensor_copy(ctx.SC2[0:1, ts(g, NG)], ctx.PSA[0:1, 0:NG]), s_ve)
        P.wait("scalar", s_ve)
        P.emit("scalar", lambda E: E.activation(out=ctx.SC2[0:1, 0:T_], in_=ctx.SC2[0:1, 0:T_],
                                                func=AF.Ln, bias=ctx.CCOL[0:1, 3:4]))
        P.fence("scalar", s_sc)
        P.inc("scalar", lambda E: E.activation(out=ctx.SC2[0:1, 0:T_], in_=ctx.SC2[0:1, 0:T_],
                                               func=AF.Exp, scale=-0.5), s_sc)
        P.wait("tensor", s_sc)
        for g in range(NGRP):
            if True:
                P.wait("tensor", s_ve)
            P.emit("tensor", lambda E, g=g: E.matmul(
                ctx.PSB[0:64, 0:NG], ctx.zp1_f32[:, 0:64], ctx.SC2[:, ts(g, NG)],
                start=True, stop=True))
            P.inc_last("tensor", s_mm)
            P.wait("vector", s_mm)
            P.inc("vector", lambda E, g=g, src=src, scol=scol: E.scalar_tensor_tensor(
                out=src[:, ts(g, NG)], in0=src[:, ts(g, NG)], scalar=scol,
                in1=ctx.PSB[0:64, 0:NG], op0=OP.mult, op1=OP.mult), s_ve)
        P.wait("vector", s_ve)
        P.emit("vector", lambda E, src=src, At=At: E.tensor_mul(ctx.SC1[0:64, :], src, At))
        P.emit("vector", lambda E, src=src: E.tensor_copy(ctx.SC2[0:32, :], src[32:64, :]))
        P.emit("vector", lambda E, src=src: E.tensor_copy(ctx.SC2[32:64, :], src[0:32, :]))
        P.fence("vector", s_ve)
        P.emit("vector", lambda E, Bt=Bt: E.tensor_mul(ctx.SC2[0:64, :], ctx.SC2[0:64, :], Bt))
        P.fence("vector", s_ve)
        P.inc("vector", lambda E, src=src: E.tensor_add(src, ctx.SC1[0:64, :], ctx.SC2[0:64, :]), s_ve)
        P.wait("tensor", s_ve)
        P.wait("scalar", s_ve)
    for half in range(2):
        P.wait("tensor", s_ve)
        for j in range(8):
            lc = half * 8 + j
            P.emit("tensor", lambda E, lc=lc, j=j: E.transpose(
                ctx.PSTR[:, j * 64:(j + 1) * 64],
                VH[0:64, hh * T_ + lc * 128: hh * T_ + (lc + 1) * 128], ctx.ident_bf[0:64, 0:64]))
        P.inc_last("tensor", s_mm)
        P.wait("vector", s_mm)
        P.inc("vector", lambda E, half=half: E.tensor_copy(
            VT[:].rearrange("q (lc s) -> q lc s", s=65)[:, half * 8:(half + 1) * 8, 0:64],
            ctx.PSTR[:, 0:512].rearrange("q (lc s) -> q lc s", s=64)), s_ve)
    EBP = [EB, ctx.SC2[:].bitcast(BF16)]
    if not hasattr(ctx, "_l0f_evac_last"):
        ctx._l0f_evac_last = {}
    evl = {0: 0, 1: 0}
    exp_done = {}
    o_wait = {}

    def ebuild(wdw, ktcs):
        par = wdw % 2
        eb = EBP[par]
        mc0 = 2032 + par * 8
        for li, ktc in enumerate(ktcs):
            rel_slot = (ktc * 128 - wdw * 512 + 512) // 128
            psl = (ctx.PSA, ctx.PSB)[li % 2]
            if evl[li % 2]:
                P.wait("tensor", s_sc, evl[li % 2])
            P.emit("tensor", lambda E, ktc=ktc, wdw=wdw, psl=psl: E.matmul(
                psl[:], KPH[0:64, hh * T_ + ktc * 128: hh * T_ + (ktc + 1) * 128],
                QPH[0:64, hh * T_ + wdw * 512: hh * T_ + (wdw + 1) * 512], start=True, stop=False))
            P.emit("tensor", lambda E, rel_slot=rel_slot, psl=psl: E.matmul(
                psl[:], ctx.ident_bf[:], MSK[:, rel_slot * 512:(rel_slot + 1) * 512],
                start=False, stop=True))
            P.inc_last("tensor", s_mm)
            P.wait("scalar", s_mm)
            evl[li % 2] = P.inc("scalar", lambda E, li=li, psl=psl, eb=eb: E.copy(
                eb[:, ts(li, NG)], psl[:]), s_sc)
            P.wait("vector", s_sc)
            P.inc("vector", lambda E, li=li, eb=eb, mc0=mc0: E.tensor_reduce(
                ctx.SC1[:, mc0 + li:mc0 + li + 1], eb[:, ts(li, NG)], axis=AX.X, op=OP.max), s_ve)
        if len(ktcs) < 8:
            P.inc("vector", lambda E, n=len(ktcs), mc0=mc0: E.memset(
                ctx.SC1[:, mc0 + n:mc0 + 8], -1e30), s_ve)

    def maxpath_exps(wdw, ktcs):
        par = wdw % 2
        eb = EBP[par]
        mc0 = 2032 + par * 8
        mcell = 6 if par == 0 else 4
        P.fence("vector", s_ve)
        P.inc("vector", lambda E, par=par, mc0=mc0: E.tensor_reduce(
            ctx.SMS[:, par:par + 1], ctx.SC1[:, mc0:mc0 + 8], axis=AX.X, op=OP.max), s_ve)
        P.wait("tensor", s_ve)
        P.emit("tensor", lambda E, par=par: E.transpose(
            ctx.PSTR[0:1, par * 128:(par + 1) * 128], ctx.SMS[:, par:par + 1], ctx.ident_bf[:]))
        P.inc_last("tensor", s_mm)
        P.wait("vector", s_mm)
        P.inc("vector", lambda E, par=par: E.tensor_copy(
            ctx.SC1[0:1, 1760 + par * 128: 1888 + par * 128],
            ctx.PSTR[0:1, par * 128:(par + 1) * 128]), s_ve)
        P.fence("vector", s_ve)
        P.inc("vector", lambda E, par=par, mcell=mcell: E.tensor_reduce(
            ctx.CCOL[0:1, mcell:mcell + 1], ctx.SC1[0:1, 1760 + par * 128: 1888 + par * 128],
            axis=AX.X, op=OP.max), s_ve)
        P.wait("tensor", s_ve)
        P.emit("tensor", lambda E, par=par, mcell=mcell: E.matmul(
            ctx.PSM[:, par:par + 1],
            ctx.zp1_f32[:, 0:128], ctx.CCOL[:, mcell:mcell + 1], start=True, stop=True))
        P.inc_last("tensor", s_mm)
        P.wait("vector", s_mm)
        P.inc("vector", lambda E, par=par, mcell=mcell: E.tensor_scalar_mul(
            ctx.CCOL[:, mcell + 1:mcell + 2], ctx.PSM[:, par:par + 1], -1.0), s_ve)
        P.wait("scalar", s_ve)
        P.fence("scalar", s_sc)
        for li in range(len(ktcs)):
            P.inc("scalar", lambda E, li=li, eb=eb, mcell=mcell: E.activation(
                out=eb[:, ts(li, NG)], in_=eb[:, ts(li, NG)], func=AF.Exp,
                bias=ctx.CCOL[:, mcell + 1:mcell + 2]), s_sc)
        exp_done[wdw] = P.cnt[s_sc]

    def ommse(wdw, ktcs):
        par = wdw % 2
        eb = EBP[par]
        P.wait("tensor", s_sc, exp_done[wdw])
        for li, ktc in enumerate(ktcs):
            P.emit("tensor", lambda E, li=li, ktc=ktc, wdw=wdw, nl=len(ktcs), eb=eb: E.matmul(
                ctx.PSBIG[0:65, wdw * 512:(wdw + 1) * 512], VT[:, ktc * 65:(ktc + 1) * 65],
                eb[:, ts(li, NG)], start=(li == 0), stop=(li == nl - 1)))
        P.inc_last("tensor", s_mm)

    prev = None
    for wdw in range(4):
        ktcs = list(range(max(0, (wdw - 1) * 4), (wdw + 1) * 4))
        ebuild(wdw, ktcs)
        if prev is not None:
            ommse(*prev)
        maxpath_exps(wdw, ktcs)
        prev = (wdw, ktcs)
    ommse(*prev)
    P.wait("scalar", s_mm)
    P.emit("scalar", lambda E: E.activation(out=ctx.SC1[0:1, 0:T_], in_=ctx.PSBIG[64:65, 0:T_],
                                            func=AF.Ln))
    P.fence("scalar", s_sc)
    P.inc("scalar", lambda E: E.activation(out=ctx.SC1[0:1, 0:T_], in_=ctx.SC1[0:1, 0:T_],
                                           func=AF.Exp, scale=-1.0), s_sc)
    P.inc("scalar", lambda E: E.copy(ctx.SC2[0:64, 0:T_], ctx.PSBIG[0:64, 0:T_]), s_sc)
    P.wait("tensor", s_sc)
    P.wait("vector", s_sc)
    for g in range(NGRP):
        psn = (ctx.PSA, ctx.PSB)[g % 2]
        if g >= 2:
            P.wait("tensor", s_ve)
        P.emit("tensor", lambda E, g=g, psn=psn: E.matmul(
            psn[0:64, 0:NG], ctx.zp1_f32[:, 0:64], ctx.SC1[:, ts(g, NG)],
            start=True, stop=True))
        P.inc_last("tensor", s_mm)
        P.wait("vector", s_mm)
        dst = OFM[(h_global % 2) * 64:(h_global % 2) * 64 + 64,
                  (h_global // 2) * T_ + g * NG:(h_global // 2) * T_ + (g + 1) * NG]
        P.inc("vector", lambda E, g=g, dst=dst, psn=psn: E.tensor_mul(
            dst, ctx.SC2[0:64, ts(g, NG)], psn[0:64, 0:NG]), s_ve)
    P.barrier2(COMPUTE, s_bar)


def build_l0f_standalone(nseq):
    nc = bass.Bass("TRN2", target_bir_lowering=False)
    xin = nc.dram_tensor("xin", [nseq, 1024, T], F32, kind="ExternalInput").ap()
    xout = nc.dram_tensor("xout", [nseq, 1024, T], F32, kind="ExternalOutput").ap()
    consts = nc.dram_tensor("consts", [128, 832], F32, kind="ExternalInput").ap()
    w = {
        "wqkv_t": nc.dram_tensor("wqkv_t", [2, 12, 128, 8, 128], F32, kind="ExternalInput").ap(),
        "wo2_t": nc.dram_tensor("wo2_t", [2, 8, 128, 4, 128], F32, kind="ExternalInput").ap(),
        "w1p_t": nc.dram_tensor("w1p_t", [2, 44, 128, 8, 128], F32, kind="ExternalInput").ap(),
        "w2p_t": nc.dram_tensor("w2p_t", [2, 8, 128, 22, 128], F32, kind="ExternalInput").ap(),
        "rotq": nc.dram_tensor("rotq", [64, 4096], F32, kind="ExternalInput").ap(),
        "rotk": nc.dram_tensor("rotk", [64, 4096], F32, kind="ExternalInput").ap(),
        "msk": nc.dram_tensor("msk", [128, 4096], F32, kind="ExternalInput").ap(),
        "bp_l0f": nc.dram_tensor("bp_l0f", [2, 128, 70], F32, kind="ExternalInput").ap(),
    }
    stack = contextlib.ExitStack()
    with stack:
        P = Prog(nc, stack)
        ctx = Ctx(P)
        ctx.use_wcache = True
        ctx.load_consts(consts)
        for s in range(nseq):
            emit_l0_ffn(ctx, {"xin": xin[s], "xout": xout[s]}, w)
        P.wait("gpsimd", ctx.s_out)
        print("n queue items:", P.n_instr())
        P.finalize()
    return nc


def l0f_tables():
    DH, WINDOW = 64, 512
    n = T
    half = np.arange(0, DH, 2, dtype=np.float64)
    inv_freq = 1.0 / 10000.0 ** (half / DH)
    t = np.arange(n, dtype=np.float64)
    freqs = t[:, None] * inv_freq[None, :]
    freqs = np.concatenate([freqs, freqs], axis=1)
    base = (half + 0.4 * DH) / (1.4 * DH)
    power = (t - n // 2) / (WINDOW // 2)
    scale = base[None, :] ** power[:, None]
    scale = np.concatenate([scale, scale], axis=1)
    cos, sin = np.cos(freqs), np.sin(freqs)
    sgn = np.concatenate([-np.ones(32), np.ones(32)])[None, :]
    QK = 8.0
    Aq = (QK * cos * scale).T
    Bq = (QK * sin * scale * sgn).T
    Ck = (cos / scale).T
    Dk = (sin / scale * sgn).T
    rotq = np.concatenate([Aq, Bq], axis=1).astype(np.float32)
    rotk = np.concatenate([Ck, Dk], axis=1).astype(np.float32)
    msk = np.zeros((128, 8 * 512), np.float32)
    kj = np.arange(128)[:, None]
    qi = np.arange(512)[None, :]
    for r in range(8):
        rel = r * 128 - 512
        d = qi - kj - rel
        msk[:, r * 512:(r + 1) * 512] = np.where((d >= 0) & (d <= 512), 0.0, -1e9)
    return rotq, rotk, msk


def pack_l0f_weights(weights):
    HID = 2730
    w1p = np.zeros((2, 1024, 5632), np.float32)
    w2p = np.zeros((2, 2816, 1024), np.float32)
    b1p = np.zeros((2, 5632), np.float32)
    for l in range(2):
        w1p[l, :, 0:HID] = weights["w1"][l][:, 0:HID]
        w1p[l, :, 2816:2816 + HID] = weights["w1"][l][:, HID:]
        w2p[l, 0:HID] = weights["w2"][l]
        b1p[l, 0:HID] = weights["b1"][l][0:HID]
        b1p[l, 2816:2816 + HID] = weights["b1"][l][HID:]
    rotq, rotk, msk = l0f_tables()
    bp = np.zeros((2, 128, 70), np.float32)
    for l in range(2):
        bp[l, :, 0:8] = weights["ln_g"][l].reshape(8, 128).T
        bp[l, :, 8:16] = weights["ln_b"][l].reshape(8, 128).T
        bp[l, :, 16:24] = weights["b2"][l].reshape(8, 128).T
        bp[l, :, 24:68] = b1p[l].reshape(44, 128).T
        bp[l, 0:64, 68] = weights["q_scale"][l]
        bp[l, 0:64, 69] = weights["k_scale"][l]
    return {
        "wqkv_t": np.stack([tile_w(weights["wqkv"][l]) for l in range(2)]),
        "wo2_t": np.stack([tile_w(weights["wo"][l]) for l in range(2)]),
        "w1p_t": np.stack([tile_w(w1p[l]) for l in range(2)]),
        "w2p_t": np.stack([tile_w(w2p[l]) for l in range(2)]),
        "rotq": rotq, "rotk": rotk, "msk": msk, "bp_l0f": bp,
    }


def np_l0f_ref(x, w):
    from scipy.special import erf as _erf
    B, N, DIM = x.shape
    HEADS, DH, WINDOW = 8, 64, 512
    xd = x.astype(np.float64)
    wd = {k: np.asarray(v, np.float64) for k, v in w.items()}

    def _lnp(t, eps=1e-5):
        m = t.mean(-1, keepdims=True)
        v = t.var(-1, keepdims=True)
        return (t - m) / np.sqrt(v + eps)

    def shift_token(t):
        t1, t2 = t[..., :512], t[..., 512:]
        t2 = np.concatenate([np.zeros_like(t2[:, :1]), t2[:, :-1]], axis=1)
        return np.concatenate([t1, t2], axis=-1)

    def l2norm(t):
        return t / np.clip(np.linalg.norm(t, axis=-1, keepdims=True), 1e-12, None)

    def rotate_half(t):
        x1, x2 = t[..., :32], t[..., 32:]
        return np.concatenate([-x2, x1], axis=-1)

    def rotary_xpos(q, k):
        n = q.shape[-2]
        half = np.arange(0, DH, 2, dtype=np.float64)
        inv_freq = 1.0 / 10000.0 ** (half / DH)
        t = np.arange(n, dtype=np.float64)
        freqs = t[:, None] * inv_freq[None, :]
        freqs = np.concatenate([freqs, freqs], axis=-1)
        base = (half + 0.4 * DH) / (1.4 * DH)
        power = (t - n // 2) / (WINDOW // 2)
        scale = base[None, :] ** power[:, None]
        scale = np.concatenate([scale, scale], axis=-1)
        cos, sin = np.cos(freqs), np.sin(freqs)
        q = q * cos * scale + rotate_half(q) * sin * scale
        k = k * cos / scale + rotate_half(k) * sin / scale
        return q, k

    def local_attention(q, k, v):
        b, h, n, d = q.shape
        wn = n // WINDOW
        bq = q.reshape(b, h, wn, WINDOW, d) * 8.0
        bk = k.reshape(b, h, wn, WINDOW, d)
        bv = v.reshape(b, h, wn, WINDOW, d)

        def look_around(t):
            p = np.concatenate([np.zeros_like(t[:, :, :1]), t], axis=2)
            return np.concatenate([p[:, :, :wn], p[:, :, 1:]], axis=3)
        bk, bv = look_around(bk), look_around(bv)
        bt = np.arange(n).reshape(wn, WINDOW)
        btp = np.concatenate([np.full((1, WINDOW), -1), bt], axis=0)
        btk = np.concatenate([btp[:wn], btp[1:]], axis=1)
        sim = np.einsum('bhwid,bhwjd->bhwij', bq, bk)
        i = bt[:, :, None]
        j = btk[:, None, :]
        mask = (i < j) | (i > j + WINDOW) | (j == -1)
        sim = np.where(mask, -1e30, sim)
        sim = sim - sim.max(-1, keepdims=True)
        attn = np.exp(sim)
        attn = attn / attn.sum(-1, keepdims=True)
        out = np.einsum('bhwij,bhwjd->bhwid', attn, bv)
        return out.reshape(b, h, n, d)

    def layer(xx, l):
        b, n, _ = xx.shape
        h = shift_token(xx)
        qkv = h @ wd["wqkv"][l]
        q, k, v = qkv[..., 0:512], qkv[..., 512:1024], qkv[..., 1024:]
        to_heads = lambda t: t.reshape(b, n, HEADS, DH).transpose(0, 2, 1, 3)
        q, k, v = to_heads(q), to_heads(k), to_heads(v)
        q = l2norm(q) * wd["q_scale"][l]
        k = l2norm(k) * wd["k_scale"][l]
        q, k = rotary_xpos(q, k)
        o = local_attention(q, k, v)
        o = o.transpose(0, 2, 1, 3).reshape(b, n, 512)
        xx = xx + o @ wd["wo"][l]
        h = shift_token(xx)
        mu = h.mean(-1, keepdims=True)
        var = h.var(-1, keepdims=True)
        h = (h - mu) / np.sqrt(var + 1e-05) * wd["ln_g"][l] + wd["ln_b"][l]
        h = h @ wd["w1"][l] + wd["b1"][l]
        a, g = h[..., :2730], h[..., 2730:]
        h = a * (0.5 * g * (1 + _erf(g / np.sqrt(2))))
        xx = xx + h @ wd["w2"][l] + wd["b2"][l]
        return xx

    y = _lnp(xd)
    for l in range(2):
        y = layer(y, l)
    return (xd + _lnp(y)).astype(np.float32)



def kernel_multilaunch(**inputs):
    g = {k: np.ascontiguousarray(np.asarray(v, np.float32)) for k, v in inputs.items()}
    cores = list(range(8))
    x = g["L0_attn_x"]
    xfm = np.ascontiguousarray(x.reshape(8, 4, 2048, 1024).transpose(0, 1, 3, 2))
    consts = make_consts()

    w_l0a = l0a_in_maps(None, {
        "wq": g["L0_attn_wq"], "wk": g["L0_attn_wk"], "wv": g["L0_attn_wv"],
        "wo": g["L0_attn_wo"], "w1": g["L0_attn_w1"], "b1": g["L0_attn_b1"],
        "w2": g["L0_attn_w2"], "b2": g["L0_attn_b2"], "g1": g["L0_attn_g1"],
        "beta1": g["L0_attn_beta1"], "g2": g["L0_attn_g2"], "beta2": g["L0_attn_beta2"],
        "proj": g["L0_attn_proj"]})
    del w_l0a["xin"]
    w_l0f = pack_l0f_weights({
        "wqkv": g["L0_ffn_wqkv"], "q_scale": g["L0_ffn_q_scale"], "k_scale": g["L0_ffn_k_scale"],
        "wo": g["L0_ffn_wo"], "ln_g": g["L0_ffn_ln_g"], "ln_b": g["L0_ffn_ln_b"],
        "w1": g["L0_ffn_w1"], "b1": g["L0_ffn_b1"], "w2": g["L0_ffn_w2"], "b2": g["L0_ffn_b2"]})
    w_l1a = l1_in_maps(None, {
        "Wq": g["L1_attn_Wq"], "bq": g["L1_attn_bq"], "Wk": g["L1_attn_Wk"], "bk": g["L1_attn_bk"],
        "Wv": g["L1_attn_Wv"], "bv": g["L1_attn_bv"], "Wo": g["L1_attn_Wo"], "bo": g["L1_attn_bo"],
        "orf": g["L1_attn_orf"]})
    del w_l1a["xin"]
    del w_l1a["consts"]
    w_l1f = l1f_in_maps(None, {
        "W_up": g["L1_ffn_W_up"], "W_down": g["L1_ffn_W_down"], "W_q": g["L1_ffn_W_q"],
        "keys": g["L1_ffn_keys"], "down_embed": g["L1_ffn_down_embed"],
        "up_embed": g["L1_ffn_up_embed"]})
    del w_l1f["xin"]
    del w_l1f["consts"]

    shards = [xfm[i] for i in range(8)]
    for build, wmap in ((build_l0a_standalone, w_l0a), (build_l0f_standalone, w_l0f),
                        (build_l1_standalone, w_l1a), (build_l1f_standalone, w_l1f)):
        nc = build(4)
        in_maps = [dict(wmap, consts=consts, xin=shards[i]) for i in range(8)]
        res = run_bass_kernel_spmd(nc, in_maps, core_ids=cores)
        shards = [np.asarray(r["xout"]) for r in res.results]
    out = np.stack(shards)
    return np.ascontiguousarray(out.transpose(0, 1, 3, 2).reshape(32, 2048, 1024))



def build_fused(nseq=4):
    nc = bass.Bass("TRN2", target_bir_lowering=False)
    xin = nc.dram_tensor("xin", [nseq, 1024, T], F32, kind="ExternalInput").ap()
    xout = nc.dram_tensor("xout", [nseq, 1024, T], F32, kind="ExternalOutput").ap()
    consts = nc.dram_tensor("consts", [128, 832], F32, kind="ExternalInput").ap()

    def dt(name, shape):
        return nc.dram_tensor(name, shape, F32, kind="ExternalInput").ap()

    w_l0a = {
        "wq_t": dt("a_wq_t", [8, 128, 8, 128]), "wk_t": dt("a_wk_t", [8, 128, 8, 128]),
        "wv_t": dt("a_wv_t", [8, 128, 8, 128]), "wo_t": dt("a_wo_t", [8, 128, 8, 128]),
        "w1_t": dt("a_w1_t", [32, 128, 8, 128]), "w2_t": dt("a_w2_t", [8, 128, 32, 128]),
        "proj": dt("a_proj", [64, 64]), "biaspack_l0a": dt("a_bp", [128, 72]),
    }
    w_l0f = {
        "wqkv_t": dt("f_wqkv_t", [2, 12, 128, 8, 128]), "wo2_t": dt("f_wo2_t", [2, 8, 128, 4, 128]),
        "w1p_t": dt("f_w1p_t", [2, 44, 128, 8, 128]), "w2p_t": dt("f_w2p_t", [2, 8, 128, 22, 128]),
        "rotq": dt("f_rotq", [64, 4096]), "rotk": dt("f_rotk", [64, 4096]),
        "msk": dt("f_msk", [128, 4096]), "bp_l0f": dt("f_bp", [2, 128, 70]),
    }
    w_l1a = {
        "wq_t": dt("b_wq_t", [8, 128, 8, 128]), "wk_t": dt("b_wk_t", [8, 128, 8, 128]),
        "wv_t": dt("b_wv_t", [8, 128, 8, 128]), "wo_t": dt("b_wo_t", [8, 128, 8, 128]),
        "biaspack": dt("b_bp", [128, 32]), "orft": dt("b_orft", [64, 256]),
    }
    w_l1f = {
        "wup_t": dt("c_wup_t", [32, 128, 8, 128]), "wdn_t": dt("c_wdn_t", [8, 128, 32, 128]),
        "wq_t": dt("c_wq_t", [4, 128, 8, 128]), "keysp": dt("c_keysp", [2, 4, 64, 64]),
        "det": dt("c_det", [8, 128, 8, 512]), "ue": dt("c_ue", [4096, 1024]),
        "sd_scr": nc.dram_tensor("sd_scr", [16, 8, 128, 512], BF16, kind="Internal").ap(),
        "at_scr": nc.dram_tensor("at_scr", [32, 16, 128, 128], BF16, kind="Internal").ap(),
    }
    stack = contextlib.ExitStack()
    with stack:
        P = Prog(nc, stack)
        ctx = Ctx(P)
        ctx.use_wcache = True
        ctx.load_consts(consts)
        for s in range(nseq):
            emit_l0_attn(ctx, {"xin": xin[s], "xout": None}, w_l0a)
            emit_l0_ffn(ctx, {"xin": None, "xout": None}, w_l0f)
            emit_l1_attn(ctx, {"xin": None, "xout": None}, w_l1a)
            emit_l1_ffn(ctx, {"xin": None, "xout": xout[s]}, w_l1f)
        P.wait("gpsimd", ctx.s_out)
        P.finalize()
    return nc


def kernel_fused(**inputs):
    g = {k: np.ascontiguousarray(np.asarray(v, np.float32)) for k, v in inputs.items()}
    x = g["L0_attn_x"]
    xfm = np.ascontiguousarray(x.reshape(8, 4, 2048, 1024).transpose(0, 1, 3, 2))
    wmap = {"consts": make_consts()}
    m = l0a_in_maps(None, {
        "wq": g["L0_attn_wq"], "wk": g["L0_attn_wk"], "wv": g["L0_attn_wv"],
        "wo": g["L0_attn_wo"], "w1": g["L0_attn_w1"], "b1": g["L0_attn_b1"],
        "w2": g["L0_attn_w2"], "b2": g["L0_attn_b2"], "g1": g["L0_attn_g1"],
        "beta1": g["L0_attn_beta1"], "g2": g["L0_attn_g2"], "beta2": g["L0_attn_beta2"],
        "proj": g["L0_attn_proj"]})
    wmap.update({"a_wq_t": m["wq_t"], "a_wk_t": m["wk_t"], "a_wv_t": m["wv_t"],
                 "a_wo_t": m["wo_t"], "a_w1_t": m["w1_t"], "a_w2_t": m["w2_t"],
                 "a_proj": m["proj"], "a_bp": m["biaspack_l0a"]})
    m = pack_l0f_weights({
        "wqkv": g["L0_ffn_wqkv"], "q_scale": g["L0_ffn_q_scale"], "k_scale": g["L0_ffn_k_scale"],
        "wo": g["L0_ffn_wo"], "ln_g": g["L0_ffn_ln_g"], "ln_b": g["L0_ffn_ln_b"],
        "w1": g["L0_ffn_w1"], "b1": g["L0_ffn_b1"], "w2": g["L0_ffn_w2"], "b2": g["L0_ffn_b2"]})
    wmap.update({"f_wqkv_t": m["wqkv_t"], "f_wo2_t": m["wo2_t"], "f_w1p_t": m["w1p_t"],
                 "f_w2p_t": m["w2p_t"], "f_rotq": m["rotq"], "f_rotk": m["rotk"],
                 "f_msk": m["msk"], "f_bp": m["bp_l0f"]})
    m = l1_in_maps(None, {
        "Wq": g["L1_attn_Wq"], "bq": g["L1_attn_bq"], "Wk": g["L1_attn_Wk"], "bk": g["L1_attn_bk"],
        "Wv": g["L1_attn_Wv"], "bv": g["L1_attn_bv"], "Wo": g["L1_attn_Wo"], "bo": g["L1_attn_bo"],
        "orf": g["L1_attn_orf"]})
    wmap.update({"b_wq_t": m["wq_t"], "b_wk_t": m["wk_t"], "b_wv_t": m["wv_t"],
                 "b_wo_t": m["wo_t"], "b_bp": m["biaspack"], "b_orft": m["orft"]})
    m = l1f_in_maps(None, {
        "W_up": g["L1_ffn_W_up"], "W_down": g["L1_ffn_W_down"], "W_q": g["L1_ffn_W_q"],
        "keys": g["L1_ffn_keys"], "down_embed": g["L1_ffn_down_embed"],
        "up_embed": g["L1_ffn_up_embed"]})
    wmap.update({"c_wup_t": m["wup_t"], "c_wdn_t": m["wdn_t"], "c_wq_t": m["wq_t"],
                 "c_keysp": m["keysp"], "c_det": m["det"], "c_ue": m["ue"]})
    nc = build_fused(4)
    in_maps = [dict(wmap, xin=xfm[i]) for i in range(8)]
    res = run_bass_kernel_spmd(nc, in_maps, core_ids=list(range(8)))
    out = np.stack([np.asarray(r["xout"]) for r in res.results])
    return np.ascontiguousarray(out.transpose(0, 1, 3, 2).reshape(32, 2048, 1024))


def kernel(**inputs):
    return kernel_fused(**inputs)
